# version 17
# baseline (speedup 1.0000x reference)
import numpy as np
import ml_dtypes
import concourse.bass as bass
import concourse.mybir as mybir
from concourse.bass_utils import run_bass_kernel_spmd

F32 = mybir.dt.float32
BF16 = mybir.dt.bfloat16
I32 = mybir.dt.int32
AX = mybir.AxisListType.X
ALU = mybir.AluOpType
ACT = mybir.ActivationFunctionType

T, D, FF, H, HD = 2048, 1024, 4096, 16, 64
NT, ND, NF = T // 128, D // 128, FF // 128
ENGS = ("sync", "scalar", "tensor", "vector", "gpsimd")
BB = ml_dtypes.bfloat16


class Op:
    __slots__ = ("fn", "incs")

    def __init__(self, fn):
        self.fn = fn
        self.incs = []


class Plan:
    def __init__(self, nc):
        self.nc = nc
        self.q = {e: [] for e in ENGS}
        self.ctr = {e: [nc.alloc_semaphore(name=f"c_{e}"), 0] for e in ENGS}
        self.dch = {}
        self.waited = {e: {} for e in ENGS}

    def emit(self, eng, fn, is_wait=False, is_dma=False):
        op = Op(fn)
        self.q[eng].append(op)
        if not (is_wait or is_dma):
            self.last_real = getattr(self, "last_real", {})
            self.last_real[eng] = op
        return (eng, op)

    def dma(self, eng, out, in_, ch="d0"):
        if ch not in self.dch:
            self.dch[ch] = [self.nc.alloc_semaphore(name=f"dm_{ch}"), 0]
        ent = self.dch[ch]
        sem = ent[0]
        if ent[1] > 0:
            self.wait(eng, (eng, "dma", sem, ent[1]))
        ent[1] += 16
        self.emit(eng, lambda e, o=out, i=in_, s=sem:
                  e.dma_start(out=o, in_=i).then_inc(s, 16), is_dma=True)
        return (eng, "dma", sem, ent[1])

    def mark(self, emitted):
        eng, op = emitted
        if op.incs:
            return (eng, "c", op.incs[0][0], op.incs[0][1])
        sem, v = self.ctr[eng]
        self.ctr[eng][1] = v + 1
        op.incs.append((sem, v + 1))
        return (eng, "c", sem, v + 1)

    def wait(self, eng, tok):
        if tok is None:
            return
        teng, kind, sem, val = tok
        seen = self.waited[eng].get(id(sem), 0)
        if val <= seen:
            return
        self.waited[eng][id(sem)] = val
        self.emit(eng, lambda e, s=sem, v=val: e.wait_ge(s, v), is_wait=True)

    def fence(self, eng):
        op = getattr(self, "last_real", {}).get(eng)
        if op is None:
            return None
        tok = self.mark((eng, op))
        self.wait(eng, tok)
        return tok

    def waits(self, eng, toks):
        for t in toks:
            self.wait(eng, t)

    def replay(self):
        with self.nc.Block() as blk:
            for name in ENGS:
                ops = self.q[name]

                def mk(ops):
                    def body(eng):
                        for op in ops:
                            ins = op.fn(eng)
                            for sem, val in op.incs:
                                ins.then_inc(sem, 1)
                    return body

                getattr(blk, name)(mk(ops))


def bfv(ap):
    return ap.bitcast(BF16)


def build_consts():
    c = {}
    c["c_idb"] = np.eye(128).astype(BB)
    ind = np.zeros((128, 16, 16), dtype=np.float32)
    for t in range(16):
        ind[:, t, t] = 1.0
    c["c_ind"] = ind.astype(BB)
    p = np.arange(128)
    c["c_lt"] = (p[None, :] <= p[:, None]).astype(BB)
    pre = np.zeros((16, 4, 5), dtype=np.float32)
    for b in range(4):
        for j in range(4):
            pre[: max(4 * b + j - 1, 0), b, j] = 1.0
        pre[:, b, 4] = 1.0
    c["c_pre"] = pre.astype(BB)
    c["c_onesrow"] = np.ones((1, 128), dtype=np.float32)
    c["c_idf32"] = np.eye(128, dtype=np.float32)
    c["c_id2"] = np.eye(2, dtype=np.float32)
    c["c_hmb"] = c["c_hm"] if False else None
    c["c_pre32"] = None
    c["c_zeros"] = np.zeros((2, 512)).astype(BB)
    c["c_onescol"] = np.ones((128, 1), dtype=np.float32)
    c["c_ones2row"] = np.ones((2, 512)).astype(BB)
    sel = np.zeros((2, 4, 128), dtype=np.float32)
    sel[0, 0:2] = 1.0
    sel[1, 2:4] = 1.0
    c["c_sel"] = sel.astype(BB)
    hm = np.zeros((128, 2), dtype=np.float32)
    hm[0:64, 0] = 1.0
    hm[64:128, 1] = 1.0
    c["c_hm"] = hm
    c["c_hmb"] = hm.astype(BB)
    c["c_pre32"] = np.ascontiguousarray(c["c_pre"].astype(np.float32))
    return c


CONSTS = {k: v for k, v in build_consts().items() if v is not None}


def build_program(stage=3):
    nc = bass.Bass("TRN2", target_bir_lowering=False, debug=False)
    pl = Plan(nc)

    x_in = nc.dram_tensor("x_in", [T, D], F32, kind="ExternalInput")
    W = {n: nc.dram_tensor(n, list(s), F32, kind="ExternalInput") for n, s in
         [("wq", (D, D)), ("wk", (D, D)), ("wv", (D, D)), ("wo", (D, D)),
          ("wg", (FF, D)), ("wu", (FF, D)), ("wd", (D, FF))]}
    Bi = {n: nc.dram_tensor(n, [D], F32, kind="ExternalInput")
          for n in ("bq", "bk", "bv", "bo")}
    relk = nc.dram_tensor("relk", [257, HD], F32, kind="ExternalInput")
    relv = nc.dram_tensor("relv", [257, HD], F32, kind="ExternalInput")
    CT = {k: nc.dram_tensor(k, list(v.shape),
                            F32 if v.dtype == np.float32 else BF16,
                            kind="ExternalInput") for k, v in CONSTS.items()}
    y_out = nc.dram_tensor("y_out", [T, D], F32, kind="ExternalOutput")

    xres = nc.dram_tensor("xres", [T, D], F32, kind="Internal")
    wqh = {"wg": nc.dram_tensor("wgq_h", [D, FF], BF16, kind="Internal"),
           "wu": nc.dram_tensor("wuq_h", [D, FF], BF16, kind="Internal"),
           "wd": nc.dram_tensor("wdq_h", [FF, D], BF16, kind="Internal")}
    e_h = nc.dram_tensor("e_h", [4, 2, 128 * 511], BF16, kind="Internal")
    att_hbm = nc.dram_tensor("att_hbm", [2 * ND * 128 * T], BF16, kind="Internal")
    s_h = nc.dram_tensor("s_h", [2, 2, 128 * 384], BF16, kind="Internal")

    ACOLS = 53056
    arena = nc.alloc_sbuf_tensor("arena", [128, ACOLS], F32)
    AAP = arena.ap()

    class Alloc:
        def __init__(self):
            self.top = 0
            self.marks = []
            self.peak = 0

        def take(self, ncols):
            a = self.top
            assert a + ncols <= ACOLS, f"arena overflow {a}+{ncols}"
            self.top = a + ncols
            self.peak = max(self.peak, self.top)
            return AAP[:, a:a + ncols]

        def takeb(self, nb):
            return bfv(self.take((nb + 1) // 2))[:, 0:nb]

        def push(self):
            self.marks.append(self.top)

        def pop(self):
            self.top = self.marks.pop()

    al = Alloc()

    pb = [nc.alloc_psum_tensor(f"pb{i}", [128, 512], F32) for i in range(6)]
    pt = [nc.alloc_psum_tensor(f"pt{i}", [128, 1024], BF16) for i in range(2)]
    btok = {}

    def bank_wait(eng, key):
        pl.wait(eng, btok.get(key))

    def bank_set(key, tok):
        btok[key] = tok

    id_b = al.takeb(128)
    ind_b = al.takeb(256)
    lt_b = al.takeb(128)
    pre_b = al.takeb(20)[0:16, :]
    onesrow = al.take(128)[0:1, :]
    onescol = al.take(1)
    ones2row = al.takeb(512)[0:2, :]
    hm_f = al.take(2)
    id_f32 = al.take(128)
    id2_f = al.take(2)[0:2, :]
    hm_b = al.takeb(2)
    pre32_b = al.take(20)[0:16, :]
    zrows_b = al.takeb(512)[0:2, :]
    zcol_b = zrows_b[0:1, 0:1]
    zrow_b = zrows_b[0:1, 0:512]
    toks = [
        pl.dma("sync", id_f32, CT["c_idf32"].ap(), ch="const"),
        pl.dma("sync", id2_f, CT["c_id2"].ap(), ch="const"),
        pl.dma("sync", hm_b, CT["c_hmb"].ap(), ch="const"),
        pl.dma("sync", pre32_b, CT["c_pre32"].ap().rearrange("p a b -> p (a b)"), ch="const"),
        pl.dma("sync", zrows_b, CT["c_zeros"].ap(), ch="const"),
        pl.dma("sync", id_b, CT["c_idb"].ap(), ch="const"),
        pl.dma("sync", ind_b, CT["c_ind"].ap().rearrange("p a b -> p (a b)"), ch="const"),
        pl.dma("sync", lt_b, CT["c_lt"].ap(), ch="const"),
        pl.dma("sync", pre_b, CT["c_pre"].ap().rearrange("p a b -> p (a b)"), ch="const"),
        pl.dma("sync", onesrow, CT["c_onesrow"].ap(), ch="const"),
        pl.dma("sync", onescol, CT["c_onescol"].ap(), ch="const"),
        pl.dma("sync", ones2row, CT["c_ones2row"].ap(), ch="const"),
        pl.dma("sync", hm_f, CT["c_hm"].ap(), ch="const"),
    ]
    sel4 = al.takeb(256)[0:4, :]
    toks.append(pl.dma("sync", sel4.rearrange("p (s n) -> p s n", n=128),
                       bass.AP(CT["c_sel"], 0, [[128, 4], [512, 2], [1, 128]]), ch="const"))
    for e in ENGS:
        pl.wait(e, toks[-1])

    def ap_of(x):
        return x if isinstance(x, bass.AP) else x[:]

    def mm(bank_ap, lhsT, rhs, start, stop, skip=False):
        return pl.emit("tensor", lambda e, a=ap_of(bank_ap), l=ap_of(lhsT),
                       r=ap_of(rhs), s=start, t=stop, k=skip:
                       e.matmul(a, l, r, start=s, stop=t, skip_group_check=k))

    def tr(pt_ap, src_b):
        return pl.emit("tensor", lambda e, a=pt_ap, s=src_b:
                       e.transpose(a, s, id_b))

    def ve(fn):
        return pl.emit("vector", fn)

    def sc(fn):
        return pl.emit("scalar", fn)

    def transpose_to(dst, src_blocks, ptsel, src3d=False):
        nblk = len(src_blocks)
        assert nblk <= 8
        pt_b = pt[ptsel]
        bank_wait("tensor", f"pt{ptsel}")
        for k, s in enumerate(src_blocks):
            ttk = tr(pt_b[:, 128 * k:128 * (k + 1)], s)
        tok = pl.mark(ttk)
        pl.wait("vector", tok)
        srcv = pt_b[:, 0:128 * nblk]
        if src3d:
            srcv = srcv.rearrange("p (c n) -> p c n", n=128)
        evk = ve(lambda e, d=dst, s=srcv: e.tensor_copy(d, s))
        etok = pl.mark(evk)
        bank_set(f"pt{ptsel}", etok)
        return etok

    al.push()
    wbuf = [al.take(1024) for _ in range(2)]
    wsum = al.take(1)
    red1 = al.take(1)
    mrow = al.take(1)[0:1, :]
    srow = al.take(1)[0:1, :]
    srep = al.take(1)
    dqk = al.take(3)[0:1, :]
    i32t = al.take(1024)
    wq_full = al.takeb(NF * 1024)
    rowbuf = [al.takeb(4096) for _ in range(2)]

    wspec = [("wg", W["wg"], wqh["wg"], FF, D), ("wu", W["wu"], wqh["wu"], FF, D),
             ("wd", W["wd"], wqh["wd"], D, FF)]
    wq_store = {}
    cons_tok = [None, None]
    st_tok = [None, None]
    for wi, (nm, wt, wqt, RO, CO) in enumerate(wspec):
        nch, ncc = RO // 128, CO // 128
        npc = CO // 1024
        pieces = [(ch, cp) for ch in range(nch) for cp in range(npc)]
        pl.fence("vector")
        ve(lambda e: e.tensor_scalar(wsum, onescol, 0.0, None, ALU.mult))
        lastv = None
        for pi, (ch, cp) in enumerate(pieces):
            bi = pi % 2
            if cons_tok[bi] is not None:
                pl.wait("sync", cons_tok[bi])
            ltok = pl.dma("sync", wbuf[bi],
                          wt.ap()[128 * ch:128 * (ch + 1), 1024 * cp:1024 * (cp + 1)],
                          ch=f"wqp{bi}")
            pl.wait("vector", ltok)
            pl.fence("vector")
            rv = ve(lambda e, b=wbuf[bi]: e.reduce_sum(
                red1, b, axis=AX, apply_absolute_value=True))
            cons_tok[bi] = pl.mark(rv)
            pl.fence("vector")
            lastv = ve(lambda e: e.tensor_tensor(wsum, wsum, red1, ALU.add))
        tok_sums = pl.mark(lastv)
        pl.wait("tensor", tok_sums)
        bank_wait("tensor", "pb0")
        m1 = mm(pb[0][0:1, 0:1], onescol, wsum, True, True)
        pl.wait("vector", pl.mark(m1))
        ve(lambda e, s=1.0 / (RO * CO): e.tensor_scalar(
            mrow, pb[0][0:1, 0:1], s, 1e-5, ALU.mult, ALU.max))
        pl.fence("vector")
        ve(lambda e: e.reciprocal(srow, mrow))
        kv = ve(lambda e, c=dqk[:, wi:wi + 1]: e.tensor_copy(c, mrow))
        tok_s = pl.mark(kv)
        bank_set("pb0", tok_s)
        pl.wait("tensor", tok_s)
        bank_wait("tensor", "pb0")
        m2 = mm(pb[0][:, 0:1], onesrow, srow, True, True)
        pl.wait("vector", pl.mark(m2))
        evs = ve(lambda e: e.tensor_copy(srep, pb[0][:, 0:1]))
        bank_set("pb0", pl.mark(evs))
        bank_wait("vector", "wqfull")
        wq3 = wq_full.rearrange("p (c n) -> p c n", n=CO)
        qv = None
        for pi, (ch, cp) in enumerate(pieces):
            bi = pi % 2
            if cons_tok[bi] is not None:
                pl.wait("sync", cons_tok[bi])
            ltok = pl.dma("sync", wbuf[bi],
                          wt.ap()[128 * ch:128 * (ch + 1), 1024 * cp:1024 * (cp + 1)],
                          ch=f"wqp{bi}")
            pl.wait("vector", ltok)
            pl.fence("vector")
            fv = ve(lambda e, b=wbuf[bi]: e.tensor_scalar(
                i32t.bitcast(I32)[:, 0:1024], b, srep, None, ALU.mult))
            cons_tok[bi] = pl.mark(fv)
            pl.fence("vector")
            qv = ve(lambda e, d=wq3[:, ch, 1024 * cp:1024 * (cp + 1)]:
                    e.tensor_scalar(d, i32t.bitcast(I32)[:, 0:1024],
                                    -1.0, 1.0, ALU.max, ALU.min))
        tok_q = pl.mark(qv)
        pl.wait("tensor", tok_q)
        for rc in range(ncc):
            bo = rc % 2
            if st_tok[bo] is not None:
                pl.wait("vector", st_tok[bo])
            evk_tok = None
            for rnd in range((nch + 7) // 8):
                nblk = min(8, nch - 8 * rnd)
                blocks = [wq3[:, 8 * rnd + k, 128 * rc:128 * rc + 128]
                          for k in range(nblk)]
                evk_tok = transpose_to(
                    rowbuf[bo][:, 1024 * rnd:1024 * rnd + 128 * nblk],
                    blocks, rnd % 2)
            pl.wait("gpsimd", evk_tok)
            st_tok[bo] = pl.dma("gpsimd", wqt.ap()[128 * rc:128 * (rc + 1), :],
                                rowbuf[bo][:, 0:RO], ch=f"wqs{bo}")
        bank_set("wqfull", evk_tok)
        wq_store[nm] = [st_tok[0], st_tok[1]]
    al.pop()

    mrep = al.take(3)
    bank_wait("tensor", "pb0")
    m3 = mm(pb[0][:, 0:3], onesrow, dqk[:, 0:3], True, True)
    pl.wait("vector", pl.mark(m3))
    evm = ve(lambda e: e.tensor_copy(mrep, pb[0][:, 0:3]))
    tok_mrep = pl.mark(evm)
    bank_set("pb0", tok_mrep)

    ln_m = al.take(1)
    ln_sq = al.take(1)
    ln_v = al.take(1)
    ln_rs = al.take(1)
    qa = al.take(1)
    qs = al.take(1)
    i32q = al.take(1024)
    sq_scr = al.take(1024)

    def ln_tile(xt, h_out):
        ve(lambda e: e.reduce_sum(ln_m, xt, axis=AX))
        pl.fence("vector")
        tok = pl.mark(ve(lambda e: e.tensor_scalar(ln_m, ln_m, 1.0 / D, None, ALU.mult)))
        s1 = sc(lambda e: e.activation(sq_scr, xt, ACT.Square, accum_out=ln_sq))
        pl.wait("vector", pl.mark(s1))
        pl.wait("vector", tok)
        ve(lambda e: e.tensor_tensor(ln_v, ln_m, ln_m, ALU.mult))
        pl.fence("vector")
        ve(lambda e: e.tensor_scalar(
            ln_v, ln_sq, 1.0 / D, ln_v, ALU.mult, ALU.subtract))
        pl.fence("vector")
        t2 = pl.mark(ve(lambda e: e.tensor_scalar(
            ln_v, ln_v, 1e-5, None, ALU.add)))
        pl.wait("scalar", t2)
        s2 = sc(lambda e: e.activation(ln_rs, ln_v, ACT.Sqrt))
        pl.wait("vector", pl.mark(s2))
        ve(lambda e: e.reciprocal(ln_rs, ln_rs))
        pl.fence("vector")
        return pl.mark(ve(lambda e: e.tensor_scalar(
            h_out, xt, ln_m, ln_rs, ALU.subtract, ALU.mult)))

    def quant_tok(src, n, qdst_b, dq_col):
        pl.fence("vector")
        ve(lambda e: e.reduce_max(qa, src, axis=AX, apply_absolute_value=True))
        pl.fence("vector")
        ve(lambda e: e.tensor_scalar(qa, qa, 1e-5, 1.0 / 127.0, ALU.max, ALU.mult))
        pl.fence("vector")
        ve(lambda e: e.tensor_copy(dq_col, qa))
        ve(lambda e: e.reciprocal(qs, qa))
        nb = (n + 1023) // 1024
        for b in range(nb):
            c0, c1 = 1024 * b, min(1024 * (b + 1), n)
            pl.fence("vector")
            ve(lambda e, s=src[:, c0:c1], d=i32q.bitcast(I32)[:, 0:c1 - c0]:
               e.tensor_scalar(d, s, qs, None, ALU.mult))
            pl.fence("vector")
            qv = ve(lambda e, s=i32q.bitcast(I32)[:, 0:c1 - c0],
                    d=qdst_b[:, c0:c1]: e.tensor_copy(d, s))
        return pl.mark(qv)


    def ffn_sublayer(src_dram, dst_dram, tag, pre_tok=None):
        al.push()
        xtile = al.take(1024)
        htile = al.take(1024)
        xq_b = al.takeb(1024)
        xqT = al.takeb(ND * 8 * 128)
        xqT4 = xqT.rearrange("p (c t n) -> p c t n", t=8, n=128)
        sg = al.take(8)
        su = al.take(8)
        sd = al.take(8)
        dqx = al.take(1)
        pamax = al.take(8)
        p_region = al.take(8 * 4096)
        p_half = [p_region[:, 4096 * k:4096 * (k + 1)] for k in range(8)]
        wgb = [al.takeb(ND * 512) for _ in range(2)]
        wub = [al.takeb(ND * 512) for _ in range(2)]
        wgbF = [bfv(b).bitcast(F32) for b in wgb]
        ytile = [wgbF[0][:, 0:1024], wgbF[1][:, 0:1024]]
        gsil_t = al.take(512)
        gsil = [gsil_t, gsil_t]

        wgb3 = [b.rearrange("p (c n) -> p c n", n=512) for b in wgb]
        wub3 = [b.rearrange("p (c n) -> p c n", n=512) for b in wub]
        p_regb = bfv(p_region)
        wdb_chunks = [p_regb[:, 8192 * (oc // 4) + 1024 * (oc % 4):
                             8192 * (oc // 4) + 1024 * (oc % 4) + 1024]
                      for oc in range(NF)]
        wdb_dst = [p_regb[:, 8192 * a:8192 * a + 4096]
                   .rearrange("p (b n) -> p b n", n=1024) for a in range(8)]
        wdb_src = [bass.AP(wqh["wd"], 512 * D * a,
                           [[D, 128], [128 * D, 4], [1, 1024]])
                   for a in range(8)]

        xcons = None
        wcons = [None, None]
        wdcons = None
        ycons = [None, None]
        gtok = [None]
        pre_list = pre_tok if isinstance(pre_tok, list) else (
            [pre_tok] if pre_tok is not None else [])
        for tk_ in pre_list:
            for e_ in ("sync", "vector", "scalar", "tensor", "gpsimd"):
                pl.wait(e_, tk_)
        half_tok = None
        out_toks = []
        for half in range(2):
            xqT_tok = None
            for ti in range(8):
                tau = 8 * half + ti
                if xcons is not None:
                    pl.wait("sync", xcons)
                xtok = pl.dma("sync", xtile,
                              src_dram.ap()[128 * tau:128 * (tau + 1), :],
                              ch="xa")
                if half_tok is not None:
                    pl.wait("vector", half_tok)
                    pl.wait("scalar", half_tok)
                    pl.wait("tensor", half_tok)
                pl.wait("vector", xtok)
                pl.fence("vector")
                pl.wait("scalar", xtok)
                htok = ln_tile(xtile, htile)
                xcons = htok
                qtok = quant_tok(htile, 1024, xq_b, dqx)
                pl.wait("vector", tok_mrep)
                pl.fence("vector")
                ve(lambda e, c=sg[:, ti:ti + 1]: e.tensor_scalar(
                    c, dqx, mrep[:, 0:1], None, ALU.mult))
                ve(lambda e, c=su[:, ti:ti + 1]: e.tensor_scalar(
                    c, dqx, mrep[:, 1:2], None, ALU.mult))
                pl.wait("tensor", qtok)
                xqT_tok = transpose_to(
                    xqT4[:, :, ti, :],
                    [xq_b[:, 128 * dc:128 * (dc + 1)] for dc in range(ND)],
                    ti % 2, src3d=True)
            for ob in range(8):
                bo = ob % 2
                if wcons[bo] is not None:
                    pl.wait("sync", wcons[bo])
                if ycons[bo] is not None:
                    pl.wait("sync", ycons[bo])
                pl.waits("sync", wq_store["wg"] + wq_store["wu"])
                wtok1 = pl.dma("sync", wgb3[bo], bass.AP(
                    wqh["wg"], 512 * ob, [[FF, 128], [128 * FF, ND], [1, 512]]),
                    ch=f"wgb{bo}")
                wtok2 = pl.dma("sync", wub3[bo], bass.AP(
                    wqh["wu"], 512 * ob, [[FF, 128], [128 * FF, ND], [1, 512]]),
                    ch=f"wub{bo}")
                pl.wait("tensor", wtok1)
                pl.wait("tensor", wtok2)
                pl.wait("tensor", xqT_tok)
                for ti in range(8):
                    bg, bgk = pb[ti % 2], f"pb{ti % 2}"
                    bu, buk = pb[2 + ti % 2], f"pb{2 + ti % 2}"
                    bank_wait("tensor", bgk)
                    bank_wait("tensor", buk)
                    mku = None
                    for dc in range(ND):
                        lhs = xqT4[:, dc, ti, :]
                        mm(bg, lhs, wgb3[bo][:, dc], dc == 0, dc == ND - 1)
                        mku = mm(bu, lhs, wub3[bo][:, dc], dc == 0, dc == ND - 1)
                    tok_mm = pl.mark(mku)
                    pl.wait("scalar", tok_mm)
                    if gtok[0] is not None:
                        pl.wait("scalar", gtok[0])
                    stok = sc(lambda e, b=bg[:], d=sg[:, ti:ti + 1], g=gsil[ti % 2]:
                              e.activation(g, b, ACT.Silu, scale=d))
                    smk = pl.mark(stok)
                    bank_set(bgk, smk)
                    pl.wait("vector", smk)
                    ptok = ve(lambda e, b=bu[:], d=su[:, ti:ti + 1], g=gsil[ti % 2],
                              dst=p_half[ti][:, 512 * ob:512 * (ob + 1)]:
                              e.scalar_tensor_tensor(dst, b, d, g,
                                                     ALU.mult, ALU.mult))
                    pmk = pl.mark(ptok)
                    bank_set(buk, pmk)
                    gtok[0] = pmk
                    pl.fence("vector")
                    ve(lambda e, p=p_half[ti][:, 512 * ob:512 * (ob + 1)]:
                       e.reduce_max(qa, p, axis=AX, apply_absolute_value=True))
                    pl.fence("vector")
                    if ob == 0:
                        mxv = ve(lambda e, a=pamax[:, ti:ti + 1]: e.tensor_copy(a, qa))
                    else:
                        mxv = ve(lambda e, a=pamax[:, ti:ti + 1]:
                                 e.tensor_tensor(a, a, qa, ALU.max))
                    wcons[bo] = pl.mark(mxv) if ti == 7 else wcons[bo]
            xq2T_tok = [None] * 8
            for ti in range(8):
                pl.fence("vector")
                ve(lambda e, a=pamax[:, ti:ti + 1]: e.tensor_scalar(
                    qa, a, 1e-5, 1.0 / 127.0, ALU.max, ALU.mult))
                pl.fence("vector")
                ve(lambda e, c=sd[:, ti:ti + 1]: e.tensor_scalar(
                    c, qa, mrep[:, 2:3], 0.5, ALU.mult, ALU.mult))
                ve(lambda e: e.reciprocal(qs, qa))
                xq2 = bfv(p_half[ti])[:, 0:4096]
                qv = None
                for b2 in range(4):
                    pl.fence("vector")
                    ve(lambda e, s=p_half[ti][:, 1024 * b2:1024 * (b2 + 1)],
                       d=i32q.bitcast(I32)[:, 0:1024]:
                       e.tensor_scalar(d, s, qs, None, ALU.mult))
                    pl.fence("vector")
                    qv = ve(lambda e, s=i32q.bitcast(I32)[:, 0:1024],
                            d=xq2[:, 1024 * b2:1024 * (b2 + 1)]: e.tensor_copy(d, s))
                qtok = pl.mark(qv)
                pl.wait("tensor", qtok)
                xq2T = bfv(p_half[ti])[:, 4096:8192]
                for g4 in range(4):
                    xq2T_tok[ti] = transpose_to(
                        xq2T[:, g4 * 1024:g4 * 1024 + 1024],
                        [xq2[:, g4 * 1024 + 128 * k:g4 * 1024 + 128 * (k + 1)]
                         for k in range(8)], g4 % 2)
            pl.waits("sync", wq_store["wd"])
            for ti in range(8):
                pl.wait("sync", xq2T_tok[ti])
            for a in range(8):
                wdtok = pl.dma("sync", wdb_dst[a], wdb_src[a], ch="wd")
            pl.wait("tensor", wdtok)
            pl.wait("gpsimd", wcons[0])
            pl.wait("gpsimd", wcons[1])
            dlast = None
            for ti in range(8):
                tau = 8 * half + ti
                yb = ytile[ti % 2]
                if ycons[ti % 2] is not None:
                    pl.wait("gpsimd", ycons[ti % 2])
                xtok = pl.dma("gpsimd", yb,
                              src_dram.ap()[128 * tau:128 * (tau + 1), :],
                              ch=f"yx{ti % 2}")
                xq2T = bfv(p_half[ti])[:, 4096:8192]
                pl.wait("tensor", xq2T_tok[ti])
                evk = None
                for db in range(2):
                    bank_wait("tensor", "pb4")
                    mkd = None
                    for oc in range(NF):
                        mkd = mm(pb[4], xq2T[:, 128 * oc:128 * (oc + 1)],
                                 wdb_chunks[oc][:, 512 * db:512 * (db + 1)],
                                 oc == 0, oc == NF - 1)
                    dlast = mkd
                    tok_mm = pl.mark(mkd)
                    pl.wait("vector", tok_mm)
                    pl.wait("vector", xtok)
                    evk = ve(lambda e, b=pb[4][:], d=sd[:, ti:ti + 1],
                             y=yb[:, 512 * db:512 * (db + 1)]:
                             e.scalar_tensor_tensor(y, b, d, y, ALU.mult, ALU.add))
                    bank_set("pb4", pl.mark(evk))
                ytok = pl.mark(evk)
                pl.wait("gpsimd", ytok)
                stok = pl.dma("gpsimd",
                              dst_dram.ap()[128 * tau:128 * (tau + 1), :], yb,
                              ch=f"ys{ti % 2}")
                ycons[ti % 2] = stok
                out_toks.append(stok)
            wdcons = pl.mark(dlast)
            half_tok = wdcons
        al.pop()
        out_toks = out_toks[-2:] + [pl.mark((e_, pl.last_real[e_]))
                                    for e_ in ("tensor", "vector", "scalar")]
        return out_toks


    def attention_sublayer(src_dram, dst_dram, with_rel=True, pre_tok=None):
        al.push()
        att0 = al.top
        xtile_a = al.take(1024)
        htile_a = al.take(1024)
        hh_t = al.takeb(1024)
        hl_t = al.takeb(1024)
        hT0 = al.top
        hTh = al.takeb(ND * NT * 128)
        hTl = al.takeb(ND * NT * 128)
        hTh4 = hTh.rearrange("p (c t n) -> p c t n", t=NT, n=128)
        hTl4 = hTl.rearrange("p (c t n) -> p c t n", t=NT, n=128)
        nhl = [al.takeb(512) for _ in range(4)]
        rkx_h = al.takeb(512)
        rkx_l = al.takeb(512)
        rkt0 = al.top
        rkt32 = al.take(768)
        rvp_h = al.takeb(2 * 64)
        rvp_l = al.takeb(2 * 64)
        rvd_f = al.take(64)[0:1, :]
        rkm = al.take(1)[0:1, :]
        r256rep = al.take(64)
        rvcat_h = al.takeb(2)[:, 0:1]
        rvcat_l = al.takeb(2)[:, 1:2]
        rvc32 = al.take(1)
        wpb = [al.take(1024) for _ in range(2)]
        WT = {w: (al.takeb(ND * 128), al.takeb(ND * 128)) for w in ("q", "k", "v")}
        bias2 = {w: al.takeb(256)[0:2, 0:128] for w in ("q", "k", "v")}
        brow = al.take(128)[0:1, :]
        bc2 = al.takeb(2)
        bcol_s = al.take(1)
        qk0 = al.top
        Qh, Ql = al.takeb(T), al.takeb(T)
        Kh, Kl = al.takeb(T), al.takeb(T)
        Vx_h = al.takeb(NT * 160)
        Vx_l = al.takeb(NT * 160)
        Vx_h3 = Vx_h.rearrange("p (t n) -> p t n", n=160)
        Vx_l3 = Vx_l.rearrange("p (t n) -> p t n", n=160)
        Kcat = bfv(AAP[:, att0:att0 + 2048])
        Kcat3 = Kcat.rearrange("p (c n) -> p c n", n=T)
        qdup2 = [al.takeb(512) for _ in range(2)]
        sqb = al.takeb(T)
        mrow4 = al.take(4)[0:1, :]
        macc = al.take(2)
        negB2 = al.take(2)
        tb_s = al.take(2)[0:1, :]
        ring0 = al.top
        ring_h = [al.takeb(512) for _ in range(6)]
        ring_l = [al.takeb(512) for _ in range(6)]
        pf32 = [al.take(512) for _ in range(2)]
        e_hi = [al.takeb(512) for _ in range(2)]
        e_lo = [al.takeb(512) for _ in range(2)]
        rs_h = [al.takeb(384) for _ in range(4)]
        rs_l = [al.takeb(384) for _ in range(4)]
        stp_h = [al.takeb(384) for _ in range(4)]
        stp_l = [al.takeb(384) for _ in range(4)]
        psk_sl = [(al.takeb(256), al.takeb(256)) for _ in range(2)]
        psk_sl = psk_sl + [
            (bfv(AAP[:, rkt0:rkt0 + 128]), bfv(AAP[:, rkt0 + 128:rkt0 + 256])),
            (bfv(AAP[:, rkt0 + 256:rkt0 + 384]), bfv(AAP[:, rkt0 + 384:rkt0 + 512]))]
        psk_rdy = [None] * 4
        pskT_h = al.takeb(256)
        pskT_l = al.takeb(256)
        tmp4 = al.takeb(16)
        bm4 = [al.takeb(256)[0:4, 0:128] for _ in range(4)]
        cs_sb = al.take(512)[0:16, :]
        tri4p = [al.take(4), al.take(4)]
        triT_s = al.take(128)[0:4, :]
        c0r4 = al.take(128)[0:4, :]
        c0flat = al.take(512)[0:1, :]
        trflat = al.take(512)[0:1, :]
        lflat = al.take(512)[0:1, :]
        pfx_sb = al.take(512)[0:5, :]
        nsc = al.take(512)
        WoT8 = [(bfv(AAP[:, hT0 + 1024 * oc:hT0 + 1024 * oc + 512]),
                 bfv(AAP[:, hT0 + 1024 * oc + 512:hT0 + 1024 * oc + 1024]))
                for oc in range(8)]
        boext8 = al.take(8)
        aosc = [AAP[:, qk0 + 512 * i:qk0 + 512 * (i + 1)] for i in range(8)]
        ytile_a = [AAP[:, qk0 + 4096 + 1024 * i:qk0 + 4096 + 1024 * (i + 1)]
                   for i in range(2)]
        attb_h = bfv(AAP[:, ring0:ring0 + 2048])
        attb_l = bfv(AAP[:, ring0 + 2048:ring0 + 4096])
        attb_h3 = attb_h.rearrange("p (c n) -> p c n", n=512)
        attb_l3 = attb_l.rearrange("p (c n) -> p c n", n=512)

        pre_list = pre_tok if isinstance(pre_tok, list) else (
            [pre_tok] if pre_tok is not None else [])
        for tk_ in pre_list:
            for e_ in ENGS:
                pl.wait(e_, tk_)

        xcons = None
        hT_tok = None
        for tau in range(NT):
            if xcons is not None:
                pl.wait("sync", xcons)
            xtok = pl.dma("sync", xtile_a,
                          src_dram.ap()[128 * tau:128 * (tau + 1), :], ch="xa")
            pl.wait("vector", xtok)
            pl.wait("scalar", xtok)
            pl.fence("vector")
            htok = ln_tile(xtile_a, htile_a)
            xcons = htok
            pl.fence("vector")
            ve(lambda e: e.tensor_copy(hh_t, htile_a))
            pl.fence("vector")
            ve(lambda e: e.tensor_tensor(hl_t, htile_a, hh_t, ALU.subtract))
            sp_tok = pl.fence("vector")
            pl.wait("tensor", sp_tok)
            transpose_to(hTh4[:, :, tau, :],
                         [hh_t[:, 128 * dc:128 * (dc + 1)] for dc in range(ND)],
                         0, src3d=True)
            hT_tok = transpose_to(hTl4[:, :, tau, :],
                                  [hl_t[:, 128 * dc:128 * (dc + 1)] for dc in range(ND)],
                                  1, src3d=True)

        if with_rel:
            rk_sb = wpb[0]
            pl.dma("sync", rk_sb[:, 0:64], relk.ap()[0:128, :], ch="rk")
            t2 = pl.dma("sync", rk_sb[:, 64:128], relk.ap()[128:256, :], ch="rk")
            t3 = pl.dma("sync", rkt32[0:64, 256:257],
                        bass.AP(relk, 256 * HD, [[1, 64], [1, 1]]), ch="rk")
            pl.wait("vector", t3)
            pl.fence("vector")
            ve(lambda e: e.tensor_scalar(rk_sb[:, 128:256], rk_sb[:, 0:128],
                                         0.125, None, ALU.mult))
            ve(lambda e: e.tensor_scalar(rkt32[0:64, 256:257],
                                         rkt32[0:64, 256:257], 0.125, None, ALU.mult))
            sctok = pl.fence("vector")
            pl.wait("tensor", sctok)
            bank_wait("tensor", "pb0")
            pl.emit("tensor", lambda e: e.transpose(
                pb[0][0:64, 0:128], rk_sb[:, 128:192], id_f32))
            tt1 = pl.emit("tensor", lambda e: e.transpose(
                pb[0][0:64, 128:256], rk_sb[:, 192:256], id_f32))
            tt1_tok = pl.mark(tt1)
            pl.wait("vector", tt1_tok)
            ev = ve(lambda e: e.tensor_copy(rkt32[0:64, 0:256], pb[0][0:64, 0:256]))
            bank_set("pb0", pl.mark(ev))
            pl.fence("vector")
            ve(lambda e: e.tensor_tensor(rkt32[0:64, 260:517], rkt32[0:64, 0:257],
                                         rkt32[0:64, 0:257], ALU.mult))
            sqtok = pl.fence("vector")
            pl.wait("tensor", sqtok)
            bank_wait("tensor", "pb0")
            nmm = mm(pb[0][0:1, 0:257], onescol[0:64, :], rkt32[0:64, 260:517],
                     True, True)
            pl.wait("vector", pl.mark(nmm))
            rmv = ve(lambda e: e.reduce_max(rkm, pb[0][0:1, 0:257], axis=AX))
            rmvt = pl.mark(rmv)
            bank_set("pb0", rmvt)
            pl.fence("vector")
            pl.wait("scalar", rmvt)
            sq2 = sc(lambda e: e.activation(rkm, rkm, ACT.Sqrt))
            pl.wait("vector", pl.mark(sq2))
            pl.fence("vector")
            ve(lambda e: e.tensor_copy(rkx_h[0:64, 127:384], rkt32[0:64, 0:257]))
            ve(lambda e, s=rkt32[0:64, 0:1].to_broadcast((64, 127)):
               e.tensor_copy(rkx_h[0:64, 0:127], s))
            ve(lambda e, s=rkt32[0:64, 256:257].to_broadcast((64, 128)):
               e.tensor_copy(rkx_h[0:64, 384:512], s))
            pl.fence("vector")
            ve(lambda e: e.tensor_tensor(rkx_l[0:64, 127:384], rkt32[0:64, 0:257],
                                         rkx_h[0:64, 127:384], ALU.subtract))
            ve(lambda e, s=rkt32[0:64, 0:1].to_broadcast((64, 127)),
               h=rkx_h[0:64, 0:127]:
               e.tensor_tensor(rkx_l[0:64, 0:127], s, h, ALU.subtract))
            ve(lambda e, s=rkt32[0:64, 256:257].to_broadcast((64, 128)),
               h=rkx_h[0:64, 384:512]:
               e.tensor_tensor(rkx_l[0:64, 384:512], s, h, ALU.subtract))
            rkx_tok0 = pl.fence("vector")
            pl.wait("gpsimd", rkx_tok0)
            pl.dma("gpsimd", rkx_h[64:128, :], rkx_h[0:64, :], ch="rkrep")
            rep_tok = pl.dma("gpsimd", rkx_l[64:128, :], rkx_l[0:64, :], ch="rkrep")
            rkx_tok = rep_tok

            rv_sb = wpb[1]
            pl.dma("sync", rv_sb[:, 0:64], relv.ap()[0:128, :], ch="rv")
            v2 = pl.dma("sync", rv_sb[:, 64:128], relv.ap()[128:256, :], ch="rv")
            v3 = pl.dma("sync", rv_sb[0:1, 128:192], relv.ap()[256:257, :], ch="rv")
            pl.wait("tensor", v3)
            bank_wait("tensor", "pb1")
            rmm = mm(pb[1][:, 0:64], onesrow, rv_sb[0:1, 128:192], True, True)
            pl.wait("vector", pl.mark(rmm))
            rrv = ve(lambda e: e.tensor_copy(r256rep, pb[1][:, 0:64]))
            bank_set("pb1", pl.mark(rrv))
            pl.wait("vector", v2)
            pl.fence("vector")
            ve(lambda e: e.tensor_tensor(rvd_f, rv_sb[0:1, 0:64],
                                         r256rep[0:1, :], ALU.subtract))
            ve(lambda e: e.tensor_tensor(rv_sb[:, 192:256], rv_sb[:, 0:64],
                                         r256rep, ALU.subtract))
            ve(lambda e: e.tensor_tensor(rv_sb[:, 256:320], rv_sb[:, 64:128],
                                         r256rep, ALU.subtract))
            pl.fence("vector")
            ve(lambda e: e.tensor_scalar(rv_sb[0:1, 192:256], rv_sb[0:1, 192:256],
                                         0.0, None, ALU.mult))
            pl.fence("vector")
            ve(lambda e: e.tensor_copy(rvp_h[:, 0:64], rv_sb[:, 192:256]))
            ve(lambda e: e.tensor_copy(rvp_h[:, 64:128], rv_sb[:, 256:320]))
            pl.fence("vector")
            ve(lambda e: e.tensor_tensor(rvp_l[:, 0:64], rv_sb[:, 192:256],
                                         rvp_h[:, 0:64], ALU.subtract))
            ve(lambda e: e.tensor_tensor(rvp_l[:, 64:128], rv_sb[:, 256:320],
                                         rvp_h[:, 64:128], ALU.subtract))
            bank_wait("tensor", "pb1")
            mm(pb[1][0:64, 0:1], rv_sb[0:1, 128:192], onesrow[0:1, 0:1], True, True)
            cmm2 = mm(pb[1][64:128, 0:1], rv_sb[0:1, 128:192], onesrow[0:1, 0:1],
                      True, True)
            pl.wait("vector", pl.mark(cmm2))
            rcv = ve(lambda e: e.tensor_copy(rvc32, pb[1][:, 0:1]))
            bank_set("pb1", pl.mark(rcv))
            pl.fence("vector")
            ve(lambda e: e.tensor_copy(rvcat_h, rvc32))
            pl.fence("vector")
            ve(lambda e: e.tensor_tensor(rvcat_l, rvc32, rvcat_h, ALU.subtract))
            rv_tok = pl.fence("vector")
        else:
            ve(lambda e: e.tensor_scalar(rkm, onescol[0:1, :], 0.0, None, ALU.mult))
            ve(lambda e: e.tensor_scalar(rvc32, onescol, 0.0, None, ALU.mult))
            pl.fence("vector")
            ve(lambda e: e.tensor_copy(rvcat_h, rvc32))
            ve(lambda e: e.tensor_copy(rvcat_l, rvc32))
            rkx_tok = pl.fence("vector")
            rv_tok = rkx_tok

        ring_cons = [None] * 6
        plo_tok = [None] * 6
        tri_done = [None] * 4
        tilestate = [None]
        pf_cons = [None, None]
        qd_cons = [None, None]
        qd_tok = [None, None]
        rot_gate = [None]
        rotK_tok = [None]
        att_wcons = [None, None, None, None]
        att_wtoks = []
        eW_tok = [None, None]
        eR_tok = [None, None]
        psW_tok = [None, None]
        psR_tok = [None, None]
        rs_ready = [None] * 4
        rs_cons = [None] * 4
        bm_tok = [None] * 4
        stp_cons = [None] * 4
        psk_cons = [None]
        c0_tok = [None]

        bm_pend = [None] * 4

        def emit_strips(h, po, hh, blk, qts):
            def p2(j):
                pl.wait("tensor", bm_pend[j])
                if j == 0:
                    bank_wait("tensor", "pt0")
                bmt = pl.emit("tensor", lambda e, c=128 * j, s=tmp4[:, 4 * j:4 * j + 4]:
                              e.transpose(pt[0][0:4, c:c + 128], s, id_b))
                pl.wait("vector", pl.mark(bmt))
                bme = ve(lambda e, d=bm4[j], c=128 * j:
                         e.tensor_copy(d, pt[0][0:4, c:c + 128]))
                bm_tok[j] = pl.mark(bme)
                if j == 3:
                    bank_set("pt0", bm_tok[j])

            def p1(j, qt):
                es = qt % 2
                rbk = "pb2" if j % 2 == 0 else "pb0"
                rb = pb[2] if j % 2 == 0 else pb[0]
                bank_wait("tensor", rbk)
                pl.wait("tensor", rkx_tok)
                mm(rb, Qh[po:po + 64, 128 * qt:128 * qt + 128],
                   rkx_h[po:po + 64, :], True, False)
                mm(rb, Qh[po:po + 64, 128 * qt:128 * qt + 128],
                   rkx_l[po:po + 64, :], False, False)
                rmk = mm(rb, Ql[po:po + 64, 128 * qt:128 * qt + 128],
                         rkx_h[po:po + 64, :], False, True)
                rtok = pl.mark(rmk)
                pl.wait("vector", rtok)
                pl.fence("vector")
                ve(lambda e, s=rb[:, 127:384:256], d=tmp4[:, 4 * j:4 * j + 4:2]:
                   e.tensor_copy(d, s))
                pl.fence("vector")
                ve(lambda e, s=rb[:, 127:384:256],
                   hsb=tmp4[:, 4 * j:4 * j + 4:2], d=tmp4[:, 4 * j + 1:4 * j + 4:2]:
                   e.tensor_tensor(d, s, hsb, ALU.subtract))
                bm_pend[j] = pl.fence("vector")
                pl.wait("vector", eW_tok[es])
                ve(lambda e, d=e_hi[es][:, 0:511]: e.tensor_copy(
                    d, rb[:, 0:511]))
                pl.fence("vector")
                elo_v = ve(lambda e, d=e_lo[es][:, 0:511], hsb=e_hi[es][:, 0:511]:
                           e.tensor_tensor(d, rb[:, 0:511], hsb, ALU.subtract))
                ebuild = pl.mark(elo_v)
                bank_set(rbk, ebuild)
                pl.wait("gpsimd", ebuild)
                pl.wait("gpsimd", eR_tok[es])
                pl.dma("gpsimd", bass.AP(e_h, (es * 2 + 0) * 128 * 511,
                                         [[511, 128], [1, 511]]),
                       e_hi[es][:, 0:511], ch=f"ehW{es}")
                wtk = pl.dma("gpsimd", bass.AP(e_h, (es * 2 + 1) * 128 * 511,
                                               [[511, 128], [1, 511]]),
                             e_lo[es][:, 0:511], ch=f"ehW{es}")
                eW_tok[es] = wtk
                pl.wait("sync", wtk)
                pl.wait("sync", rs_cons[j])
                pl.dma("sync", rs_h[j], bass.AP(
                    e_h, (es * 2 + 0) * 128 * 511 + 127, [[510, 128], [1, 384]]),
                    ch=f"ehR{es}")
                rtk = pl.dma("sync", rs_l[j], bass.AP(
                    e_h, (es * 2 + 1) * 128 * 511 + 127, [[510, 128], [1, 384]]),
                    ch=f"ehR{es}")
                eR_tok[es] = rtk
                rs_ready[j] = rtk

            for j, qt in enumerate(qts):
                p1(j, qt)
                p2(j)

        def emit_tiles(h, po, hh, blk, par, qts, with_rel, pend, carry_out):
            cols = slice(512 * blk, 512 * (blk + 1))
            bank = pb[4 + par]
            bkey = f"pb{4 + par}"
            csb = 64 if po == 0 else 32
            bst = {"first": True, "ztok": None}

            def bank_mm(rows, lhsT, rhs, stop=False):
                st = bst["first"]
                if st:
                    pl.wait("tensor", bst["ztok"])
                bst["first"] = False
                return mm(bank[rows[0]:rows[1], :] if rows[2] is None else
                          bank[rows[0]:rows[1], rows[2]:rows[3]],
                          lhsT, rhs, st, stop, skip=True)

            def psk_part2(qt):
                j = qt - 4 * blk
                psk_h, psk_l = psk_sl[j]
                pl.wait("tensor", psk_rdy[j])
                bank_wait("tensor", "pt0")
                pl.emit("tensor", lambda e, a=psk_h: e.transpose(
                    pt[0][:, 0:128], a[:, 0:128], id_b))
                pl.emit("tensor", lambda e, a=psk_h: e.transpose(
                    pt[0][:, 128:256], a[:, 128:256], id_b))
                pl.emit("tensor", lambda e, a=psk_l: e.transpose(
                    pt[0][:, 256:384], a[:, 0:128], id_b))
                ptk = pl.emit("tensor", lambda e, a=psk_l: e.transpose(
                    pt[0][:, 384:512], a[:, 128:256], id_b))
                pl.wait("vector", pl.mark(ptk))
                pl.fence("vector")
                ve(lambda e: e.tensor_copy(pskT_h, pt[0][:, 0:256]))
                pve = ve(lambda e: e.tensor_copy(pskT_l, pt[0][:, 256:512]))
                pskt = pl.mark(pve)
                bank_set("pt0", pskt)
                psk_cons[0] = pskt
                pl.wait("tensor", pskt)
                pl.wait("tensor", rv_tok)
                for ck in range(2):
                    bank_mm((po, po + 64, 128 * j, 128 * j + 128),
                            rvp_h[:, 64 * ck:64 * ck + 64],
                            pskT_h[:, 128 * ck:128 * ck + 128])
                    bank_mm((po, po + 64, 128 * j, 128 * j + 128),
                            rvp_l[:, 64 * ck:64 * ck + 64],
                            pskT_h[:, 128 * ck:128 * ck + 128])
                    bank_mm((po, po + 64, 128 * j, 128 * j + 128),
                            rvp_h[:, 64 * ck:64 * ck + 64],
                            pskT_l[:, 128 * ck:128 * ck + 128])

            bst["psk2"] = psk_part2
            tilestate[0] = bst
            base = 4 * blk
            far = [(base + 5 + k) % NT for k in range(10)]
            diag = [(base + 15 + k) % NT for k in range(6)]
            order = far[0:3] + diag + far[3:10]
            done_tiles = set()
            strips_emitted = set()
            strip_ready = {}
            psk_pend = {}
            slot_of = {}
            sbanks = (0, 1, 3)
            vlo, vhi = (0, 80) if hh == 0 else (32, 160)
            ro, rh = (0, 80) if hh == 0 else (0, 128)
            for oi in range(NT):
                tt = order[oi]
                if pend:
                    for sl_, fn_ in pend:
                        if sl_ == oi:
                            fn_()
                sb_ = pb[sbanks[oi % 3]]
                sbk = f"pb{sbanks[oi % 3]}"
                bank_wait("tensor", sbk)
                if oi == 0:
                    bank_wait("tensor", bkey)
                    bank_wait("vector", bkey)
                    pl.fence("vector")
                    zb = ve(lambda e: e.tensor_scalar(
                        bank[:], Qh[:, 0:512], 0.0, None, ALU.mult))
                    bst["ztok"] = pl.mark(zb)
                if oi == 0:
                    pl.wait("tensor", qd_tok[blk % 2])
                    pl.wait("tensor", rotK_tok[0])
                mm(sb_, Kcat3[:, hh, 128 * tt:128 * tt + 128],
                   qdup2[blk % 2], True, False)
                if with_rel:
                    for j, qt in enumerate(qts):
                        dqt = tt - qt
                        if -1 <= dqt <= 1:
                            pl.wait("tensor", rs_ready[j])
                            co = 128 * (dqt + 1)
                            mm(sb_[:, 128 * j:128 * j + 128],
                               rs_h[j][:, co:co + 128], id_b, False, False)
                            rmm2 = mm(sb_[:, 128 * j:128 * j + 128],
                                      rs_l[j][:, co:co + 128], id_b, False, False)
                            rs_cons[j] = pl.mark(rmm2)
                        elif dqt < -1:
                            pl.wait("tensor", bm_tok[j])
                            mm(sb_[:, 128 * j:128 * j + 128],
                               sel4[0:4, 0:128], bm4[j], False, False)
                        else:
                            pl.wait("tensor", bm_tok[j])
                            mm(sb_[:, 128 * j:128 * j + 128],
                               sel4[0:4, 128:256], bm4[j], False, False)
                lmk = mm(sb_, Kl[po:po + 64, 128 * tt:128 * tt + 128],
                         Qh[po:po + 64, cols], False, True)
                smk = pl.mark(lmk)
                s6 = oi % 6
                slot_of[tt] = s6
                pl.wait("scalar", smk)
                pl.wait("scalar", pf_cons[oi % 2])
                petok = pl.mark(sc(lambda e, d=pf32[oi % 2], b=sb_[:],
                                   c=negB2[:, hh:hh + 1]:
                                   e.activation(d, b, ACT.Exp, bias=c)))
                bank_set(sbk, petok)
                pl.wait("scalar", ring_cons[s6])
                pl.fence("scalar")
                phi_t = pl.mark(sc(lambda e, d=ring_h[s6], p=pf32[oi % 2]:
                                   e.activation(d, p, ACT.Copy)))
                pl.wait("vector", phi_t)
                pl.fence("vector")
                pltok = pl.mark(ve(lambda e, d=ring_l[s6], p=pf32[oi % 2],
                                   hsb=ring_h[s6]:
                                   e.tensor_tensor(d, p, hsb, ALU.subtract)))
                plo_tok[s6] = pltok
                pf_cons[oi % 2] = pltok
                lagged = []
                if oi >= 5:
                    lagged.append(order[oi - 5])
                if oi == NT - 1 and not carry_out:
                    lagged += [order[NT - 5 + k3] for k3 in range(5)]
                for lt in lagged:
                    l6 = slot_of[lt]
                    pl.wait("tensor", plo_tok[l6])
                    bank_mm((ro, rh, None, None),
                            Vx_h3[:, lt, vlo:vhi], ring_h[l6])
                    bank_mm((ro, rh, None, None),
                            Vx_h3[:, lt, vlo:vhi], ring_l[l6])
                    pvk = bank_mm((ro, rh, None, None),
                                  Vx_l3[:, lt, vlo:vhi], ring_h[l6],
                                  stop=(oi == NT - 1 and lt == lagged[-1]
                                        and not with_rel and not carry_out))
                    ring_cons[l6] = pl.mark(pvk)
                done_tiles.add(tt)
                if with_rel:
                    for qt in qts:
                        if qt in strip_ready or qt in strips_emitted:
                            continue
                        need = {x for x in (qt - 1, qt, qt + 1) if 0 <= x < NT}
                        if need <= done_tiles:
                            strip_ready[qt] = oi
                    strip_qts = [qt for qt, ro in sorted(strip_ready.items())
                                 if ro + 3 <= oi or oi == NT - 1]
                    for qt in strip_qts:
                        strips_emitted.add(qt)
                        del strip_ready[qt]
                        j = qt - 4 * blk
                        jj = j
                        pl.wait("vector", stp_cons[j])
                        if 0 < qt < NT - 1:
                            bank_wait("tensor", "pt1")
                            tl = None
                            for kk, st_t in enumerate((qt - 1, qt, qt + 1)):
                                s6b = slot_of[st_t]
                                pl.wait("tensor", plo_tok[s6b])
                                pl.emit("tensor", lambda e, c=256 * kk,
                                        s=ring_h[s6b][:, 128 * jj:128 * jj + 128]:
                                        e.transpose(pt[1][:, c:c + 128], s, id_b))
                                tl = pl.emit("tensor", lambda e, c=256 * kk + 128,
                                             s=ring_l[s6b][:, 128 * jj:128 * jj + 128]:
                                             e.transpose(pt[1][:, c:c + 128], s, id_b))
                            pl.wait("vector", pl.mark(tl))
                            sev = None
                            for kk in range(3):
                                ve(lambda e, d=stp_h[j][:, 128 * kk:128 * kk + 128],
                                   c=256 * kk: e.tensor_copy(d, pt[1][:, c:c + 128]))
                                sev = ve(lambda e, d=stp_l[j][:, 128 * kk:128 * kk + 128],
                                         c=256 * kk + 128:
                                         e.tensor_copy(d, pt[1][:, c:c + 128]))
                            bank_set("pt1", pl.mark(sev))
                        else:
                            for kk, st_t in enumerate((qt - 1, qt, qt + 1)):
                                if 0 <= st_t < NT:
                                    s6b = slot_of[st_t]
                                    bank_wait("tensor", "pt1")
                                    pl.wait("tensor", plo_tok[s6b])
                                    pl.emit("tensor", lambda e, s=ring_h[s6b][:, 128 * jj:128 * jj + 128]:
                                            e.transpose(pt[1][:, 0:128], s, id_b))
                                    tl = pl.emit("tensor", lambda e, s=ring_l[s6b][:, 128 * jj:128 * jj + 128]:
                                                 e.transpose(pt[1][:, 128:256], s, id_b))
                                    pl.wait("vector", pl.mark(tl))
                                    ve(lambda e, d=stp_h[j][:, 128 * kk:128 * kk + 128]:
                                       e.tensor_copy(d, pt[1][:, 0:128]))
                                    sev = ve(lambda e, d=stp_l[j][:, 128 * kk:128 * kk + 128]:
                                             e.tensor_copy(d, pt[1][:, 128:256]))
                                    bank_set("pt1", pl.mark(sev))
                                else:
                                    pl.fence("vector")
                                    ve(lambda e, d=stp_h[j][:, 128 * kk:128 * kk + 128]:
                                       e.tensor_scalar(d, id_b, 0.0, None, ALU.mult))
                                    sev = ve(lambda e, d=stp_l[j][:, 128 * kk:128 * kk + 128]:
                                             e.tensor_scalar(d, id_b, 0.0, None, ALU.mult))
                                    pl.mark(sev)
                        stp_cons[j] = pl.fence("vector")
                        t4 = tri4p[par]
                        tcs = 128 + 8 * par
                        if qt == 0:
                            ve(lambda e, d=t4[:, 0:1]: e.tensor_scalar(
                                d, onescol, 0.0, None, ALU.mult))
                        else:
                            ve(lambda e, d=t4[:, j:j + 1], s=stp_h[j][:, 0:128]:
                               e.scalar_tensor_tensor(nsc[:, 0:64].bitcast(BF16),
                                                      s, 1.0, lt_b, ALU.mult,
                                                      ALU.mult, accum_out=d))
                            ve(lambda e, d=nsc[:, tcs + j:tcs + j + 1],
                               s=stp_l[j][:, 0:128]:
                               e.scalar_tensor_tensor(nsc[:, 64:128].bitcast(BF16),
                                                      s, 1.0, lt_b, ALU.mult,
                                                      ALU.mult, accum_out=d))
                        pl.fence("vector")
                        if qt != 0:
                            ve(lambda e, d=t4[:, j:j + 1], a=nsc[:, tcs + j:tcs + j + 1]:
                               e.tensor_tensor(d, d, a, ALU.add))
                        tri_done[j] = pl.fence("vector")
                        ps = j % 2
                        pl.wait("gpsimd", stp_cons[j])
                        pl.wait("gpsimd", psR_tok[ps])
                        pl.dma("gpsimd",
                               bass.AP(s_h, (ps * 2 + 0) * 128 * 384,
                                       [[384, 128], [1, 384]]),
                               stp_h[j], ch=f"psW{ps}")
                        pwt = pl.dma("gpsimd",
                                     bass.AP(s_h, (ps * 2 + 1) * 128 * 384,
                                             [[384, 128], [1, 384]]),
                                     stp_l[j], ch=f"psW{ps}")
                        psW_tok[ps] = pwt
                        stp_cons[j] = pwt
                        pl.wait("sync", pwt)
                        pl.wait("sync", psk_cons[0])
                        pl.dma("sync", psk_sl[j][0], bass.AP(
                            s_h, (ps * 2 + 0) * 128 * 384,
                            [[385, 128], [1, 256]]), ch=f"psR{ps}")
                        prt = pl.dma("sync", psk_sl[j][1], bass.AP(
                            s_h, (ps * 2 + 1) * 128 * 384,
                            [[385, 128], [1, 256]]), ch=f"psR{ps}")
                        psR_tok[ps] = prt
                        psk_rdy[j] = prt
                        psk_pend[qt] = oi
                if with_rel:
                    for qt in list(psk_pend):
                        if psk_pend[qt] + 4 <= oi:
                            bst["psk2"](qt)
                            del psk_pend[qt]
            if pend:
                for sl_, fn_ in pend:
                    if sl_ >= NT:
                        fn_()
            carries = []
            if carry_out:
                for k2 in range(5):
                    lt_c = order[NT - 5 + k2]
                    l6_c = slot_of[lt_c]
                    tok_c = plo_tok[l6_c]

                    def cfn(lt=lt_c, l6=l6_c, tok=tok_c, last=(k2 == 0)):
                        pl.wait("tensor", tok)
                        bank_mm((ro, rh, None, None),
                                Vx_h3[:, lt, vlo:vhi], ring_h[l6])
                        bank_mm((ro, rh, None, None),
                                Vx_h3[:, lt, vlo:vhi], ring_l[l6])
                        pvk2 = bank_mm((ro, rh, None, None),
                                       Vx_l3[:, lt, vlo:vhi], ring_h[l6],
                                       stop=(last and not with_rel))
                        ring_cons[l6] = pl.mark(pvk2)
                    carries.append(((NT - 5 + k2) % 6, cfn))
                if with_rel:
                    for ip, qt in enumerate(sorted(psk_pend)):
                        def pfn(qt=qt):
                            bst["psk2"](qt)
                        carries.append(((1 + ip) % 4, pfn))
                    psk_pend.clear()
            elif with_rel:
                for qt in sorted(psk_pend):
                    bst["psk2"](qt)
                psk_pend.clear()
            return carries

        def make_post(h, po, hh, blk, par, qts, with_rel, rsnap, bst):
            bank = pb[4 + par]
            bkey = f"pb{4 + par}"
            csb = 64 if po == 0 else 32

            S = {}

            def st0():
                pl.wait("vector", rsnap[(NT - 1) % 6])
                pl.fence("vector")
                csv = ve(lambda e: e.tensor_copy(cs_sb, bank[csb:csb + 16, :]))
                cst = pl.mark(csv)
                pl.wait("tensor", cst)
                bank_wait("tensor", "pb2")
                pfm = mm(pb[2][0:5, :], pre32_b[:, 5 * blk:5 * blk + 5], cs_sb,
                         True, True)
                pl.wait("vector", pl.mark(pfm))
                pfe = ve(lambda e: e.tensor_copy(pfx_sb, pb[2][0:5, :]))
                S["pfe"] = pl.mark(pfe)
                bank_set("pb2", S["pfe"])
                pl.wait("gpsimd", S["pfe"])
                ldm = pl.dma("gpsimd", lflat, pfx_sb[4:5, :], ch="c0")
                pl.wait("vector", ldm)
                pl.fence("vector")
                ve(lambda e: e.reciprocal(lflat, lflat))
                S["lr"] = pl.fence("vector")

            def st1():
                pfe_tok = S["pfe"]
                if with_rel:
                    for j in range(4):
                        pl.wait("tensor", tri_done[j])
                    bank_wait("tensor", "pb2")
                    trt = pl.emit("tensor", lambda e: e.transpose(
                        pb[2][0:4, 0:128], tri4p[par], id_f32))
                    pl.wait("vector", pl.mark(trt))
                    pl.fence("vector")
                    tre = ve(lambda e: e.tensor_copy(triT_s, pb[2][0:4, 0:128]))
                    tre_t = pl.mark(tre)
                    bank_set("pb2", tre_t)
                    pl.wait("gpsimd", tre_t)
                    pl.wait("gpsimd", pfe_tok)
                    pl.wait("gpsimd", c0_tok[0])
                    pl.dma("gpsimd", c0flat,
                           bass.AP(arena, pfx_sb.offset, [[ACOLS + 128, 4], [1, 128]]),
                           ch="c0")
                    c0d = pl.dma("gpsimd", trflat,
                                 bass.AP(arena, triT_s.offset, [[ACOLS, 4], [1, 128]]),
                                 ch="c0")
                    pl.wait("vector", c0d)
                    pl.fence("vector")
                    ve(lambda e: e.tensor_tensor(c0flat, c0flat, trflat, ALU.add))
                    c0t = pl.fence("vector")
                    c0_tok[0] = c0t

            def st2():
                if with_rel:
                    pl.wait("tensor", c0_tok[0])
                    for j in range(4):
                        mm(bank[po:po + 64, 128 * j:128 * j + 128],
                           rvd_f, c0flat[0:1, 128 * j:128 * j + 128],
                           False, j == 3, skip=True)
                pl.wait("tensor", S["lr"])
                bank_wait("tensor", "pb2")
                lrm = mm(pb[2], onesrow, lflat, True, True)
                S["lrt"] = pl.mark(lrm)

            def st3():
                pl.wait("vector", S["lrt"])
                pl.wait("vector", rsnap[(NT - 1) % 6])
                pl.fence("vector")
                ve(lambda e, d=nsc[po:po + 64, :]: e.tensor_copy(
                    d, bank[po:po + 64, :]))
                pl.fence("vector")
                nv = ve(lambda e, d=nsc[po:po + 64, :]: e.tensor_tensor(
                    d, d, pb[2][po:po + 64, :], ALU.mult))
                ntok = pl.mark(nv)
                bank_set("pb2", ntok)
                bank_set(bkey, ntok)
                ns2 = (blk % 2) * 2
                pl.wait("vector", att_wcons[ns2])
                pl.fence("vector")
                ve(lambda e, d=nhl[ns2][po:po + 64, :], s=nsc[po:po + 64, :]:
                   e.tensor_copy(d, s))
                pl.fence("vector")
                ve(lambda e, d=nhl[ns2 + 1][po:po + 64, :],
                   s=nsc[po:po + 64, :], hb=nhl[ns2][po:po + 64, :]:
                   e.tensor_tensor(d, s, hb, ALU.subtract))
                awt = pl.fence("vector")
                pl.wait("gpsimd", awt)
                row = (h // 2) * 128 + 64 * (h % 2)
                pl.dma("gpsimd", bass.AP(att_hbm, row * T + 512 * blk,
                                         [[T, 64], [1, 512]]),
                       nhl[ns2][po:po + 64, :], ch=f"atw{ns2}")
                att_wcons[ns2] = pl.dma(
                    "gpsimd", bass.AP(att_hbm, (ND * 128 + row) * T + 512 * blk,
                                      [[T, 64], [1, 512]]),
                    nhl[ns2 + 1][po:po + 64, :], ch=f"atw{ns2}")
                att_wtoks.append(att_wcons[ns2])
            return [st0, st1, st2, st3]

        par_ctr = [0]

        def attn_head(h, po, hh, with_rel, pending):
            for blk in range(4):
                qts = [4 * blk + j for j in range(4)]
                par = par_ctr[0] % 2
                par_ctr[0] += 1
                b2 = blk % 2
                qsrc = Qh[64 * hh:64 * hh + 64, 512 * blk:512 * (blk + 1)]
                pl.wait("gpsimd", qd_cons[b2])
                pl.wait("gpsimd", rot_gate[0])
                pl.dma("gpsimd", qdup2[b2][0:64, :], qsrc, ch="rot0")
                qd_tok[b2] = pl.dma("gpsimd", qdup2[b2][64:128, :], qsrc,
                                    ch="rot0")
                if with_rel:
                    emit_strips(h, po, hh, blk, qts)
                cross = (hh == 1 and blk == 3)
                carries = emit_tiles(h, po, hh, blk, par, qts, with_rel,
                                     pending, not cross)
                qd_cons[b2] = pl.mark(("tensor", pl.last_real["tensor"]))
                stages = make_post(h, po, hh, blk, par, qts, with_rel,
                                   ring_cons, tilestate[0])
                pending = carries + [(6, stages[0]), (8, stages[1]),
                                     (11, stages[2]), (13, stages[3])]
            return pending

        wo_out = []

        bo_cons = [None]

        def wo_phase():
            barr = [psW_tok[0], psW_tok[1], psR_tok[0], psR_tok[1],
                    eW_tok[0], eW_tok[1], eR_tok[0], eR_tok[1],
                    rotK_tok[0], qd_tok[0], qd_tok[1]] +                    [ring_cons[i] for i in range(6)] +                    [pl.mark((e_, pl.last_real[e_]))
                    for e_ in ("tensor", "vector", "scalar")]
            for t_ in barr:
                for e_ in ("sync", "gpsimd", "tensor", "vector", "scalar"):
                    pl.wait(e_, t_)
            wp2 = [None, None]
            wrv_toks = []
            for oc in range(8):
                bslot = oc % 2
                if wp2[bslot] is not None:
                    pl.wait("sync", wp2[bslot])
                wtok = pl.dma("sync", wpb[bslot],
                              W["wo"].ap()[128 * oc:128 * (oc + 1), :], ch=f"wp{bslot}")
                pl.wait("vector", wtok)
                pl.fence("vector")
                ve(lambda e, b=wpb[bslot]: e.tensor_copy(hh_t, b))
                pl.fence("vector")
                spv = ve(lambda e, b=wpb[bslot]: e.tensor_tensor(
                    hl_t, b, hh_t, ALU.subtract))
                wp2[bslot] = pl.mark(spv)
                sp_tok = pl.fence("vector")
                pl.wait("tensor", sp_tok)
                transpose_to(WoT8[oc][0].rearrange("p (c n) -> p c n", n=128),
                             [hh_t[:, 128 * dc:128 * (dc + 1)] for dc in range(ND)],
                             0, src3d=True)
                wotok = transpose_to(WoT8[oc][1].rearrange("p (c n) -> p c n", n=128),
                                     [hl_t[:, 128 * dc:128 * (dc + 1)] for dc in range(ND)],
                                     1, src3d=True)
                WoTh3 = WoT8[oc][0].rearrange("p (c n) -> p c n", n=128)
                WoTl3 = WoT8[oc][1].rearrange("p (c n) -> p c n", n=128)
                pl.wait("tensor", wotok)
                pl.wait("tensor", rv_tok)
                bank_wait("tensor", "pb5")
                for dc in range(ND):
                    mm(pb[5][:, 0:1], WoTh3[:, dc], rvcat_h, dc == 0, False)
                    mm(pb[5][:, 0:1], WoTh3[:, dc], rvcat_l, False, False)
                    wrm = mm(pb[5][:, 0:1], WoTl3[:, dc], rvcat_h, False,
                             dc == ND - 1)
                pl.wait("sync", bo_cons[0])
                bctok = pl.dma("sync", nsc[:, 0:1],
                               bass.AP(Bi["bo"], 128 * oc, [[1, 128], [1, 1]]),
                               ch="boc")
                pl.wait("vector", pl.mark(wrm))
                pl.wait("vector", bctok)
                pl.fence("vector")
                bev = ve(lambda e, d=boext8[:, oc:oc + 1]: e.tensor_tensor(
                    d, pb[5][:, 0:1], nsc[:, 0:1], ALU.add))
                bo_cons[0] = pl.mark(bev)
                bank_set("pb5", bo_cons[0])
            ycons_a = [None, None]
            attb_cons = None
            for blk in range(4):
                cols = slice(512 * blk, 512 * (blk + 1))
                if attb_cons is not None:
                    pl.wait("sync", attb_cons)
                for t_ in att_wtoks[-4:]:
                    pl.wait("sync", t_)
                pl.dma("sync", attb_h3, bass.AP(
                    att_hbm, 512 * blk, [[T, 128], [128 * T, ND], [1, 512]]),
                    ch="atr")
                abtok = pl.dma("sync", attb_l3, bass.AP(
                    att_hbm, ND * 128 * T + 512 * blk,
                    [[T, 128], [128 * T, ND], [1, 512]]), ch="atr")
                pl.wait("tensor", abtok)
                ao_toks = []
                for oc in range(8):
                    WoTh3 = WoT8[oc][0].rearrange("p (c n) -> p c n", n=128)
                    WoTl3 = WoT8[oc][1].rearrange("p (c n) -> p c n", n=128)
                    bk_ = oc % 2
                    bank_wait("tensor", f"pb{bk_}")
                    for dc in range(ND):
                        mm(pb[bk_], WoTh3[:, dc], attb_h3[:, dc], dc == 0, False)
                        mm(pb[bk_], WoTh3[:, dc], attb_l3[:, dc], False, False)
                        prm = mm(pb[bk_], WoTl3[:, dc], attb_h3[:, dc],
                                 False, dc == ND - 1)
                    prt = pl.mark(prm)
                    attb_cons = prt
                    pl.wait("vector", prt)
                    pl.fence("vector")
                    aov = ve(lambda e, d=aosc[oc], b=pb[bk_][:], c=boext8[:, oc:oc + 1]:
                             e.tensor_scalar(d, b, c, None, ALU.add))
                    bank_set(f"pb{bk_}", pl.mark(aov))
                    ao_toks.append(pl.mark(aov))
                for jt in range(4):
                    tau = 4 * blk + jt
                    yb = ytile_a[jt % 2]
                    if ycons_a[jt % 2] is not None:
                        pl.wait("gpsimd", ycons_a[jt % 2])
                    xt2 = pl.dma("gpsimd", yb,
                                 src_dram.ap()[128 * tau:128 * (tau + 1), :],
                                 ch=f"ya{jt % 2}")
                    for oc in range(8):
                        pl.wait("tensor", ao_toks[oc])
                        bank_wait("tensor", f"pb{2 + oc % 2}")
                        trm = pl.emit("tensor", lambda e, s=aosc[oc][:, 128 * jt:128 * jt + 128],
                                      b=pb[2 + oc % 2][:, 0:128]:
                                      e.transpose(b, s, id_f32))
                        trt2 = pl.mark(trm)
                        pl.wait("vector", trt2)
                        pl.wait("vector", xt2)
                        pl.fence("vector")
                        yv = ve(lambda e, d=yb[:, 128 * oc:128 * oc + 128],
                                b=pb[2 + oc % 2][:, 0:128]:
                                e.tensor_tensor(d, d, b, ALU.add))
                        bank_set(f"pb{2 + oc % 2}", pl.mark(yv))
                    ytk = pl.fence("vector")
                    pl.wait("gpsimd", ytk)
                    stk = pl.dma("gpsimd", dst_dram.ap()[128 * tau:128 * (tau + 1), :],
                                 yb, ch=f"ys{jt % 2}")
                    ycons_a[jt % 2] = stk
                    wo_out.append(stk)

        pending_post = [None]
        if with_rel:
            pl.wait("scalar", rv_tok)
            pl.wait("scalar", rkx_tok0)
            wp_cons = [tt1_tok, rv_tok]
        else:
            wp_cons = [None, None]
        brow_cons = [None]
        for hp in range(8):
            po2 = 128 * hp
            for iw, (w, wt_d, bi_d, kscale) in enumerate([
                    ("q", W["wq"], Bi["bq"], None),
                    ("k", W["wk"], Bi["bk"], 0.125),
                    ("v", W["wv"], Bi["bv"], None)]):
                bslot = iw % 2
                if wp_cons[bslot] is not None:
                    pl.wait("sync", wp_cons[bslot])
                wtok = pl.dma("sync", wpb[bslot],
                              wt_d.ap()[po2:po2 + 128, :], ch=f"wp{bslot}")
                pl.wait("sync", brow_cons[0])
                btk = pl.dma("sync", brow, bass.AP(bi_d, po2, [[1, 1], [1, 128]]),
                             ch="br")
                pl.wait("vector", wtok)
                pl.wait("vector", btk)
                pl.fence("vector")
                if kscale is not None:
                    ve(lambda e, b=wpb[bslot], s=kscale: e.tensor_scalar(
                        b, b, s, None, ALU.mult))
                    ve(lambda e, s=kscale: e.tensor_scalar(brow, brow, s, None, ALU.mult))
                    pl.fence("vector")
                brt = pl.fence("vector")
                pl.wait("tensor", brt)
                pl.wait("tensor", btk)
                bank_wait("tensor", "pb2")
                btr = pl.emit("tensor", lambda e: e.transpose(
                    pb[2][:, 0:1], brow, onesrow[0:1, 0:1]))
                brow_cons[0] = pl.mark(btr)
                pl.wait("vector", brow_cons[0])
                ve(lambda e: e.tensor_copy(bcol_s, pb[2][:, 0:1]))
                bank_set("pb2", pl.fence("vector") or None)
                pl.fence("vector")
                ve(lambda e, d=bc2[:, 0:1]: e.tensor_copy(d, bcol_s))
                pl.fence("vector")
                ve(lambda e, d=bc2[:, 1:2], hb=bc2[:, 0:1]:
                   e.tensor_tensor(d, bcol_s, hb, ALU.subtract))
                bct = pl.fence("vector")
                pl.wait("tensor", bct)
                bank_wait("tensor", "pt0")
                bt2 = pl.emit("tensor", lambda e: e.transpose(
                    pt[0][0:2, 0:128], bc2, id_b))
                pl.wait("vector", pl.mark(bt2))
                b2v = ve(lambda e, d=bias2[w]: e.tensor_copy(d, pt[0][0:2, 0:128]))
                bank_set("pt0", pl.mark(b2v))
                ve(lambda e, b=wpb[bslot]: e.tensor_copy(hh_t, b))
                pl.fence("vector")
                spv = ve(lambda e, b=wpb[bslot]: e.tensor_tensor(
                    hl_t, b, hh_t, ALU.subtract))
                wp_cons[bslot] = pl.mark(spv)
                sp_tok = pl.fence("vector")
                pl.wait("tensor", sp_tok)
                transpose_to(WT[w][0].rearrange("p (c n) -> p c n", n=128),
                             [hh_t[:, 128 * dc:128 * (dc + 1)] for dc in range(ND)],
                             0, src3d=True)
                transpose_to(WT[w][1].rearrange("p (c n) -> p c n", n=128),
                             [hl_t[:, 128 * dc:128 * (dc + 1)] for dc in range(ND)],
                             1, src3d=True)
            WTq3 = [WT["q"][k].rearrange("p (c n) -> p c n", n=128) for k in (0, 1)]
            WTk3 = [WT["k"][k].rearrange("p (c n) -> p c n", n=128) for k in (0, 1)]
            WTv3 = [WT["v"][k].rearrange("p (c n) -> p c n", n=128) for k in (0, 1)]
            pl.wait("tensor", hT_tok)
            for blk in range(4):
                cols = slice(512 * blk, 512 * (blk + 1))
                for w, WT3, Dh, Dl in (("q", WTq3, Qh, Ql), ("k", WTk3, Kh, Kl)):
                    bk_ = blk % 2
                    bank_wait("tensor", f"pb{bk_}")
                    for dc in range(ND):
                        rhsh = hTh4[:, dc, 4 * blk:4 * blk + 4, :]
                        rhsl = hTl4[:, dc, 4 * blk:4 * blk + 4, :]
                        mm(pb[bk_], WT3[0][:, dc], rhsh, dc == 0, False)
                        mm(pb[bk_], WT3[0][:, dc], rhsl, False, False)
                        mm(pb[bk_], WT3[1][:, dc], rhsh, False, False)
                    mmk = mm(pb[bk_], bias2[w], ones2row, False, True)
                    tok_mm = pl.mark(mmk)
                    pl.wait("vector", tok_mm)
                    pl.fence("vector")
                    ve(lambda e, b=pb[bk_][:], d=Dh[:, cols]: e.tensor_copy(d, b))
                    pl.fence("vector")
                    evv = ve(lambda e, b=pb[bk_][:], hb=Dh[:, cols], d=Dl[:, cols]:
                             e.tensor_tensor(d, b, hb, ALU.subtract))
                    bank_set(f"pb{bk_}", pl.mark(evv))
            for tau in range(NT):
                bk_ = 2 + tau % 2
                bank_wait("tensor", f"pb{bk_}")
                for dc in range(ND):
                    lh = hTh4[:, dc, tau, :]
                    ll = hTl4[:, dc, tau, :]
                    mm(pb[bk_][:, 0:128], lh, WTv3[0][:, dc], dc == 0, False)
                    mm(pb[bk_][:, 0:128], lh, WTv3[1][:, dc], False, False)
                    mm(pb[bk_][:, 0:128], ll, WTv3[0][:, dc], False, False)
                mmk = mm(pb[bk_][:, 0:128], ones2row[:, 0:128], bias2["v"],
                         False, True)
                tok_mm = pl.mark(mmk)
                pl.wait("vector", tok_mm)
                pl.fence("vector")
                ve(lambda e, b=pb[bk_][:, 0:64],
                   d=Vx_h3[:, tau, 0:64]: e.tensor_copy(d, b))
                ve(lambda e, b=pb[bk_][:, 64:128],
                   d=Vx_h3[:, tau, 96:160]: e.tensor_copy(d, b))
                pl.fence("vector")
                ve(lambda e, b=pb[bk_][:, 0:64], hb=Vx_h3[:, tau, 0:64],
                   d=Vx_l3[:, tau, 0:64]:
                   e.tensor_tensor(d, b, hb, ALU.subtract))
                evv = ve(lambda e, b=pb[bk_][:, 64:128], hb=Vx_h3[:, tau, 96:160],
                         d=Vx_l3[:, tau, 96:160]:
                         e.tensor_tensor(d, b, hb, ALU.subtract))
                bank_set(f"pb{bk_}", pl.mark(evv))
            ind3 = ind_b.rearrange("p (t n) -> p t n", n=16)
            pl.fence("vector")
            ve(lambda e, d=Vx_h3[:, :, 64:80]: e.tensor_copy(d, ind3))
            ve(lambda e, d=Vx_h3[:, :, 80:96]: e.tensor_scalar(
                d, ind3, 0.0, None, ALU.mult))
            ve(lambda e, d=Vx_l3[:, :, 64:80]: e.tensor_scalar(
                d, ind3, 0.0, None, ALU.mult))
            ve(lambda e, d=Vx_l3[:, :, 80:96]: e.tensor_scalar(
                d, ind3, 0.0, None, ALU.mult))
            for src_b, col0 in ((Qh, 0), (Kh, 2)):
                pl.fence("vector")
                ve(lambda e, s=src_b: e.tensor_tensor(sqb, s, s, ALU.mult))
                sq_tok = pl.fence("vector")
                pl.wait("tensor", sq_tok)
                for blkn in range(4):
                    bank_wait("tensor", "pb2")
                    nm2 = mm(pb[2][0:2, 0:512], hm_b,
                             sqb[:, 512 * blkn:512 * (blkn + 1)], True, True)
                    pl.wait("vector", pl.mark(nm2))
                    pl.fence("vector")
                    rv2 = ve(lambda e: e.reduce_max(macc[0:2, 1:2],
                                                    pb[2][0:2, 0:512], axis=AX))
                    bank_set("pb2", pl.mark(rv2))
                    pl.fence("vector")
                    if blkn == 0:
                        ve(lambda e: e.tensor_copy(macc[0:2, 0:1], macc[0:2, 1:2]))
                    else:
                        ve(lambda e: e.tensor_tensor(macc[0:2, 0:1], macc[0:2, 0:1],
                                                     macc[0:2, 1:2], ALU.max))
                tp_tok = pl.fence("vector")
                pl.wait("tensor", tp_tok)
                bank_wait("tensor", "pb2")
                tmm = mm(pb[2][0:1, col0:col0 + 2], macc[0:2, 0:1], id2_f, True, True)
                pl.wait("vector", pl.mark(tmm))
                mvv = ve(lambda e, c=col0: e.tensor_copy(mrow4[:, c:c + 2],
                                                         pb[2][0:1, c:c + 2]))
                bank_set("pb2", pl.mark(mvv))
                mvt = pl.fence("vector")
            pl.wait("scalar", mvt)
            sq3 = sc(lambda e: e.activation(mrow4, mrow4, ACT.Sqrt))
            pl.wait("vector", pl.mark(sq3))
            pl.fence("vector")
            ve(lambda e: e.tensor_scalar(tb_s, mrow4[:, 2:4], rkm, None, ALU.add))
            pl.fence("vector")
            nbv = ve(lambda e: e.scalar_tensor_tensor(
                tb_s, tb_s, -1.05, mrow4[:, 0:2], ALU.mult, ALU.mult))
            nb_tok = pl.fence("vector")
            pl.wait("tensor", nb_tok)
            bank_wait("tensor", "pb2")
            nbm = mm(pb[2][:, 0:2], onesrow, tb_s, True, True)
            pl.wait("vector", pl.mark(nbm))
            nbe = ve(lambda e: e.tensor_copy(negB2, pb[2][:, 0:2]))
            bank_set("pb2", pl.mark(nbe))
            nbe_t = pl.fence("vector")
            rot_gate[0] = nbe_t
            pl.wait("sync", nbe_t)
            for hh2 in range(2):
                pl.dma("sync", Kcat3[0:64, hh2, :],
                       Kh[64 * hh2:64 * hh2 + 64, :], ch="rot1")
                rotK_tok[0] = pl.dma("sync", Kcat3[64:128, hh2, :],
                                     Kl[64 * hh2:64 * hh2 + 64, :], ch="rot1")
            for hh in range(2):
                pending_post[0] = attn_head(2 * hp + hh, 64 * hh, hh, with_rel,
                                            pending_post[0])
        if pending_post[0] is not None:
            for sl_, fn_ in sorted(pending_post[0], key=lambda x: x[0]):
                fn_()
            pending_post[0] = None
        wo_phase()
        out_toks = wo_out[-2:] + [pl.mark((e_, pl.last_real[e_]))
                                  for e_ in ("tensor", "vector", "scalar")]
        al.pop()
        return out_toks


    prep_done = [tok_mrep, btok.get("wqfull"),
                 wq_store["wd"][0], wq_store["wd"][1]]
    if stage == 1:
        final_toks = ffn_sublayer(x_in, y_out, "ffn1", pre_tok=prep_done)
    elif stage == 3:
        final_toks = attention_sublayer(x_in, y_out, with_rel=False,
                                        pre_tok=prep_done)
    elif stage == 4:
        final_toks = attention_sublayer(x_in, y_out, with_rel=True,
                                        pre_tok=prep_done)
    elif stage == 5:
        t1 = ffn_sublayer(x_in, xres, "ffn1", pre_tok=prep_done)
        t2 = attention_sublayer(xres, xres, with_rel=True, pre_tok=t1)
        final_toks = ffn_sublayer(xres, y_out, "ffn2", pre_tok=t2)
    else:
        raise NotImplementedError

    for t in final_toks:
        teng, kind, sem, val = t
        if kind == "dma":
            pl.emit("gpsimd", lambda e, s=sem, v=val: e.wait_ge(s, v))

    pl.replay()
    return nc


_NC_CACHE = {}


def _get_nc(stage):
    if stage not in _NC_CACHE:
        _NC_CACHE[stage] = build_program(stage)
    return _NC_CACHE[stage]


def kernel(L0_attn_x, L0_attn_Wq, L0_attn_bq, L0_attn_Wk, L0_attn_bk,
           L0_attn_Wv, L0_attn_bv, L0_attn_Wo, L0_attn_bo,
           L0_attn_rel_k, L0_attn_rel_v,
           L0_ffn_w_gate, L0_ffn_w_up, L0_ffn_w_down, _stage=5):
    nc = _get_nc(_stage)
    shared = dict(
        wq=np.ascontiguousarray(L0_attn_Wq, dtype=np.float32),
        wk=np.ascontiguousarray(L0_attn_Wk, dtype=np.float32),
        wv=np.ascontiguousarray(L0_attn_Wv, dtype=np.float32),
        wo=np.ascontiguousarray(L0_attn_Wo, dtype=np.float32),
        bq=np.ascontiguousarray(L0_attn_bq, dtype=np.float32),
        bk=np.ascontiguousarray(L0_attn_bk, dtype=np.float32),
        bv=np.ascontiguousarray(L0_attn_bv, dtype=np.float32),
        bo=np.ascontiguousarray(L0_attn_bo, dtype=np.float32),
        relk=np.ascontiguousarray(L0_attn_rel_k, dtype=np.float32),
        relv=np.ascontiguousarray(L0_attn_rel_v, dtype=np.float32),
        wg=np.ascontiguousarray(L0_ffn_w_gate, dtype=np.float32),
        wu=np.ascontiguousarray(L0_ffn_w_up, dtype=np.float32),
        wd=np.ascontiguousarray(L0_ffn_w_down, dtype=np.float32),
    )
    shared.update({k: v for k, v in CONSTS.items()})
    xb = np.ascontiguousarray(np.asarray(L0_attn_x, dtype=np.float32))
    in_maps = []
    for c in range(8):
        m = dict(shared)
        m["x_in"] = xb[c]
        in_maps.append(m)
    res = run_bass_kernel_spmd(nc, in_maps, core_ids=list(range(8)))
    return np.stack([r["y_out"] for r in res.results], axis=0)


# version 18
# speedup vs baseline: 1.0168x; 1.0168x over previous
import numpy as np
import ml_dtypes
import concourse.bass as bass
import concourse.mybir as mybir
from concourse.bass_utils import run_bass_kernel_spmd

F32 = mybir.dt.float32
BF16 = mybir.dt.bfloat16
I32 = mybir.dt.int32
AX = mybir.AxisListType.X
ALU = mybir.AluOpType
ACT = mybir.ActivationFunctionType

T, D, FF, H, HD = 2048, 1024, 4096, 16, 64
NT, ND, NF = T // 128, D // 128, FF // 128
ENGS = ("sync", "scalar", "tensor", "vector", "gpsimd")
BB = ml_dtypes.bfloat16


class Op:
    __slots__ = ("fn", "incs")

    def __init__(self, fn):
        self.fn = fn
        self.incs = []


class Plan:
    def __init__(self, nc):
        self.nc = nc
        self.q = {e: [] for e in ENGS}
        self.ctr = {e: [nc.alloc_semaphore(name=f"c_{e}"), 0] for e in ENGS}
        self.dch = {}
        self.waited = {e: {} for e in ENGS}

    def emit(self, eng, fn, is_wait=False, is_dma=False):
        op = Op(fn)
        self.q[eng].append(op)
        if not (is_wait or is_dma):
            self.last_real = getattr(self, "last_real", {})
            self.last_real[eng] = op
        return (eng, op)

    def dma(self, eng, out, in_, ch="d0"):
        if ch not in self.dch:
            self.dch[ch] = [self.nc.alloc_semaphore(name=f"dm_{ch}"), 0]
        ent = self.dch[ch]
        sem = ent[0]
        if ent[1] > 0:
            self.wait(eng, (eng, "dma", sem, ent[1]))
        ent[1] += 16
        self.emit(eng, lambda e, o=out, i=in_, s=sem:
                  e.dma_start(out=o, in_=i).then_inc(s, 16), is_dma=True)
        return (eng, "dma", sem, ent[1])

    def mark(self, emitted):
        eng, op = emitted
        if op.incs:
            return (eng, "c", op.incs[0][0], op.incs[0][1])
        sem, v = self.ctr[eng]
        self.ctr[eng][1] = v + 1
        op.incs.append((sem, v + 1))
        return (eng, "c", sem, v + 1)

    def wait(self, eng, tok):
        if tok is None:
            return
        teng, kind, sem, val = tok
        seen = self.waited[eng].get(id(sem), 0)
        if val <= seen:
            return
        self.waited[eng][id(sem)] = val
        self.emit(eng, lambda e, s=sem, v=val: e.wait_ge(s, v), is_wait=True)

    def fence(self, eng):
        op = getattr(self, "last_real", {}).get(eng)
        if op is None:
            return None
        tok = self.mark((eng, op))
        self.wait(eng, tok)
        return tok

    def waits(self, eng, toks):
        for t in toks:
            self.wait(eng, t)

    def replay(self):
        with self.nc.Block() as blk:
            for name in ENGS:
                ops = self.q[name]

                def mk(ops):
                    def body(eng):
                        for op in ops:
                            ins = op.fn(eng)
                            for sem, val in op.incs:
                                ins.then_inc(sem, 1)
                    return body

                getattr(blk, name)(mk(ops))


def bfv(ap):
    return ap.bitcast(BF16)


def build_consts():
    c = {}
    c["c_idb"] = np.eye(128).astype(BB)
    ind = np.zeros((128, 16, 16), dtype=np.float32)
    for t in range(16):
        ind[:, t, t] = 1.0
    c["c_ind"] = ind.astype(BB)
    p = np.arange(128)
    c["c_lt"] = (p[None, :] <= p[:, None]).astype(BB)
    pre = np.zeros((16, 4, 5), dtype=np.float32)
    for b in range(4):
        for j in range(4):
            pre[: max(4 * b + j - 1, 0), b, j] = 1.0
        pre[:, b, 4] = 1.0
    c["c_pre"] = pre.astype(BB)
    c["c_onesrow"] = np.ones((1, 128), dtype=np.float32)
    c["c_idf32"] = np.eye(128, dtype=np.float32)
    c["c_id2"] = np.eye(2, dtype=np.float32)
    c["c_hmb"] = c["c_hm"] if False else None
    c["c_pre32"] = None
    c["c_zeros"] = np.zeros((2, 512)).astype(BB)
    c["c_onescol"] = np.ones((128, 1), dtype=np.float32)
    c["c_ones2row"] = np.ones((2, 512)).astype(BB)
    sel = np.zeros((2, 4, 128), dtype=np.float32)
    sel[0, 0:2] = 1.0
    sel[1, 2:4] = 1.0
    c["c_sel"] = sel.astype(BB)
    hm = np.zeros((128, 2), dtype=np.float32)
    hm[0:64, 0] = 1.0
    hm[64:128, 1] = 1.0
    c["c_hm"] = hm
    c["c_hmb"] = hm.astype(BB)
    c["c_pre32"] = np.ascontiguousarray(c["c_pre"].astype(np.float32))
    return c


CONSTS = {k: v for k, v in build_consts().items() if v is not None}


def build_program(stage=3):
    nc = bass.Bass("TRN2", target_bir_lowering=False, debug=False)
    pl = Plan(nc)

    x_in = nc.dram_tensor("x_in", [T, D], F32, kind="ExternalInput")
    W = {n: nc.dram_tensor(n, list(s), F32, kind="ExternalInput") for n, s in
         [("wq", (D, D)), ("wk", (D, D)), ("wv", (D, D)), ("wo", (D, D)),
          ("wg", (FF, D)), ("wu", (FF, D)), ("wd", (D, FF))]}
    Bi = {n: nc.dram_tensor(n, [D], F32, kind="ExternalInput")
          for n in ("bq", "bk", "bv", "bo")}
    relk = nc.dram_tensor("relk", [257, HD], F32, kind="ExternalInput")
    relv = nc.dram_tensor("relv", [257, HD], F32, kind="ExternalInput")
    CT = {k: nc.dram_tensor(k, list(v.shape),
                            F32 if v.dtype == np.float32 else BF16,
                            kind="ExternalInput") for k, v in CONSTS.items()}
    y_out = nc.dram_tensor("y_out", [T, D], F32, kind="ExternalOutput")

    xres = nc.dram_tensor("xres", [T, D], F32, kind="Internal")
    wqh = {"wg": nc.dram_tensor("wgq_h", [D, FF], BF16, kind="Internal"),
           "wu": nc.dram_tensor("wuq_h", [D, FF], BF16, kind="Internal"),
           "wd": nc.dram_tensor("wdq_h", [FF, D], BF16, kind="Internal")}
    e_h = nc.dram_tensor("e_h", [4, 2, 128 * 511], BF16, kind="Internal")
    att_hbm = nc.dram_tensor("att_hbm", [2 * ND * 128 * T], BF16, kind="Internal")
    s_h = nc.dram_tensor("s_h", [2, 2, 128 * 384], BF16, kind="Internal")

    ACOLS = 53056
    arena = nc.alloc_sbuf_tensor("arena", [128, ACOLS], F32)
    AAP = arena.ap()

    class Alloc:
        def __init__(self):
            self.top = 0
            self.marks = []
            self.peak = 0

        def take(self, ncols):
            a = self.top
            assert a + ncols <= ACOLS, f"arena overflow {a}+{ncols}"
            self.top = a + ncols
            self.peak = max(self.peak, self.top)
            return AAP[:, a:a + ncols]

        def takeb(self, nb):
            return bfv(self.take((nb + 1) // 2))[:, 0:nb]

        def push(self):
            self.marks.append(self.top)

        def pop(self):
            self.top = self.marks.pop()

    al = Alloc()

    pb = [nc.alloc_psum_tensor(f"pb{i}", [128, 512], F32) for i in range(6)]
    pt = [nc.alloc_psum_tensor(f"pt{i}", [128, 1024], BF16) for i in range(2)]
    btok = {}

    def bank_wait(eng, key):
        pl.wait(eng, btok.get(key))

    def bank_set(key, tok):
        btok[key] = tok

    id_b = al.takeb(128)
    ind_b = al.takeb(256)
    lt_b = al.takeb(128)
    pre_b = al.takeb(20)[0:16, :]
    onesrow = al.take(128)[0:1, :]
    onescol = al.take(1)
    ones2row = al.takeb(512)[0:2, :]
    hm_f = al.take(2)
    id_f32 = al.take(128)
    id2_f = al.take(2)[0:2, :]
    hm_b = al.takeb(2)
    pre32_b = al.take(20)[0:16, :]
    zrows_b = al.takeb(512)[0:2, :]
    zcol_b = zrows_b[0:1, 0:1]
    zrow_b = zrows_b[0:1, 0:512]
    toks = [
        pl.dma("sync", id_f32, CT["c_idf32"].ap(), ch="const"),
        pl.dma("sync", id2_f, CT["c_id2"].ap(), ch="const"),
        pl.dma("sync", hm_b, CT["c_hmb"].ap(), ch="const"),
        pl.dma("sync", pre32_b, CT["c_pre32"].ap().rearrange("p a b -> p (a b)"), ch="const"),
        pl.dma("sync", zrows_b, CT["c_zeros"].ap(), ch="const"),
        pl.dma("sync", id_b, CT["c_idb"].ap(), ch="const"),
        pl.dma("sync", ind_b, CT["c_ind"].ap().rearrange("p a b -> p (a b)"), ch="const"),
        pl.dma("sync", lt_b, CT["c_lt"].ap(), ch="const"),
        pl.dma("sync", pre_b, CT["c_pre"].ap().rearrange("p a b -> p (a b)"), ch="const"),
        pl.dma("sync", onesrow, CT["c_onesrow"].ap(), ch="const"),
        pl.dma("sync", onescol, CT["c_onescol"].ap(), ch="const"),
        pl.dma("sync", ones2row, CT["c_ones2row"].ap(), ch="const"),
        pl.dma("sync", hm_f, CT["c_hm"].ap(), ch="const"),
    ]
    sel4 = al.takeb(256)[0:4, :]
    toks.append(pl.dma("sync", sel4.rearrange("p (s n) -> p s n", n=128),
                       bass.AP(CT["c_sel"], 0, [[128, 4], [512, 2], [1, 128]]), ch="const"))
    for e in ENGS:
        pl.wait(e, toks[-1])

    def ap_of(x):
        return x if isinstance(x, bass.AP) else x[:]

    def mm(bank_ap, lhsT, rhs, start, stop, skip=False):
        return pl.emit("tensor", lambda e, a=ap_of(bank_ap), l=ap_of(lhsT),
                       r=ap_of(rhs), s=start, t=stop, k=skip:
                       e.matmul(a, l, r, start=s, stop=t, skip_group_check=k))

    def tr(pt_ap, src_b):
        return pl.emit("tensor", lambda e, a=pt_ap, s=src_b:
                       e.transpose(a, s, id_b))

    def ve(fn):
        return pl.emit("vector", fn)

    def sc(fn):
        return pl.emit("scalar", fn)

    def transpose_to(dst, src_blocks, ptsel, src3d=False):
        nblk = len(src_blocks)
        assert nblk <= 8
        pt_b = pt[ptsel]
        bank_wait("tensor", f"pt{ptsel}")
        for k, s in enumerate(src_blocks):
            ttk = tr(pt_b[:, 128 * k:128 * (k + 1)], s)
        tok = pl.mark(ttk)
        pl.wait("vector", tok)
        srcv = pt_b[:, 0:128 * nblk]
        if src3d:
            srcv = srcv.rearrange("p (c n) -> p c n", n=128)
        evk = ve(lambda e, d=dst, s=srcv: e.tensor_copy(d, s))
        etok = pl.mark(evk)
        bank_set(f"pt{ptsel}", etok)
        return etok

    al.push()
    wbuf = [al.take(1024) for _ in range(2)]
    wsum = al.take(1)
    red1 = al.take(1)
    mrow = al.take(1)[0:1, :]
    srow = al.take(1)[0:1, :]
    srep = al.take(1)
    dqk = al.take(3)[0:1, :]
    i32t = al.take(1024)
    wq_full = al.takeb(NF * 1024)
    rowbuf = [al.takeb(4096) for _ in range(2)]

    wspec = [("wg", W["wg"], wqh["wg"], FF, D), ("wu", W["wu"], wqh["wu"], FF, D),
             ("wd", W["wd"], wqh["wd"], D, FF)]
    wq_store = {}
    cons_tok = [None, None]
    st_tok = [None, None]
    for wi, (nm, wt, wqt, RO, CO) in enumerate(wspec):
        nch, ncc = RO // 128, CO // 128
        npc = CO // 1024
        pieces = [(ch, cp) for ch in range(nch) for cp in range(npc)]
        pl.fence("vector")
        ve(lambda e: e.tensor_scalar(wsum, onescol, 0.0, None, ALU.mult))
        lastv = None
        for pi, (ch, cp) in enumerate(pieces):
            bi = pi % 2
            if cons_tok[bi] is not None:
                pl.wait("sync", cons_tok[bi])
            ltok = pl.dma("sync", wbuf[bi],
                          wt.ap()[128 * ch:128 * (ch + 1), 1024 * cp:1024 * (cp + 1)],
                          ch=f"wqp{bi}")
            pl.wait("vector", ltok)
            pl.fence("vector")
            rv = ve(lambda e, b=wbuf[bi]: e.reduce_sum(
                red1, b, axis=AX, apply_absolute_value=True))
            cons_tok[bi] = pl.mark(rv)
            pl.fence("vector")
            lastv = ve(lambda e: e.tensor_tensor(wsum, wsum, red1, ALU.add))
        tok_sums = pl.mark(lastv)
        pl.wait("tensor", tok_sums)
        bank_wait("tensor", "pb0")
        m1 = mm(pb[0][0:1, 0:1], onescol, wsum, True, True)
        pl.wait("vector", pl.mark(m1))
        ve(lambda e, s=1.0 / (RO * CO): e.tensor_scalar(
            mrow, pb[0][0:1, 0:1], s, 1e-5, ALU.mult, ALU.max))
        pl.fence("vector")
        ve(lambda e: e.reciprocal(srow, mrow))
        kv = ve(lambda e, c=dqk[:, wi:wi + 1]: e.tensor_copy(c, mrow))
        tok_s = pl.mark(kv)
        bank_set("pb0", tok_s)
        pl.wait("tensor", tok_s)
        bank_wait("tensor", "pb0")
        m2 = mm(pb[0][:, 0:1], onesrow, srow, True, True)
        pl.wait("vector", pl.mark(m2))
        evs = ve(lambda e: e.tensor_copy(srep, pb[0][:, 0:1]))
        bank_set("pb0", pl.mark(evs))
        bank_wait("vector", "wqfull")
        wq3 = wq_full.rearrange("p (c n) -> p c n", n=CO)
        qv = None
        for pi, (ch, cp) in enumerate(pieces):
            bi = pi % 2
            if cons_tok[bi] is not None:
                pl.wait("sync", cons_tok[bi])
            ltok = pl.dma("sync", wbuf[bi],
                          wt.ap()[128 * ch:128 * (ch + 1), 1024 * cp:1024 * (cp + 1)],
                          ch=f"wqp{bi}")
            pl.wait("vector", ltok)
            pl.fence("vector")
            fv = ve(lambda e, b=wbuf[bi]: e.tensor_scalar(
                i32t.bitcast(I32)[:, 0:1024], b, srep, None, ALU.mult))
            cons_tok[bi] = pl.mark(fv)
            pl.fence("vector")
            qv = ve(lambda e, d=wq3[:, ch, 1024 * cp:1024 * (cp + 1)]:
                    e.tensor_scalar(d, i32t.bitcast(I32)[:, 0:1024],
                                    -1.0, 1.0, ALU.max, ALU.min))
        tok_q = pl.mark(qv)
        pl.wait("tensor", tok_q)
        for rc in range(ncc):
            bo = rc % 2
            if st_tok[bo] is not None:
                pl.wait("vector", st_tok[bo])
            evk_tok = None
            for rnd in range((nch + 7) // 8):
                nblk = min(8, nch - 8 * rnd)
                blocks = [wq3[:, 8 * rnd + k, 128 * rc:128 * rc + 128]
                          for k in range(nblk)]
                evk_tok = transpose_to(
                    rowbuf[bo][:, 1024 * rnd:1024 * rnd + 128 * nblk],
                    blocks, rnd % 2)
            pl.wait("gpsimd", evk_tok)
            st_tok[bo] = pl.dma("gpsimd", wqt.ap()[128 * rc:128 * (rc + 1), :],
                                rowbuf[bo][:, 0:RO], ch=f"wqs{bo}")
        bank_set("wqfull", evk_tok)
        wq_store[nm] = [st_tok[0], st_tok[1]]
    al.pop()

    mrep = al.take(3)
    bank_wait("tensor", "pb0")
    m3 = mm(pb[0][:, 0:3], onesrow, dqk[:, 0:3], True, True)
    pl.wait("vector", pl.mark(m3))
    evm = ve(lambda e: e.tensor_copy(mrep, pb[0][:, 0:3]))
    tok_mrep = pl.mark(evm)
    bank_set("pb0", tok_mrep)

    ln_m = al.take(1)
    ln_sq = al.take(1)
    ln_v = al.take(1)
    ln_rs = al.take(1)
    qa = al.take(1)
    qs = al.take(1)
    i32q = al.take(1024)
    sq_scr = al.take(1024)

    def ln_tile(xt, h_out):
        ve(lambda e: e.reduce_sum(ln_m, xt, axis=AX))
        pl.fence("vector")
        tok = pl.mark(ve(lambda e: e.tensor_scalar(ln_m, ln_m, 1.0 / D, None, ALU.mult)))
        s1 = sc(lambda e: e.activation(sq_scr, xt, ACT.Square, accum_out=ln_sq))
        pl.wait("vector", pl.mark(s1))
        pl.wait("vector", tok)
        ve(lambda e: e.tensor_tensor(ln_v, ln_m, ln_m, ALU.mult))
        pl.fence("vector")
        ve(lambda e: e.tensor_scalar(
            ln_v, ln_sq, 1.0 / D, ln_v, ALU.mult, ALU.subtract))
        pl.fence("vector")
        t2 = pl.mark(ve(lambda e: e.tensor_scalar(
            ln_v, ln_v, 1e-5, None, ALU.add)))
        pl.wait("scalar", t2)
        s2 = sc(lambda e: e.activation(ln_rs, ln_v, ACT.Sqrt))
        pl.wait("vector", pl.mark(s2))
        ve(lambda e: e.reciprocal(ln_rs, ln_rs))
        pl.fence("vector")
        return pl.mark(ve(lambda e: e.tensor_scalar(
            h_out, xt, ln_m, ln_rs, ALU.subtract, ALU.mult)))

    def quant_tok(src, n, qdst_b, dq_col):
        pl.fence("vector")
        ve(lambda e: e.reduce_max(qa, src, axis=AX, apply_absolute_value=True))
        pl.fence("vector")
        ve(lambda e: e.tensor_scalar(qa, qa, 1e-5, 1.0 / 127.0, ALU.max, ALU.mult))
        pl.fence("vector")
        ve(lambda e: e.tensor_copy(dq_col, qa))
        ve(lambda e: e.reciprocal(qs, qa))
        nb = (n + 1023) // 1024
        for b in range(nb):
            c0, c1 = 1024 * b, min(1024 * (b + 1), n)
            pl.fence("vector")
            ve(lambda e, s=src[:, c0:c1], d=i32q.bitcast(I32)[:, 0:c1 - c0]:
               e.tensor_scalar(d, s, qs, None, ALU.mult))
            pl.fence("vector")
            qv = ve(lambda e, s=i32q.bitcast(I32)[:, 0:c1 - c0],
                    d=qdst_b[:, c0:c1]: e.tensor_copy(d, s))
        return pl.mark(qv)


    def ffn_sublayer(src_dram, dst_dram, tag, pre_tok=None):
        al.push()
        xtile = al.take(1024)
        htile = al.take(1024)
        xq_b = al.takeb(1024)
        xqT = al.takeb(ND * 8 * 128)
        xqT4 = xqT.rearrange("p (c t n) -> p c t n", t=8, n=128)
        sg = al.take(8)
        su = al.take(8)
        sd = al.take(8)
        dqx = al.take(1)
        pamax = al.take(8)
        p_region = al.take(8 * 4096)
        p_half = [p_region[:, 4096 * k:4096 * (k + 1)] for k in range(8)]
        wgb = [al.takeb(ND * 512) for _ in range(2)]
        wub = [al.takeb(ND * 512) for _ in range(2)]
        wgbF = [bfv(b).bitcast(F32) for b in wgb]
        ytile = [wgbF[0][:, 0:1024], wgbF[1][:, 0:1024]]
        gsil_t = al.take(512)
        gsil = [gsil_t, gsil_t]

        wgb3 = [b.rearrange("p (c n) -> p c n", n=512) for b in wgb]
        wub3 = [b.rearrange("p (c n) -> p c n", n=512) for b in wub]
        p_regb = bfv(p_region)
        wdb_chunks = [p_regb[:, 8192 * (oc // 4) + 1024 * (oc % 4):
                             8192 * (oc // 4) + 1024 * (oc % 4) + 1024]
                      for oc in range(NF)]
        wdb_dst = [p_regb[:, 8192 * a:8192 * a + 4096]
                   .rearrange("p (b n) -> p b n", n=1024) for a in range(8)]
        wdb_src = [bass.AP(wqh["wd"], 512 * D * a,
                           [[D, 128], [128 * D, 4], [1, 1024]])
                   for a in range(8)]

        xcons = None
        wcons = [None, None]
        wdcons = None
        ycons = [None, None]
        gtok = [None]
        pre_list = pre_tok if isinstance(pre_tok, list) else (
            [pre_tok] if pre_tok is not None else [])
        for tk_ in pre_list:
            for e_ in ("sync", "vector", "scalar", "tensor", "gpsimd"):
                pl.wait(e_, tk_)
        half_tok = None
        out_toks = []
        for half in range(2):
            xqT_tok = None
            for ti in range(8):
                tau = 8 * half + ti
                if xcons is not None:
                    pl.wait("sync", xcons)
                xtok = pl.dma("sync", xtile,
                              src_dram.ap()[128 * tau:128 * (tau + 1), :],
                              ch="xa")
                if half_tok is not None:
                    pl.wait("vector", half_tok)
                    pl.wait("scalar", half_tok)
                    pl.wait("tensor", half_tok)
                pl.wait("vector", xtok)
                pl.fence("vector")
                pl.wait("scalar", xtok)
                htok = ln_tile(xtile, htile)
                xcons = htok
                qtok = quant_tok(htile, 1024, xq_b, dqx)
                pl.wait("vector", tok_mrep)
                pl.fence("vector")
                ve(lambda e, c=sg[:, ti:ti + 1]: e.tensor_scalar(
                    c, dqx, mrep[:, 0:1], None, ALU.mult))
                ve(lambda e, c=su[:, ti:ti + 1]: e.tensor_scalar(
                    c, dqx, mrep[:, 1:2], None, ALU.mult))
                pl.wait("tensor", qtok)
                xqT_tok = transpose_to(
                    xqT4[:, :, ti, :],
                    [xq_b[:, 128 * dc:128 * (dc + 1)] for dc in range(ND)],
                    ti % 2, src3d=True)
            for ob in range(8):
                bo = ob % 2
                if wcons[bo] is not None:
                    pl.wait("sync", wcons[bo])
                if ycons[bo] is not None:
                    pl.wait("sync", ycons[bo])
                pl.waits("sync", wq_store["wg"] + wq_store["wu"])
                wtok1 = pl.dma("sync", wgb3[bo], bass.AP(
                    wqh["wg"], 512 * ob, [[FF, 128], [128 * FF, ND], [1, 512]]),
                    ch=f"wgb{bo}")
                wtok2 = pl.dma("sync", wub3[bo], bass.AP(
                    wqh["wu"], 512 * ob, [[FF, 128], [128 * FF, ND], [1, 512]]),
                    ch=f"wub{bo}")
                pl.wait("tensor", wtok1)
                pl.wait("tensor", wtok2)
                pl.wait("tensor", xqT_tok)
                for ti in range(8):
                    bg, bgk = pb[ti % 2], f"pb{ti % 2}"
                    bu, buk = pb[2 + ti % 2], f"pb{2 + ti % 2}"
                    bank_wait("tensor", bgk)
                    bank_wait("tensor", buk)
                    mku = None
                    for dc in range(ND):
                        lhs = xqT4[:, dc, ti, :]
                        mm(bg, lhs, wgb3[bo][:, dc], dc == 0, dc == ND - 1)
                        mku = mm(bu, lhs, wub3[bo][:, dc], dc == 0, dc == ND - 1)
                    tok_mm = pl.mark(mku)
                    pl.wait("scalar", tok_mm)
                    if gtok[0] is not None:
                        pl.wait("scalar", gtok[0])
                    stok = sc(lambda e, b=bg[:], d=sg[:, ti:ti + 1], g=gsil[ti % 2]:
                              e.activation(g, b, ACT.Silu, scale=d))
                    smk = pl.mark(stok)
                    bank_set(bgk, smk)
                    pl.wait("vector", smk)
                    ptok = ve(lambda e, b=bu[:], d=su[:, ti:ti + 1], g=gsil[ti % 2],
                              dst=p_half[ti][:, 512 * ob:512 * (ob + 1)]:
                              e.scalar_tensor_tensor(dst, b, d, g,
                                                     ALU.mult, ALU.mult))
                    pmk = pl.mark(ptok)
                    bank_set(buk, pmk)
                    gtok[0] = pmk
                    pl.fence("vector")
                    ve(lambda e, p=p_half[ti][:, 512 * ob:512 * (ob + 1)]:
                       e.reduce_max(qa, p, axis=AX, apply_absolute_value=True))
                    pl.fence("vector")
                    if ob == 0:
                        mxv = ve(lambda e, a=pamax[:, ti:ti + 1]: e.tensor_copy(a, qa))
                    else:
                        mxv = ve(lambda e, a=pamax[:, ti:ti + 1]:
                                 e.tensor_tensor(a, a, qa, ALU.max))
                    wcons[bo] = pl.mark(mxv) if ti == 7 else wcons[bo]
            xq2T_tok = [None] * 8
            for ti in range(8):
                pl.fence("vector")
                ve(lambda e, a=pamax[:, ti:ti + 1]: e.tensor_scalar(
                    qa, a, 1e-5, 1.0 / 127.0, ALU.max, ALU.mult))
                pl.fence("vector")
                ve(lambda e, c=sd[:, ti:ti + 1]: e.tensor_scalar(
                    c, qa, mrep[:, 2:3], 0.5, ALU.mult, ALU.mult))
                ve(lambda e: e.reciprocal(qs, qa))
                xq2 = bfv(p_half[ti])[:, 0:4096]
                qv = None
                for b2 in range(4):
                    pl.fence("vector")
                    ve(lambda e, s=p_half[ti][:, 1024 * b2:1024 * (b2 + 1)],
                       d=i32q.bitcast(I32)[:, 0:1024]:
                       e.tensor_scalar(d, s, qs, None, ALU.mult))
                    pl.fence("vector")
                    qv = ve(lambda e, s=i32q.bitcast(I32)[:, 0:1024],
                            d=xq2[:, 1024 * b2:1024 * (b2 + 1)]: e.tensor_copy(d, s))
                qtok = pl.mark(qv)
                pl.wait("tensor", qtok)
                xq2T = bfv(p_half[ti])[:, 4096:8192]
                for g4 in range(4):
                    xq2T_tok[ti] = transpose_to(
                        xq2T[:, g4 * 1024:g4 * 1024 + 1024],
                        [xq2[:, g4 * 1024 + 128 * k:g4 * 1024 + 128 * (k + 1)]
                         for k in range(8)], g4 % 2)
            pl.waits("sync", wq_store["wd"])
            for ti in range(8):
                pl.wait("sync", xq2T_tok[ti])
            for a in range(8):
                wdtok = pl.dma("sync", wdb_dst[a], wdb_src[a], ch="wd")
            pl.wait("tensor", wdtok)
            pl.wait("gpsimd", wcons[0])
            pl.wait("gpsimd", wcons[1])
            dlast = None
            for ti in range(8):
                tau = 8 * half + ti
                yb = ytile[ti % 2]
                if ycons[ti % 2] is not None:
                    pl.wait("gpsimd", ycons[ti % 2])
                xtok = pl.dma("gpsimd", yb,
                              src_dram.ap()[128 * tau:128 * (tau + 1), :],
                              ch=f"yx{ti % 2}")
                xq2T = bfv(p_half[ti])[:, 4096:8192]
                pl.wait("tensor", xq2T_tok[ti])
                evk = None
                for db in range(2):
                    bank_wait("tensor", "pb4")
                    mkd = None
                    for oc in range(NF):
                        mkd = mm(pb[4], xq2T[:, 128 * oc:128 * (oc + 1)],
                                 wdb_chunks[oc][:, 512 * db:512 * (db + 1)],
                                 oc == 0, oc == NF - 1)
                    dlast = mkd
                    tok_mm = pl.mark(mkd)
                    pl.wait("vector", tok_mm)
                    pl.wait("vector", xtok)
                    evk = ve(lambda e, b=pb[4][:], d=sd[:, ti:ti + 1],
                             y=yb[:, 512 * db:512 * (db + 1)]:
                             e.scalar_tensor_tensor(y, b, d, y, ALU.mult, ALU.add))
                    bank_set("pb4", pl.mark(evk))
                ytok = pl.mark(evk)
                pl.wait("gpsimd", ytok)
                stok = pl.dma("gpsimd",
                              dst_dram.ap()[128 * tau:128 * (tau + 1), :], yb,
                              ch=f"ys{ti % 2}")
                ycons[ti % 2] = stok
                out_toks.append(stok)
            wdcons = pl.mark(dlast)
            half_tok = wdcons
        al.pop()
        out_toks = out_toks[-2:] + [pl.mark((e_, pl.last_real[e_]))
                                    for e_ in ("tensor", "vector", "scalar")]
        return out_toks


    def attention_sublayer(src_dram, dst_dram, with_rel=True, pre_tok=None):
        al.push()
        att0 = al.top
        xtile_a = al.take(1024)
        htile_a = al.take(1024)
        hh_t = al.takeb(1024)
        hl_t = al.takeb(1024)
        hT0 = al.top
        hTh = al.takeb(ND * NT * 128)
        hTl = al.takeb(ND * NT * 128)
        hTh4 = hTh.rearrange("p (c t n) -> p c t n", t=NT, n=128)
        hTl4 = hTl.rearrange("p (c t n) -> p c t n", t=NT, n=128)
        nhl = [al.takeb(512) for _ in range(4)]
        rkx_h = al.takeb(512)
        rkx_l = al.takeb(512)
        rkt0 = al.top
        rkt32 = al.take(768)
        rvp_h = al.takeb(2 * 64)
        rvp_l = al.takeb(2 * 64)
        rvd_f = al.take(64)[0:1, :]
        rkm = al.take(1)[0:1, :]
        r256rep = al.take(64)
        rvcat_h = al.takeb(2)[:, 0:1]
        rvcat_l = al.takeb(2)[:, 1:2]
        rvc32 = al.take(1)
        wpb = [al.take(1024) for _ in range(2)]
        WT = {w: (al.takeb(ND * 128), al.takeb(ND * 128)) for w in ("q", "k", "v")}
        bias2 = {w: al.takeb(256)[0:2, 0:128] for w in ("q", "k", "v")}
        brow = al.take(128)[0:1, :]
        bc2 = al.takeb(2)
        bcol_s = al.take(1)
        qk0 = al.top
        Qh, Ql = al.takeb(T), al.takeb(T)
        Kh, Kl = al.takeb(T), al.takeb(T)
        Vx_h = al.takeb(NT * 160)
        Vx_l = al.takeb(NT * 160)
        Vx_h3 = Vx_h.rearrange("p (t n) -> p t n", n=160)
        Vx_l3 = Vx_l.rearrange("p (t n) -> p t n", n=160)
        Kcat = bfv(AAP[:, att0:att0 + 2048])
        Kcat3 = Kcat.rearrange("p (c n) -> p c n", n=T)
        qdup2 = [al.takeb(512) for _ in range(2)]
        sqb = al.takeb(T)
        mrow4 = al.take(4)[0:1, :]
        macc = al.take(2)
        negB2 = al.take(2)
        tb_s = al.take(2)[0:1, :]
        ring0 = al.top
        ring_h = [al.takeb(512) for _ in range(6)]
        ring_l = [al.takeb(512) for _ in range(6)]
        pf32 = [al.take(512) for _ in range(2)]
        e_hi = [al.takeb(512) for _ in range(2)]
        e_lo = [al.takeb(512) for _ in range(2)]
        rs_h = [al.takeb(384) for _ in range(4)]
        rs_l = [al.takeb(384) for _ in range(4)]
        stp_h = [al.takeb(384) for _ in range(4)]
        stp_l = [al.takeb(384) for _ in range(4)]
        psk_sl = [(al.takeb(256), al.takeb(256)) for _ in range(2)]
        psk_sl = psk_sl + [
            (bfv(AAP[:, rkt0:rkt0 + 128]), bfv(AAP[:, rkt0 + 128:rkt0 + 256])),
            (bfv(AAP[:, rkt0 + 256:rkt0 + 384]), bfv(AAP[:, rkt0 + 384:rkt0 + 512]))]
        psk_rdy = [None] * 4
        pskT_h = al.takeb(256)
        pskT_l = al.takeb(256)
        tmp4 = al.takeb(16)
        bm4 = [al.takeb(256)[0:4, 0:128] for _ in range(4)]
        cs_sb = al.take(512)[0:16, :]
        tri4p = [al.take(4), al.take(4)]
        triT_s = al.take(128)[0:4, :]
        c0r4 = al.take(128)[0:4, :]
        c0flat = al.take(512)[0:1, :]
        trflat = al.take(512)[0:1, :]
        lflat = al.take(512)[0:1, :]
        lrec = al.take(4)
        pfx_sb = al.take(512)[0:5, :]
        nsc = al.take(512)
        WoT8 = [(bfv(AAP[:, hT0 + 1024 * oc:hT0 + 1024 * oc + 512]),
                 bfv(AAP[:, hT0 + 1024 * oc + 512:hT0 + 1024 * oc + 1024]))
                for oc in range(8)]
        boext8 = al.take(8)
        aosc = [AAP[:, qk0 + 512 * i:qk0 + 512 * (i + 1)] for i in range(8)]
        ytile_a = [AAP[:, qk0 + 4096 + 1024 * i:qk0 + 4096 + 1024 * (i + 1)]
                   for i in range(2)]
        attb_h = bfv(AAP[:, ring0:ring0 + 2048])
        attb_l = bfv(AAP[:, ring0 + 2048:ring0 + 4096])
        attb_h3 = attb_h.rearrange("p (c n) -> p c n", n=512)
        attb_l3 = attb_l.rearrange("p (c n) -> p c n", n=512)

        pre_list = pre_tok if isinstance(pre_tok, list) else (
            [pre_tok] if pre_tok is not None else [])
        for tk_ in pre_list:
            for e_ in ENGS:
                pl.wait(e_, tk_)

        xcons = None
        hT_tok = None
        for tau in range(NT):
            if xcons is not None:
                pl.wait("sync", xcons)
            xtok = pl.dma("sync", xtile_a,
                          src_dram.ap()[128 * tau:128 * (tau + 1), :], ch="xa")
            pl.wait("vector", xtok)
            pl.wait("scalar", xtok)
            pl.fence("vector")
            htok = ln_tile(xtile_a, htile_a)
            xcons = htok
            pl.fence("vector")
            ve(lambda e: e.tensor_copy(hh_t, htile_a))
            pl.fence("vector")
            ve(lambda e: e.tensor_tensor(hl_t, htile_a, hh_t, ALU.subtract))
            sp_tok = pl.fence("vector")
            pl.wait("tensor", sp_tok)
            transpose_to(hTh4[:, :, tau, :],
                         [hh_t[:, 128 * dc:128 * (dc + 1)] for dc in range(ND)],
                         0, src3d=True)
            hT_tok = transpose_to(hTl4[:, :, tau, :],
                                  [hl_t[:, 128 * dc:128 * (dc + 1)] for dc in range(ND)],
                                  1, src3d=True)

        if with_rel:
            rk_sb = wpb[0]
            pl.dma("sync", rk_sb[:, 0:64], relk.ap()[0:128, :], ch="rk")
            t2 = pl.dma("sync", rk_sb[:, 64:128], relk.ap()[128:256, :], ch="rk")
            t3 = pl.dma("sync", rkt32[0:64, 256:257],
                        bass.AP(relk, 256 * HD, [[1, 64], [1, 1]]), ch="rk")
            pl.wait("vector", t3)
            pl.fence("vector")
            ve(lambda e: e.tensor_scalar(rk_sb[:, 128:256], rk_sb[:, 0:128],
                                         0.125, None, ALU.mult))
            ve(lambda e: e.tensor_scalar(rkt32[0:64, 256:257],
                                         rkt32[0:64, 256:257], 0.125, None, ALU.mult))
            sctok = pl.fence("vector")
            pl.wait("tensor", sctok)
            bank_wait("tensor", "pb0")
            pl.emit("tensor", lambda e: e.transpose(
                pb[0][0:64, 0:128], rk_sb[:, 128:192], id_f32))
            tt1 = pl.emit("tensor", lambda e: e.transpose(
                pb[0][0:64, 128:256], rk_sb[:, 192:256], id_f32))
            tt1_tok = pl.mark(tt1)
            pl.wait("vector", tt1_tok)
            ev = ve(lambda e: e.tensor_copy(rkt32[0:64, 0:256], pb[0][0:64, 0:256]))
            bank_set("pb0", pl.mark(ev))
            pl.fence("vector")
            ve(lambda e: e.tensor_tensor(rkt32[0:64, 260:517], rkt32[0:64, 0:257],
                                         rkt32[0:64, 0:257], ALU.mult))
            sqtok = pl.fence("vector")
            pl.wait("tensor", sqtok)
            bank_wait("tensor", "pb0")
            nmm = mm(pb[0][0:1, 0:257], onescol[0:64, :], rkt32[0:64, 260:517],
                     True, True)
            pl.wait("vector", pl.mark(nmm))
            rmv = ve(lambda e: e.reduce_max(rkm, pb[0][0:1, 0:257], axis=AX))
            rmvt = pl.mark(rmv)
            bank_set("pb0", rmvt)
            pl.fence("vector")
            pl.wait("scalar", rmvt)
            sq2 = sc(lambda e: e.activation(rkm, rkm, ACT.Sqrt))
            pl.wait("vector", pl.mark(sq2))
            pl.fence("vector")
            ve(lambda e: e.tensor_copy(rkx_h[0:64, 127:384], rkt32[0:64, 0:257]))
            ve(lambda e, s=rkt32[0:64, 0:1].to_broadcast((64, 127)):
               e.tensor_copy(rkx_h[0:64, 0:127], s))
            ve(lambda e, s=rkt32[0:64, 256:257].to_broadcast((64, 128)):
               e.tensor_copy(rkx_h[0:64, 384:512], s))
            pl.fence("vector")
            ve(lambda e: e.tensor_tensor(rkx_l[0:64, 127:384], rkt32[0:64, 0:257],
                                         rkx_h[0:64, 127:384], ALU.subtract))
            ve(lambda e, s=rkt32[0:64, 0:1].to_broadcast((64, 127)),
               h=rkx_h[0:64, 0:127]:
               e.tensor_tensor(rkx_l[0:64, 0:127], s, h, ALU.subtract))
            ve(lambda e, s=rkt32[0:64, 256:257].to_broadcast((64, 128)),
               h=rkx_h[0:64, 384:512]:
               e.tensor_tensor(rkx_l[0:64, 384:512], s, h, ALU.subtract))
            rkx_tok0 = pl.fence("vector")
            pl.wait("gpsimd", rkx_tok0)
            pl.dma("gpsimd", rkx_h[64:128, :], rkx_h[0:64, :], ch="rkrep")
            rep_tok = pl.dma("gpsimd", rkx_l[64:128, :], rkx_l[0:64, :], ch="rkrep")
            rkx_tok = rep_tok

            rv_sb = wpb[1]
            pl.dma("sync", rv_sb[:, 0:64], relv.ap()[0:128, :], ch="rv")
            v2 = pl.dma("sync", rv_sb[:, 64:128], relv.ap()[128:256, :], ch="rv")
            v3 = pl.dma("sync", rv_sb[0:1, 128:192], relv.ap()[256:257, :], ch="rv")
            pl.wait("tensor", v3)
            bank_wait("tensor", "pb1")
            rmm = mm(pb[1][:, 0:64], onesrow, rv_sb[0:1, 128:192], True, True)
            pl.wait("vector", pl.mark(rmm))
            rrv = ve(lambda e: e.tensor_copy(r256rep, pb[1][:, 0:64]))
            bank_set("pb1", pl.mark(rrv))
            pl.wait("vector", v2)
            pl.fence("vector")
            ve(lambda e: e.tensor_tensor(rvd_f, rv_sb[0:1, 0:64],
                                         r256rep[0:1, :], ALU.subtract))
            ve(lambda e: e.tensor_tensor(rv_sb[:, 192:256], rv_sb[:, 0:64],
                                         r256rep, ALU.subtract))
            ve(lambda e: e.tensor_tensor(rv_sb[:, 256:320], rv_sb[:, 64:128],
                                         r256rep, ALU.subtract))
            pl.fence("vector")
            ve(lambda e: e.tensor_scalar(rv_sb[0:1, 192:256], rv_sb[0:1, 192:256],
                                         0.0, None, ALU.mult))
            pl.fence("vector")
            ve(lambda e: e.tensor_copy(rvp_h[:, 0:64], rv_sb[:, 192:256]))
            ve(lambda e: e.tensor_copy(rvp_h[:, 64:128], rv_sb[:, 256:320]))
            pl.fence("vector")
            ve(lambda e: e.tensor_tensor(rvp_l[:, 0:64], rv_sb[:, 192:256],
                                         rvp_h[:, 0:64], ALU.subtract))
            ve(lambda e: e.tensor_tensor(rvp_l[:, 64:128], rv_sb[:, 256:320],
                                         rvp_h[:, 64:128], ALU.subtract))
            bank_wait("tensor", "pb1")
            mm(pb[1][0:64, 0:1], rv_sb[0:1, 128:192], onesrow[0:1, 0:1], True, True)
            cmm2 = mm(pb[1][64:128, 0:1], rv_sb[0:1, 128:192], onesrow[0:1, 0:1],
                      True, True)
            pl.wait("vector", pl.mark(cmm2))
            rcv = ve(lambda e: e.tensor_copy(rvc32, pb[1][:, 0:1]))
            bank_set("pb1", pl.mark(rcv))
            pl.fence("vector")
            ve(lambda e: e.tensor_copy(rvcat_h, rvc32))
            pl.fence("vector")
            ve(lambda e: e.tensor_tensor(rvcat_l, rvc32, rvcat_h, ALU.subtract))
            rv_tok = pl.fence("vector")
        else:
            ve(lambda e: e.tensor_scalar(rkm, onescol[0:1, :], 0.0, None, ALU.mult))
            ve(lambda e: e.tensor_scalar(rvc32, onescol, 0.0, None, ALU.mult))
            pl.fence("vector")
            ve(lambda e: e.tensor_copy(rvcat_h, rvc32))
            ve(lambda e: e.tensor_copy(rvcat_l, rvc32))
            rkx_tok = pl.fence("vector")
            rv_tok = rkx_tok

        ring_cons = [None] * 6
        plo_tok = [None] * 6
        tri_done = [None] * 4
        tilestate = [None]
        pf_cons = [None, None]
        qd_cons = [None, None]
        qd_tok = [None, None]
        rot_gate = [None]
        rotK_tok = [None]
        att_wcons = [None, None, None, None]
        att_wtoks = []
        eW_tok = [None, None]
        eR_tok = [None, None]
        psW_tok = [None, None]
        psR_tok = [None, None]
        rs_ready = [None] * 4
        rs_cons = [None] * 4
        bm_tok = [None] * 4
        stp_cons = [None] * 4
        psk_cons = [None]
        c0_tok = [None]

        bm_pend = [None] * 4

        def emit_strips(h, po, hh, blk, qts):
            def p2(j):
                pl.wait("tensor", bm_pend[j])
                if j == 0:
                    bank_wait("tensor", "pt0")
                bmt = pl.emit("tensor", lambda e, c=128 * j, s=tmp4[:, 4 * j:4 * j + 4]:
                              e.transpose(pt[0][0:4, c:c + 128], s, id_b))
                pl.wait("vector", pl.mark(bmt))
                bme = ve(lambda e, d=bm4[j], c=128 * j:
                         e.tensor_copy(d, pt[0][0:4, c:c + 128]))
                bm_tok[j] = pl.mark(bme)
                if j == 3:
                    bank_set("pt0", bm_tok[j])

            def p1(j, qt):
                es = qt % 2
                rbk = "pb2" if j % 2 == 0 else "pb0"
                rb = pb[2] if j % 2 == 0 else pb[0]
                bank_wait("tensor", rbk)
                pl.wait("tensor", rkx_tok)
                mm(rb, Qh[po:po + 64, 128 * qt:128 * qt + 128],
                   rkx_h[po:po + 64, :], True, False)
                mm(rb, Qh[po:po + 64, 128 * qt:128 * qt + 128],
                   rkx_l[po:po + 64, :], False, False)
                rmk = mm(rb, Ql[po:po + 64, 128 * qt:128 * qt + 128],
                         rkx_h[po:po + 64, :], False, True)
                rtok = pl.mark(rmk)
                pl.wait("vector", rtok)
                pl.fence("vector")
                ve(lambda e, s=rb[:, 127:384:256], d=tmp4[:, 4 * j:4 * j + 4:2]:
                   e.tensor_copy(d, s))
                pl.fence("vector")
                ve(lambda e, s=rb[:, 127:384:256],
                   hsb=tmp4[:, 4 * j:4 * j + 4:2], d=tmp4[:, 4 * j + 1:4 * j + 4:2]:
                   e.tensor_tensor(d, s, hsb, ALU.subtract))
                bm_pend[j] = pl.fence("vector")
                pl.wait("vector", eW_tok[es])
                ve(lambda e, d=e_hi[es][:, 0:511]: e.tensor_copy(
                    d, rb[:, 0:511]))
                pl.fence("vector")
                elo_v = ve(lambda e, d=e_lo[es][:, 0:511], hsb=e_hi[es][:, 0:511]:
                           e.tensor_tensor(d, rb[:, 0:511], hsb, ALU.subtract))
                ebuild = pl.mark(elo_v)
                bank_set(rbk, ebuild)
                pl.wait("gpsimd", ebuild)
                pl.wait("gpsimd", eR_tok[es])
                pl.dma("gpsimd", bass.AP(e_h, (es * 2 + 0) * 128 * 511,
                                         [[511, 128], [1, 511]]),
                       e_hi[es][:, 0:511], ch=f"ehW{es}")
                wtk = pl.dma("gpsimd", bass.AP(e_h, (es * 2 + 1) * 128 * 511,
                                               [[511, 128], [1, 511]]),
                             e_lo[es][:, 0:511], ch=f"ehW{es}")
                eW_tok[es] = wtk
                pl.wait("sync", wtk)
                pl.wait("sync", rs_cons[j])
                pl.dma("sync", rs_h[j], bass.AP(
                    e_h, (es * 2 + 0) * 128 * 511 + 127, [[510, 128], [1, 384]]),
                    ch=f"ehR{es}")
                rtk = pl.dma("sync", rs_l[j], bass.AP(
                    e_h, (es * 2 + 1) * 128 * 511 + 127, [[510, 128], [1, 384]]),
                    ch=f"ehR{es}")
                eR_tok[es] = rtk
                rs_ready[j] = rtk

            for j, qt in enumerate(qts):
                p1(j, qt)
                p2(j)

        def emit_tiles(h, po, hh, blk, par, qts, with_rel, pend, carry_out):
            cols = slice(512 * blk, 512 * (blk + 1))
            bank = pb[4 + par]
            bkey = f"pb{4 + par}"
            csb = 64 if po == 0 else 32
            bst = {"first": True, "ztok": None}

            def bank_mm(rows, lhsT, rhs, stop=False):
                st = bst["first"]
                if st:
                    pl.wait("tensor", bst["ztok"])
                bst["first"] = False
                return mm(bank[rows[0]:rows[1], :] if rows[2] is None else
                          bank[rows[0]:rows[1], rows[2]:rows[3]],
                          lhsT, rhs, st, stop, skip=True)

            def psk_part2(qt):
                j = qt - 4 * blk
                psk_h, psk_l = psk_sl[j]
                pl.wait("tensor", psk_rdy[j])
                bank_wait("tensor", "pt0")
                pl.emit("tensor", lambda e, a=psk_h: e.transpose(
                    pt[0][:, 0:128], a[:, 0:128], id_b))
                pl.emit("tensor", lambda e, a=psk_h: e.transpose(
                    pt[0][:, 128:256], a[:, 128:256], id_b))
                pl.emit("tensor", lambda e, a=psk_l: e.transpose(
                    pt[0][:, 256:384], a[:, 0:128], id_b))
                ptk = pl.emit("tensor", lambda e, a=psk_l: e.transpose(
                    pt[0][:, 384:512], a[:, 128:256], id_b))
                pl.wait("vector", pl.mark(ptk))
                pl.fence("vector")
                ve(lambda e: e.tensor_copy(pskT_h, pt[0][:, 0:256]))
                pve = ve(lambda e: e.tensor_copy(pskT_l, pt[0][:, 256:512]))
                pskt = pl.mark(pve)
                bank_set("pt0", pskt)
                psk_cons[0] = pskt
                pl.wait("tensor", pskt)
                pl.wait("tensor", rv_tok)
                for ck in range(2):
                    bank_mm((po, po + 64, 128 * j, 128 * j + 128),
                            rvp_h[:, 64 * ck:64 * ck + 64],
                            pskT_h[:, 128 * ck:128 * ck + 128])
                    bank_mm((po, po + 64, 128 * j, 128 * j + 128),
                            rvp_l[:, 64 * ck:64 * ck + 64],
                            pskT_h[:, 128 * ck:128 * ck + 128])
                    bank_mm((po, po + 64, 128 * j, 128 * j + 128),
                            rvp_h[:, 64 * ck:64 * ck + 64],
                            pskT_l[:, 128 * ck:128 * ck + 128])

            bst["psk2"] = psk_part2
            tilestate[0] = bst
            base = 4 * blk
            far = [(base + 5 + k) % NT for k in range(10)]
            diag = [(base + 15 + k) % NT for k in range(6)]
            order = far[0:3] + diag + far[3:10]
            done_tiles = set()
            strips_emitted = set()
            strip_ready = {}
            psk_pend = {}
            slot_of = {}
            sbanks = (0, 1, 3)
            vlo, vhi = (0, 80) if hh == 0 else (32, 160)
            ro, rh = (0, 80) if hh == 0 else (0, 128)
            for oi in range(NT):
                tt = order[oi]
                if pend:
                    for sl_, fn_ in pend:
                        if sl_ == oi:
                            fn_()
                sb_ = pb[sbanks[oi % 3]]
                sbk = f"pb{sbanks[oi % 3]}"
                bank_wait("tensor", sbk)
                if oi == 0:
                    bank_wait("tensor", bkey)
                    bank_wait("vector", bkey)
                    pl.fence("vector")
                    zb = ve(lambda e: e.tensor_scalar(
                        bank[:], Qh[:, 0:512], 0.0, None, ALU.mult))
                    bst["ztok"] = pl.mark(zb)
                if oi == 0:
                    pl.wait("tensor", qd_tok[blk % 2])
                    pl.wait("tensor", rotK_tok[0])
                mm(sb_, Kcat3[:, hh, 128 * tt:128 * tt + 128],
                   qdup2[blk % 2], True, False)
                if with_rel:
                    for j, qt in enumerate(qts):
                        dqt = tt - qt
                        if -1 <= dqt <= 1:
                            pl.wait("tensor", rs_ready[j])
                            co = 128 * (dqt + 1)
                            mm(sb_[:, 128 * j:128 * j + 128],
                               rs_h[j][:, co:co + 128], id_b, False, False)
                            rmm2 = mm(sb_[:, 128 * j:128 * j + 128],
                                      rs_l[j][:, co:co + 128], id_b, False, False)
                            rs_cons[j] = pl.mark(rmm2)
                        elif dqt < -1:
                            pl.wait("tensor", bm_tok[j])
                            mm(sb_[:, 128 * j:128 * j + 128],
                               sel4[0:4, 0:128], bm4[j], False, False)
                        else:
                            pl.wait("tensor", bm_tok[j])
                            mm(sb_[:, 128 * j:128 * j + 128],
                               sel4[0:4, 128:256], bm4[j], False, False)
                lmk = mm(sb_, Kl[po:po + 64, 128 * tt:128 * tt + 128],
                         Qh[po:po + 64, cols], False, True)
                smk = pl.mark(lmk)
                s6 = oi % 6
                slot_of[tt] = s6
                pl.wait("scalar", smk)
                pl.wait("scalar", pf_cons[oi % 2])
                petok = pl.mark(sc(lambda e, d=pf32[oi % 2], b=sb_[:],
                                   c=negB2[:, hh:hh + 1]:
                                   e.activation(d, b, ACT.Exp, bias=c)))
                bank_set(sbk, petok)
                pl.wait("scalar", ring_cons[s6])
                pl.fence("scalar")
                phi_t = pl.mark(sc(lambda e, d=ring_h[s6], p=pf32[oi % 2]:
                                   e.activation(d, p, ACT.Copy)))
                pl.wait("vector", phi_t)
                pl.fence("vector")
                pltok = pl.mark(ve(lambda e, d=ring_l[s6], p=pf32[oi % 2],
                                   hsb=ring_h[s6]:
                                   e.tensor_tensor(d, p, hsb, ALU.subtract)))
                plo_tok[s6] = pltok
                pf_cons[oi % 2] = pltok
                lagged = []
                if oi >= 5:
                    lagged.append(order[oi - 5])
                if oi == NT - 1 and not carry_out:
                    lagged += [order[NT - 5 + k3] for k3 in range(5)]
                for lt in lagged:
                    l6 = slot_of[lt]
                    pl.wait("tensor", plo_tok[l6])
                    bank_mm((ro, rh, None, None),
                            Vx_h3[:, lt, vlo:vhi], ring_h[l6])
                    bank_mm((ro, rh, None, None),
                            Vx_h3[:, lt, vlo:vhi], ring_l[l6])
                    pvk = bank_mm((ro, rh, None, None),
                                  Vx_l3[:, lt, vlo:vhi], ring_h[l6],
                                  stop=(oi == NT - 1 and lt == lagged[-1]
                                        and not with_rel and not carry_out))
                    ring_cons[l6] = pl.mark(pvk)
                done_tiles.add(tt)
                if with_rel:
                    for qt in qts:
                        if qt in strip_ready or qt in strips_emitted:
                            continue
                        need = {x for x in (qt - 1, qt, qt + 1) if 0 <= x < NT}
                        if need <= done_tiles:
                            strip_ready[qt] = oi
                    strip_qts = [qt for qt, ro in sorted(strip_ready.items())
                                 if ro + 3 <= oi or oi == NT - 1]
                    for qt in strip_qts:
                        strips_emitted.add(qt)
                        del strip_ready[qt]
                        j = qt - 4 * blk
                        jj = j
                        pl.wait("vector", stp_cons[j])
                        if 0 < qt < NT - 1:
                            bank_wait("tensor", "pt1")
                            tl = None
                            for kk, st_t in enumerate((qt - 1, qt, qt + 1)):
                                s6b = slot_of[st_t]
                                pl.wait("tensor", plo_tok[s6b])
                                pl.emit("tensor", lambda e, c=256 * kk,
                                        s=ring_h[s6b][:, 128 * jj:128 * jj + 128]:
                                        e.transpose(pt[1][:, c:c + 128], s, id_b))
                                tl = pl.emit("tensor", lambda e, c=256 * kk + 128,
                                             s=ring_l[s6b][:, 128 * jj:128 * jj + 128]:
                                             e.transpose(pt[1][:, c:c + 128], s, id_b))
                            pl.wait("vector", pl.mark(tl))
                            sev = None
                            for kk in range(3):
                                ve(lambda e, d=stp_h[j][:, 128 * kk:128 * kk + 128],
                                   c=256 * kk: e.tensor_copy(d, pt[1][:, c:c + 128]))
                                sev = ve(lambda e, d=stp_l[j][:, 128 * kk:128 * kk + 128],
                                         c=256 * kk + 128:
                                         e.tensor_copy(d, pt[1][:, c:c + 128]))
                            bank_set("pt1", pl.mark(sev))
                        else:
                            for kk, st_t in enumerate((qt - 1, qt, qt + 1)):
                                if 0 <= st_t < NT:
                                    s6b = slot_of[st_t]
                                    bank_wait("tensor", "pt1")
                                    pl.wait("tensor", plo_tok[s6b])
                                    pl.emit("tensor", lambda e, s=ring_h[s6b][:, 128 * jj:128 * jj + 128]:
                                            e.transpose(pt[1][:, 0:128], s, id_b))
                                    tl = pl.emit("tensor", lambda e, s=ring_l[s6b][:, 128 * jj:128 * jj + 128]:
                                                 e.transpose(pt[1][:, 128:256], s, id_b))
                                    pl.wait("vector", pl.mark(tl))
                                    ve(lambda e, d=stp_h[j][:, 128 * kk:128 * kk + 128]:
                                       e.tensor_copy(d, pt[1][:, 0:128]))
                                    sev = ve(lambda e, d=stp_l[j][:, 128 * kk:128 * kk + 128]:
                                             e.tensor_copy(d, pt[1][:, 128:256]))
                                    bank_set("pt1", pl.mark(sev))
                                else:
                                    pl.fence("vector")
                                    ve(lambda e, d=stp_h[j][:, 128 * kk:128 * kk + 128]:
                                       e.tensor_scalar(d, id_b, 0.0, None, ALU.mult))
                                    sev = ve(lambda e, d=stp_l[j][:, 128 * kk:128 * kk + 128]:
                                             e.tensor_scalar(d, id_b, 0.0, None, ALU.mult))
                                    pl.mark(sev)
                        stp_cons[j] = pl.fence("vector")
                        t4 = tri4p[par]
                        tcs = 128 + 8 * par
                        if qt == 0:
                            ve(lambda e, d=t4[:, 0:1]: e.tensor_scalar(
                                d, onescol, 0.0, None, ALU.mult))
                        else:
                            ve(lambda e, d=t4[:, j:j + 1], s=stp_h[j][:, 0:128]:
                               e.scalar_tensor_tensor(nsc[:, 0:64].bitcast(BF16),
                                                      s, 1.0, lt_b, ALU.mult,
                                                      ALU.mult, accum_out=d))
                            ve(lambda e, d=nsc[:, tcs + j:tcs + j + 1],
                               s=stp_l[j][:, 0:128]:
                               e.scalar_tensor_tensor(nsc[:, 64:128].bitcast(BF16),
                                                      s, 1.0, lt_b, ALU.mult,
                                                      ALU.mult, accum_out=d))
                        pl.fence("vector")
                        if qt != 0:
                            ve(lambda e, d=t4[:, j:j + 1], a=nsc[:, tcs + j:tcs + j + 1]:
                               e.tensor_tensor(d, d, a, ALU.add))
                        tri_done[j] = pl.fence("vector")
                        ps = j % 2
                        pl.wait("gpsimd", stp_cons[j])
                        pl.wait("gpsimd", psR_tok[ps])
                        pl.dma("gpsimd",
                               bass.AP(s_h, (ps * 2 + 0) * 128 * 384,
                                       [[384, 128], [1, 384]]),
                               stp_h[j], ch=f"psW{ps}")
                        pwt = pl.dma("gpsimd",
                                     bass.AP(s_h, (ps * 2 + 1) * 128 * 384,
                                             [[384, 128], [1, 384]]),
                                     stp_l[j], ch=f"psW{ps}")
                        psW_tok[ps] = pwt
                        stp_cons[j] = pwt
                        pl.wait("sync", pwt)
                        pl.wait("sync", psk_cons[0])
                        pl.dma("sync", psk_sl[j][0], bass.AP(
                            s_h, (ps * 2 + 0) * 128 * 384,
                            [[385, 128], [1, 256]]), ch=f"psR{ps}")
                        prt = pl.dma("sync", psk_sl[j][1], bass.AP(
                            s_h, (ps * 2 + 1) * 128 * 384,
                            [[385, 128], [1, 256]]), ch=f"psR{ps}")
                        psR_tok[ps] = prt
                        psk_rdy[j] = prt
                        psk_pend[qt] = oi
                if with_rel:
                    for qt in list(psk_pend):
                        if psk_pend[qt] + 6 <= oi:
                            bst["psk2"](qt)
                            del psk_pend[qt]
            if pend:
                for sl_, fn_ in pend:
                    if sl_ >= NT:
                        fn_()
            carries = []
            if carry_out:
                for k2 in range(5):
                    lt_c = order[NT - 5 + k2]
                    l6_c = slot_of[lt_c]
                    tok_c = plo_tok[l6_c]

                    def cfn(lt=lt_c, l6=l6_c, tok=tok_c, last=(k2 == 0)):
                        pl.wait("tensor", tok)
                        bank_mm((ro, rh, None, None),
                                Vx_h3[:, lt, vlo:vhi], ring_h[l6])
                        bank_mm((ro, rh, None, None),
                                Vx_h3[:, lt, vlo:vhi], ring_l[l6])
                        pvk2 = bank_mm((ro, rh, None, None),
                                       Vx_l3[:, lt, vlo:vhi], ring_h[l6],
                                       stop=(last and not with_rel))
                        ring_cons[l6] = pl.mark(pvk2)
                    carries.append(((NT - 5 + k2) % 6, cfn))
                if with_rel:
                    for ip, qt in enumerate(sorted(psk_pend)):
                        def pfn(qt=qt):
                            bst["psk2"](qt)
                        carries.append(((1 + ip) % 4, pfn))
                    psk_pend.clear()
            elif with_rel:
                for qt in sorted(psk_pend):
                    bst["psk2"](qt)
                psk_pend.clear()
            return carries

        def make_post(h, po, hh, blk, par, qts, with_rel, rsnap, bst):
            bank = pb[4 + par]
            bkey = f"pb{4 + par}"
            csb = 64 if po == 0 else 32

            S = {}

            def st0():
                pl.wait("vector", rsnap[(NT - 1) % 6])
                pl.fence("vector")
                csv = ve(lambda e: e.tensor_copy(cs_sb, bank[csb:csb + 16, :]))
                cst = pl.mark(csv)
                pl.wait("tensor", cst)
                bank_wait("tensor", "pb2")
                pfm = mm(pb[2][0:5, :], pre32_b[:, 5 * blk:5 * blk + 5], cs_sb,
                         True, True)
                pl.wait("vector", pl.mark(pfm))
                pfe = ve(lambda e: e.tensor_copy(pfx_sb, pb[2][0:5, :]))
                S["pfe"] = pl.mark(pfe)
                bank_set("pb2", S["pfe"])
                pl.wait("gpsimd", S["pfe"])
                ldm = pl.dma("gpsimd", lrec, pfx_sb[4:5, :], ch="c0")
                pl.wait("vector", ldm)
                pl.fence("vector")
                ve(lambda e: e.reciprocal(lrec, lrec))
                rv_t = pl.fence("vector")
                pl.wait("gpsimd", rv_t)
                S["lr"] = pl.dma("gpsimd", lflat, lrec, ch="c0")

            def st1():
                pfe_tok = S["pfe"]
                if with_rel:
                    for j in range(4):
                        pl.wait("tensor", tri_done[j])
                    bank_wait("tensor", "pb2")
                    trt = pl.emit("tensor", lambda e: e.transpose(
                        pb[2][0:4, 0:128], tri4p[par], id_f32))
                    pl.wait("vector", pl.mark(trt))
                    pl.fence("vector")
                    tre = ve(lambda e: e.tensor_copy(triT_s, pb[2][0:4, 0:128]))
                    tre_t = pl.mark(tre)
                    bank_set("pb2", tre_t)
                    pl.wait("gpsimd", tre_t)
                    pl.wait("gpsimd", pfe_tok)
                    pl.wait("gpsimd", c0_tok[0])
                    pl.dma("gpsimd", c0flat,
                           bass.AP(arena, pfx_sb.offset, [[ACOLS + 128, 4], [1, 128]]),
                           ch="c0")
                    c0d = pl.dma("gpsimd", trflat,
                                 bass.AP(arena, triT_s.offset, [[ACOLS, 4], [1, 128]]),
                                 ch="c0")
                    pl.wait("vector", c0d)
                    pl.fence("vector")
                    ve(lambda e: e.tensor_tensor(c0flat, c0flat, trflat, ALU.add))
                    c0t = pl.fence("vector")
                    c0_tok[0] = c0t

            def st2():
                if with_rel:
                    pl.wait("tensor", c0_tok[0])
                    for j in range(4):
                        mm(bank[po:po + 64, 128 * j:128 * j + 128],
                           rvd_f, c0flat[0:1, 128 * j:128 * j + 128],
                           False, j == 3, skip=True)
                pl.wait("tensor", S["lr"])
                bank_wait("tensor", "pb2")
                lrm = mm(pb[2], onesrow, lflat, True, True)
                S["lrt"] = pl.mark(lrm)

            def st3():
                pl.wait("vector", S["lrt"])
                pl.wait("vector", rsnap[(NT - 1) % 6])
                pl.fence("vector")
                ve(lambda e, d=nsc[po:po + 64, :]: e.tensor_copy(
                    d, bank[po:po + 64, :]))
                pl.fence("vector")
                nv = ve(lambda e, d=nsc[po:po + 64, :]: e.tensor_tensor(
                    d, d, pb[2][po:po + 64, :], ALU.mult))
                ntok = pl.mark(nv)
                bank_set("pb2", ntok)
                bank_set(bkey, ntok)
                ns2 = (blk % 2) * 2
                pl.wait("vector", att_wcons[ns2])
                pl.fence("vector")
                ve(lambda e, d=nhl[ns2][po:po + 64, :], s=nsc[po:po + 64, :]:
                   e.tensor_copy(d, s))
                pl.fence("vector")
                ve(lambda e, d=nhl[ns2 + 1][po:po + 64, :],
                   s=nsc[po:po + 64, :], hb=nhl[ns2][po:po + 64, :]:
                   e.tensor_tensor(d, s, hb, ALU.subtract))
                awt = pl.fence("vector")
                pl.wait("gpsimd", awt)
                row = (h // 2) * 128 + 64 * (h % 2)
                pl.dma("gpsimd", bass.AP(att_hbm, row * T + 512 * blk,
                                         [[T, 64], [1, 512]]),
                       nhl[ns2][po:po + 64, :], ch=f"atw{ns2}")
                att_wcons[ns2] = pl.dma(
                    "gpsimd", bass.AP(att_hbm, (ND * 128 + row) * T + 512 * blk,
                                      [[T, 64], [1, 512]]),
                    nhl[ns2 + 1][po:po + 64, :], ch=f"atw{ns2}")
                att_wtoks.append(att_wcons[ns2])
            return [st0, st1, st2, st3]

        par_ctr = [0]

        def attn_head(h, po, hh, with_rel, pending):
            for blk in range(4):
                qts = [4 * blk + j for j in range(4)]
                par = par_ctr[0] % 2
                par_ctr[0] += 1
                b2 = blk % 2
                qsrc = Qh[64 * hh:64 * hh + 64, 512 * blk:512 * (blk + 1)]
                pl.wait("gpsimd", qd_cons[b2])
                pl.wait("gpsimd", rot_gate[0])
                pl.dma("gpsimd", qdup2[b2][0:64, :], qsrc, ch="rot0")
                qd_tok[b2] = pl.dma("gpsimd", qdup2[b2][64:128, :], qsrc,
                                    ch="rot0")
                if with_rel:
                    emit_strips(h, po, hh, blk, qts)
                cross = (hh == 1 and blk == 3)
                carries = emit_tiles(h, po, hh, blk, par, qts, with_rel,
                                     pending, not cross)
                qd_cons[b2] = pl.mark(("tensor", pl.last_real["tensor"]))
                stages = make_post(h, po, hh, blk, par, qts, with_rel,
                                   ring_cons, tilestate[0])
                pending = carries + [(6, stages[0]), (8, stages[1]),
                                     (11, stages[2]), (13, stages[3])]
            return pending

        wo_out = []

        bo_cons = [None]

        def wo_phase():
            barr = [psW_tok[0], psW_tok[1], psR_tok[0], psR_tok[1],
                    eW_tok[0], eW_tok[1], eR_tok[0], eR_tok[1],
                    rotK_tok[0], qd_tok[0], qd_tok[1]] +                    [ring_cons[i] for i in range(6)] +                    [pl.mark((e_, pl.last_real[e_]))
                    for e_ in ("tensor", "vector", "scalar")]
            for t_ in barr:
                for e_ in ("sync", "gpsimd", "tensor", "vector", "scalar"):
                    pl.wait(e_, t_)
            wp2 = [None, None]
            wrv_toks = []
            for oc in range(8):
                bslot = oc % 2
                if wp2[bslot] is not None:
                    pl.wait("sync", wp2[bslot])
                wtok = pl.dma("sync", wpb[bslot],
                              W["wo"].ap()[128 * oc:128 * (oc + 1), :], ch=f"wp{bslot}")
                pl.wait("vector", wtok)
                pl.fence("vector")
                ve(lambda e, b=wpb[bslot]: e.tensor_copy(hh_t, b))
                pl.fence("vector")
                spv = ve(lambda e, b=wpb[bslot]: e.tensor_tensor(
                    hl_t, b, hh_t, ALU.subtract))
                wp2[bslot] = pl.mark(spv)
                sp_tok = pl.fence("vector")
                pl.wait("tensor", sp_tok)
                transpose_to(WoT8[oc][0].rearrange("p (c n) -> p c n", n=128),
                             [hh_t[:, 128 * dc:128 * (dc + 1)] for dc in range(ND)],
                             0, src3d=True)
                wotok = transpose_to(WoT8[oc][1].rearrange("p (c n) -> p c n", n=128),
                                     [hl_t[:, 128 * dc:128 * (dc + 1)] for dc in range(ND)],
                                     1, src3d=True)
                WoTh3 = WoT8[oc][0].rearrange("p (c n) -> p c n", n=128)
                WoTl3 = WoT8[oc][1].rearrange("p (c n) -> p c n", n=128)
                pl.wait("tensor", wotok)
                pl.wait("tensor", rv_tok)
                bank_wait("tensor", "pb5")
                for dc in range(ND):
                    mm(pb[5][:, 0:1], WoTh3[:, dc], rvcat_h, dc == 0, False)
                    mm(pb[5][:, 0:1], WoTh3[:, dc], rvcat_l, False, False)
                    wrm = mm(pb[5][:, 0:1], WoTl3[:, dc], rvcat_h, False,
                             dc == ND - 1)
                pl.wait("sync", bo_cons[0])
                bctok = pl.dma("sync", nsc[:, 0:1],
                               bass.AP(Bi["bo"], 128 * oc, [[1, 128], [1, 1]]),
                               ch="boc")
                pl.wait("vector", pl.mark(wrm))
                pl.wait("vector", bctok)
                pl.fence("vector")
                bev = ve(lambda e, d=boext8[:, oc:oc + 1]: e.tensor_tensor(
                    d, pb[5][:, 0:1], nsc[:, 0:1], ALU.add))
                bo_cons[0] = pl.mark(bev)
                bank_set("pb5", bo_cons[0])
            ycons_a = [None, None]
            attb_cons = None
            for blk in range(4):
                cols = slice(512 * blk, 512 * (blk + 1))
                if attb_cons is not None:
                    pl.wait("sync", attb_cons)
                for t_ in att_wtoks[-4:]:
                    pl.wait("sync", t_)
                pl.dma("sync", attb_h3, bass.AP(
                    att_hbm, 512 * blk, [[T, 128], [128 * T, ND], [1, 512]]),
                    ch="atr")
                abtok = pl.dma("sync", attb_l3, bass.AP(
                    att_hbm, ND * 128 * T + 512 * blk,
                    [[T, 128], [128 * T, ND], [1, 512]]), ch="atr")
                pl.wait("tensor", abtok)
                ao_toks = []
                for oc in range(8):
                    WoTh3 = WoT8[oc][0].rearrange("p (c n) -> p c n", n=128)
                    WoTl3 = WoT8[oc][1].rearrange("p (c n) -> p c n", n=128)
                    bk_ = oc % 2
                    bank_wait("tensor", f"pb{bk_}")
                    for dc in range(ND):
                        mm(pb[bk_], WoTh3[:, dc], attb_h3[:, dc], dc == 0, False)
                        mm(pb[bk_], WoTh3[:, dc], attb_l3[:, dc], False, False)
                        prm = mm(pb[bk_], WoTl3[:, dc], attb_h3[:, dc],
                                 False, dc == ND - 1)
                    prt = pl.mark(prm)
                    attb_cons = prt
                    pl.wait("vector", prt)
                    pl.fence("vector")
                    aov = ve(lambda e, d=aosc[oc], b=pb[bk_][:], c=boext8[:, oc:oc + 1]:
                             e.tensor_scalar(d, b, c, None, ALU.add))
                    bank_set(f"pb{bk_}", pl.mark(aov))
                    ao_toks.append(pl.mark(aov))
                for jt in range(4):
                    tau = 4 * blk + jt
                    yb = ytile_a[jt % 2]
                    if ycons_a[jt % 2] is not None:
                        pl.wait("gpsimd", ycons_a[jt % 2])
                    xt2 = pl.dma("gpsimd", yb,
                                 src_dram.ap()[128 * tau:128 * (tau + 1), :],
                                 ch=f"ya{jt % 2}")
                    for oc in range(8):
                        pl.wait("tensor", ao_toks[oc])
                        bank_wait("tensor", f"pb{2 + oc % 2}")
                        trm = pl.emit("tensor", lambda e, s=aosc[oc][:, 128 * jt:128 * jt + 128],
                                      b=pb[2 + oc % 2][:, 0:128]:
                                      e.transpose(b, s, id_f32))
                        trt2 = pl.mark(trm)
                        pl.wait("vector", trt2)
                        pl.wait("vector", xt2)
                        pl.fence("vector")
                        yv = ve(lambda e, d=yb[:, 128 * oc:128 * oc + 128],
                                b=pb[2 + oc % 2][:, 0:128]:
                                e.tensor_tensor(d, d, b, ALU.add))
                        bank_set(f"pb{2 + oc % 2}", pl.mark(yv))
                    ytk = pl.fence("vector")
                    pl.wait("gpsimd", ytk)
                    stk = pl.dma("gpsimd", dst_dram.ap()[128 * tau:128 * (tau + 1), :],
                                 yb, ch=f"ys{jt % 2}")
                    ycons_a[jt % 2] = stk
                    wo_out.append(stk)

        pending_post = [None]
        if with_rel:
            pl.wait("scalar", rv_tok)
            pl.wait("scalar", rkx_tok0)
            wp_cons = [tt1_tok, rv_tok]
        else:
            wp_cons = [None, None]
        brow_cons = [None]
        for hp in range(8):
            po2 = 128 * hp
            for iw, (w, wt_d, bi_d, kscale) in enumerate([
                    ("q", W["wq"], Bi["bq"], None),
                    ("k", W["wk"], Bi["bk"], 0.125),
                    ("v", W["wv"], Bi["bv"], None)]):
                bslot = iw % 2
                if wp_cons[bslot] is not None:
                    pl.wait("sync", wp_cons[bslot])
                wtok = pl.dma("sync", wpb[bslot],
                              wt_d.ap()[po2:po2 + 128, :], ch=f"wp{bslot}")
                pl.wait("sync", brow_cons[0])
                btk = pl.dma("sync", brow, bass.AP(bi_d, po2, [[1, 1], [1, 128]]),
                             ch="br")
                pl.wait("vector", wtok)
                pl.wait("vector", btk)
                pl.fence("vector")
                if kscale is not None:
                    ve(lambda e, b=wpb[bslot], s=kscale: e.tensor_scalar(
                        b, b, s, None, ALU.mult))
                    ve(lambda e, s=kscale: e.tensor_scalar(brow, brow, s, None, ALU.mult))
                    pl.fence("vector")
                brt = pl.fence("vector")
                pl.wait("tensor", brt)
                pl.wait("tensor", btk)
                bank_wait("tensor", "pb2")
                btr = pl.emit("tensor", lambda e: e.transpose(
                    pb[2][:, 0:1], brow, onesrow[0:1, 0:1]))
                brow_cons[0] = pl.mark(btr)
                pl.wait("vector", brow_cons[0])
                ve(lambda e: e.tensor_copy(bcol_s, pb[2][:, 0:1]))
                bank_set("pb2", pl.fence("vector") or None)
                pl.fence("vector")
                ve(lambda e, d=bc2[:, 0:1]: e.tensor_copy(d, bcol_s))
                pl.fence("vector")
                ve(lambda e, d=bc2[:, 1:2], hb=bc2[:, 0:1]:
                   e.tensor_tensor(d, bcol_s, hb, ALU.subtract))
                bct = pl.fence("vector")
                pl.wait("tensor", bct)
                bank_wait("tensor", "pt0")
                bt2 = pl.emit("tensor", lambda e: e.transpose(
                    pt[0][0:2, 0:128], bc2, id_b))
                pl.wait("vector", pl.mark(bt2))
                b2v = ve(lambda e, d=bias2[w]: e.tensor_copy(d, pt[0][0:2, 0:128]))
                bank_set("pt0", pl.mark(b2v))
                ve(lambda e, b=wpb[bslot]: e.tensor_copy(hh_t, b))
                pl.fence("vector")
                spv = ve(lambda e, b=wpb[bslot]: e.tensor_tensor(
                    hl_t, b, hh_t, ALU.subtract))
                wp_cons[bslot] = pl.mark(spv)
                sp_tok = pl.fence("vector")
                pl.wait("tensor", sp_tok)
                transpose_to(WT[w][0].rearrange("p (c n) -> p c n", n=128),
                             [hh_t[:, 128 * dc:128 * (dc + 1)] for dc in range(ND)],
                             0, src3d=True)
                transpose_to(WT[w][1].rearrange("p (c n) -> p c n", n=128),
                             [hl_t[:, 128 * dc:128 * (dc + 1)] for dc in range(ND)],
                             1, src3d=True)
            WTq3 = [WT["q"][k].rearrange("p (c n) -> p c n", n=128) for k in (0, 1)]
            WTk3 = [WT["k"][k].rearrange("p (c n) -> p c n", n=128) for k in (0, 1)]
            WTv3 = [WT["v"][k].rearrange("p (c n) -> p c n", n=128) for k in (0, 1)]
            pl.wait("tensor", hT_tok)
            for blk in range(4):
                cols = slice(512 * blk, 512 * (blk + 1))
                for w, WT3, Dh, Dl in (("q", WTq3, Qh, Ql), ("k", WTk3, Kh, Kl)):
                    bk_ = blk % 2
                    bank_wait("tensor", f"pb{bk_}")
                    for dc in range(ND):
                        rhsh = hTh4[:, dc, 4 * blk:4 * blk + 4, :]
                        rhsl = hTl4[:, dc, 4 * blk:4 * blk + 4, :]
                        mm(pb[bk_], WT3[0][:, dc], rhsh, dc == 0, False)
                        mm(pb[bk_], WT3[0][:, dc], rhsl, False, False)
                        mm(pb[bk_], WT3[1][:, dc], rhsh, False, False)
                    mmk = mm(pb[bk_], bias2[w], ones2row, False, True)
                    tok_mm = pl.mark(mmk)
                    pl.wait("vector", tok_mm)
                    pl.fence("vector")
                    ve(lambda e, b=pb[bk_][:], d=Dh[:, cols]: e.tensor_copy(d, b))
                    pl.fence("vector")
                    evv = ve(lambda e, b=pb[bk_][:], hb=Dh[:, cols], d=Dl[:, cols]:
                             e.tensor_tensor(d, b, hb, ALU.subtract))
                    bank_set(f"pb{bk_}", pl.mark(evv))
            for tau in range(NT):
                bk_ = 2 + tau % 2
                bank_wait("tensor", f"pb{bk_}")
                for dc in range(ND):
                    lh = hTh4[:, dc, tau, :]
                    ll = hTl4[:, dc, tau, :]
                    mm(pb[bk_][:, 0:128], lh, WTv3[0][:, dc], dc == 0, False)
                    mm(pb[bk_][:, 0:128], lh, WTv3[1][:, dc], False, False)
                    mm(pb[bk_][:, 0:128], ll, WTv3[0][:, dc], False, False)
                mmk = mm(pb[bk_][:, 0:128], ones2row[:, 0:128], bias2["v"],
                         False, True)
                tok_mm = pl.mark(mmk)
                pl.wait("vector", tok_mm)
                pl.fence("vector")
                ve(lambda e, b=pb[bk_][:, 0:64],
                   d=Vx_h3[:, tau, 0:64]: e.tensor_copy(d, b))
                ve(lambda e, b=pb[bk_][:, 64:128],
                   d=Vx_h3[:, tau, 96:160]: e.tensor_copy(d, b))
                pl.fence("vector")
                ve(lambda e, b=pb[bk_][:, 0:64], hb=Vx_h3[:, tau, 0:64],
                   d=Vx_l3[:, tau, 0:64]:
                   e.tensor_tensor(d, b, hb, ALU.subtract))
                evv = ve(lambda e, b=pb[bk_][:, 64:128], hb=Vx_h3[:, tau, 96:160],
                         d=Vx_l3[:, tau, 96:160]:
                         e.tensor_tensor(d, b, hb, ALU.subtract))
                bank_set(f"pb{bk_}", pl.mark(evv))
            ind3 = ind_b.rearrange("p (t n) -> p t n", n=16)
            pl.fence("vector")
            ve(lambda e, d=Vx_h3[:, :, 64:80]: e.tensor_copy(d, ind3))
            ve(lambda e, d=Vx_h3[:, :, 80:96]: e.tensor_scalar(
                d, ind3, 0.0, None, ALU.mult))
            ve(lambda e, d=Vx_l3[:, :, 64:80]: e.tensor_scalar(
                d, ind3, 0.0, None, ALU.mult))
            ve(lambda e, d=Vx_l3[:, :, 80:96]: e.tensor_scalar(
                d, ind3, 0.0, None, ALU.mult))
            for src_b, col0 in ((Qh, 0), (Kh, 2)):
                pl.fence("vector")
                ve(lambda e, s=src_b: e.tensor_tensor(sqb, s, s, ALU.mult))
                sq_tok = pl.fence("vector")
                pl.wait("tensor", sq_tok)
                for blkn in range(4):
                    bank_wait("tensor", "pb2")
                    nm2 = mm(pb[2][0:2, 0:512], hm_b,
                             sqb[:, 512 * blkn:512 * (blkn + 1)], True, True)
                    pl.wait("vector", pl.mark(nm2))
                    pl.fence("vector")
                    rv2 = ve(lambda e: e.reduce_max(macc[0:2, 1:2],
                                                    pb[2][0:2, 0:512], axis=AX))
                    bank_set("pb2", pl.mark(rv2))
                    pl.fence("vector")
                    if blkn == 0:
                        ve(lambda e: e.tensor_copy(macc[0:2, 0:1], macc[0:2, 1:2]))
                    else:
                        ve(lambda e: e.tensor_tensor(macc[0:2, 0:1], macc[0:2, 0:1],
                                                     macc[0:2, 1:2], ALU.max))
                tp_tok = pl.fence("vector")
                pl.wait("tensor", tp_tok)
                bank_wait("tensor", "pb2")
                tmm = mm(pb[2][0:1, col0:col0 + 2], macc[0:2, 0:1], id2_f, True, True)
                pl.wait("vector", pl.mark(tmm))
                mvv = ve(lambda e, c=col0: e.tensor_copy(mrow4[:, c:c + 2],
                                                         pb[2][0:1, c:c + 2]))
                bank_set("pb2", pl.mark(mvv))
                mvt = pl.fence("vector")
            pl.wait("scalar", mvt)
            sq3 = sc(lambda e: e.activation(mrow4, mrow4, ACT.Sqrt))
            pl.wait("vector", pl.mark(sq3))
            pl.fence("vector")
            ve(lambda e: e.tensor_scalar(tb_s, mrow4[:, 2:4], rkm, None, ALU.add))
            pl.fence("vector")
            nbv = ve(lambda e: e.scalar_tensor_tensor(
                tb_s, tb_s, -1.05, mrow4[:, 0:2], ALU.mult, ALU.mult))
            nb_tok = pl.fence("vector")
            pl.wait("tensor", nb_tok)
            bank_wait("tensor", "pb2")
            nbm = mm(pb[2][:, 0:2], onesrow, tb_s, True, True)
            pl.wait("vector", pl.mark(nbm))
            nbe = ve(lambda e: e.tensor_copy(negB2, pb[2][:, 0:2]))
            bank_set("pb2", pl.mark(nbe))
            nbe_t = pl.fence("vector")
            rot_gate[0] = nbe_t
            pl.wait("sync", nbe_t)
            for hh2 in range(2):
                pl.dma("sync", Kcat3[0:64, hh2, :],
                       Kh[64 * hh2:64 * hh2 + 64, :], ch="rot1")
                rotK_tok[0] = pl.dma("sync", Kcat3[64:128, hh2, :],
                                     Kl[64 * hh2:64 * hh2 + 64, :], ch="rot1")
            for hh in range(2):
                pending_post[0] = attn_head(2 * hp + hh, 64 * hh, hh, with_rel,
                                            pending_post[0])
        if pending_post[0] is not None:
            for sl_, fn_ in sorted(pending_post[0], key=lambda x: x[0]):
                fn_()
            pending_post[0] = None
        wo_phase()
        out_toks = wo_out[-2:] + [pl.mark((e_, pl.last_real[e_]))
                                  for e_ in ("tensor", "vector", "scalar")]
        al.pop()
        return out_toks


    prep_done = [tok_mrep, btok.get("wqfull"),
                 wq_store["wd"][0], wq_store["wd"][1]]
    if stage == 1:
        final_toks = ffn_sublayer(x_in, y_out, "ffn1", pre_tok=prep_done)
    elif stage == 3:
        final_toks = attention_sublayer(x_in, y_out, with_rel=False,
                                        pre_tok=prep_done)
    elif stage == 4:
        final_toks = attention_sublayer(x_in, y_out, with_rel=True,
                                        pre_tok=prep_done)
    elif stage == 5:
        t1 = ffn_sublayer(x_in, xres, "ffn1", pre_tok=prep_done)
        t2 = attention_sublayer(xres, xres, with_rel=True, pre_tok=t1)
        final_toks = ffn_sublayer(xres, y_out, "ffn2", pre_tok=t2)
    else:
        raise NotImplementedError

    for t in final_toks:
        teng, kind, sem, val = t
        if kind == "dma":
            pl.emit("gpsimd", lambda e, s=sem, v=val: e.wait_ge(s, v))

    pl.replay()
    return nc


_NC_CACHE = {}


def _get_nc(stage):
    if stage not in _NC_CACHE:
        _NC_CACHE[stage] = build_program(stage)
    return _NC_CACHE[stage]


def kernel(L0_attn_x, L0_attn_Wq, L0_attn_bq, L0_attn_Wk, L0_attn_bk,
           L0_attn_Wv, L0_attn_bv, L0_attn_Wo, L0_attn_bo,
           L0_attn_rel_k, L0_attn_rel_v,
           L0_ffn_w_gate, L0_ffn_w_up, L0_ffn_w_down, _stage=5):
    nc = _get_nc(_stage)
    shared = dict(
        wq=np.ascontiguousarray(L0_attn_Wq, dtype=np.float32),
        wk=np.ascontiguousarray(L0_attn_Wk, dtype=np.float32),
        wv=np.ascontiguousarray(L0_attn_Wv, dtype=np.float32),
        wo=np.ascontiguousarray(L0_attn_Wo, dtype=np.float32),
        bq=np.ascontiguousarray(L0_attn_bq, dtype=np.float32),
        bk=np.ascontiguousarray(L0_attn_bk, dtype=np.float32),
        bv=np.ascontiguousarray(L0_attn_bv, dtype=np.float32),
        bo=np.ascontiguousarray(L0_attn_bo, dtype=np.float32),
        relk=np.ascontiguousarray(L0_attn_rel_k, dtype=np.float32),
        relv=np.ascontiguousarray(L0_attn_rel_v, dtype=np.float32),
        wg=np.ascontiguousarray(L0_ffn_w_gate, dtype=np.float32),
        wu=np.ascontiguousarray(L0_ffn_w_up, dtype=np.float32),
        wd=np.ascontiguousarray(L0_ffn_w_down, dtype=np.float32),
    )
    shared.update({k: v for k, v in CONSTS.items()})
    xb = np.ascontiguousarray(np.asarray(L0_attn_x, dtype=np.float32))
    in_maps = []
    for c in range(8):
        m = dict(shared)
        m["x_in"] = xb[c]
        in_maps.append(m)
    res = run_bass_kernel_spmd(nc, in_maps, core_ids=list(range(8)))
    return np.stack([r["y_out"] for r in res.results], axis=0)


# version 20 (final)
# speedup vs baseline: 1.0388x; 1.0216x over previous
import numpy as np
import ml_dtypes
import concourse.bass as bass
import concourse.mybir as mybir
from concourse.bass_utils import run_bass_kernel_spmd

F32 = mybir.dt.float32
BF16 = mybir.dt.bfloat16
I32 = mybir.dt.int32
AX = mybir.AxisListType.X
ALU = mybir.AluOpType
ACT = mybir.ActivationFunctionType

T, D, FF, H, HD = 2048, 1024, 4096, 16, 64
NT, ND, NF = T // 128, D // 128, FF // 128
ENGS = ("sync", "scalar", "tensor", "vector", "gpsimd")
BB = ml_dtypes.bfloat16


class Op:
    __slots__ = ("fn", "incs")

    def __init__(self, fn):
        self.fn = fn
        self.incs = []


class Plan:
    def __init__(self, nc):
        self.nc = nc
        self.q = {e: [] for e in ENGS}
        self.ctr = {e: [nc.alloc_semaphore(name=f"c_{e}"), 0] for e in ENGS}
        self.dch = {}
        self.waited = {e: {} for e in ENGS}

    def emit(self, eng, fn, is_wait=False, is_dma=False):
        op = Op(fn)
        self.q[eng].append(op)
        if not (is_wait or is_dma):
            self.last_real = getattr(self, "last_real", {})
            self.last_real[eng] = op
        return (eng, op)

    def dma(self, eng, out, in_, ch="d0"):
        if ch not in self.dch:
            self.dch[ch] = [self.nc.alloc_semaphore(name=f"dm_{ch}"), 0]
        ent = self.dch[ch]
        sem = ent[0]
        if ent[1] > 0:
            self.wait(eng, (eng, "dma", sem, ent[1]))
        ent[1] += 16
        self.emit(eng, lambda e, o=out, i=in_, s=sem:
                  e.dma_start(out=o, in_=i).then_inc(s, 16), is_dma=True)
        return (eng, "dma", sem, ent[1])

    def mark(self, emitted):
        eng, op = emitted
        if op.incs:
            return (eng, "c", op.incs[0][0], op.incs[0][1])
        sem, v = self.ctr[eng]
        self.ctr[eng][1] = v + 1
        op.incs.append((sem, v + 1))
        return (eng, "c", sem, v + 1)

    def wait(self, eng, tok):
        if tok is None:
            return
        teng, kind, sem, val = tok
        seen = self.waited[eng].get(id(sem), 0)
        if val <= seen:
            return
        self.waited[eng][id(sem)] = val
        self.emit(eng, lambda e, s=sem, v=val: e.wait_ge(s, v), is_wait=True)

    def fence(self, eng):
        op = getattr(self, "last_real", {}).get(eng)
        if op is None:
            return None
        tok = self.mark((eng, op))
        self.wait(eng, tok)
        return tok

    def waits(self, eng, toks):
        for t in toks:
            self.wait(eng, t)

    def replay(self):
        with self.nc.Block() as blk:
            for name in ENGS:
                ops = self.q[name]

                def mk(ops):
                    def body(eng):
                        for op in ops:
                            ins = op.fn(eng)
                            for sem, val in op.incs:
                                ins.then_inc(sem, 1)
                    return body

                getattr(blk, name)(mk(ops))


def bfv(ap):
    return ap.bitcast(BF16)


def build_consts():
    c = {}
    c["c_idb"] = np.eye(128).astype(BB)
    ind = np.zeros((128, 16, 16), dtype=np.float32)
    for t in range(16):
        ind[:, t, t] = 1.0
    c["c_ind"] = ind.astype(BB)
    p = np.arange(128)
    c["c_lt"] = (p[None, :] <= p[:, None]).astype(BB)
    pre = np.zeros((16, 4, 5), dtype=np.float32)
    for b in range(4):
        for j in range(4):
            pre[: max(4 * b + j - 1, 0), b, j] = 1.0
        pre[:, b, 4] = 1.0
    c["c_pre"] = pre.astype(BB)
    c["c_onesrow"] = np.ones((1, 128), dtype=np.float32)
    c["c_idf32"] = np.eye(128, dtype=np.float32)
    c["c_id2"] = np.eye(2, dtype=np.float32)
    c["c_hmb"] = c["c_hm"] if False else None
    c["c_pre32"] = None
    c["c_zeros"] = np.zeros((2, 512)).astype(BB)
    c["c_onescol"] = np.ones((128, 1), dtype=np.float32)
    c["c_ones2row"] = np.ones((2, 512)).astype(BB)
    sel = np.zeros((2, 4, 128), dtype=np.float32)
    sel[0, 0:2] = 1.0
    sel[1, 2:4] = 1.0
    c["c_sel"] = sel.astype(BB)
    hm = np.zeros((128, 2), dtype=np.float32)
    hm[0:64, 0] = 1.0
    hm[64:128, 1] = 1.0
    c["c_hm"] = hm
    c["c_hmb"] = hm.astype(BB)
    c["c_pre32"] = np.ascontiguousarray(c["c_pre"].astype(np.float32))
    return c


CONSTS = {k: v for k, v in build_consts().items() if v is not None}


def build_program(stage=3):
    nc = bass.Bass("TRN2", target_bir_lowering=False, debug=False)
    pl = Plan(nc)

    x_in = nc.dram_tensor("x_in", [T, D], F32, kind="ExternalInput")
    W = {n: nc.dram_tensor(n, list(s), F32, kind="ExternalInput") for n, s in
         [("wq", (D, D)), ("wk", (D, D)), ("wv", (D, D)), ("wo", (D, D)),
          ("wg", (FF, D)), ("wu", (FF, D)), ("wd", (D, FF))]}
    Bi = {n: nc.dram_tensor(n, [D], F32, kind="ExternalInput")
          for n in ("bq", "bk", "bv", "bo")}
    relk = nc.dram_tensor("relk", [257, HD], F32, kind="ExternalInput")
    relv = nc.dram_tensor("relv", [257, HD], F32, kind="ExternalInput")
    CT = {k: nc.dram_tensor(k, list(v.shape),
                            F32 if v.dtype == np.float32 else BF16,
                            kind="ExternalInput") for k, v in CONSTS.items()}
    y_out = nc.dram_tensor("y_out", [T, D], F32, kind="ExternalOutput")

    xres = nc.dram_tensor("xres", [T, D], F32, kind="Internal")
    wqh = {"wg": nc.dram_tensor("wgq_h", [D, FF], BF16, kind="Internal"),
           "wu": nc.dram_tensor("wuq_h", [D, FF], BF16, kind="Internal"),
           "wd": nc.dram_tensor("wdq_h", [FF, D], BF16, kind="Internal")}
    e_h = nc.dram_tensor("e_h", [4, 2, 128 * 511], BF16, kind="Internal")
    att_hbm = nc.dram_tensor("att_hbm", [2 * ND * 128 * T], BF16, kind="Internal")
    s_h = nc.dram_tensor("s_h", [2, 2, 128 * 384], BF16, kind="Internal")

    ACOLS = 53056
    arena = nc.alloc_sbuf_tensor("arena", [128, ACOLS], F32)
    AAP = arena.ap()

    class Alloc:
        def __init__(self):
            self.top = 0
            self.marks = []
            self.peak = 0

        def take(self, ncols):
            a = self.top
            assert a + ncols <= ACOLS, f"arena overflow {a}+{ncols}"
            self.top = a + ncols
            self.peak = max(self.peak, self.top)
            return AAP[:, a:a + ncols]

        def takeb(self, nb):
            return bfv(self.take((nb + 1) // 2))[:, 0:nb]

        def push(self):
            self.marks.append(self.top)

        def pop(self):
            self.top = self.marks.pop()

    al = Alloc()

    pb = [nc.alloc_psum_tensor(f"pb{i}", [128, 512], F32) for i in range(6)]
    pt = [nc.alloc_psum_tensor(f"pt{i}", [128, 1024], BF16) for i in range(2)]
    btok = {}

    def bank_wait(eng, key):
        pl.wait(eng, btok.get(key))

    def bank_set(key, tok):
        btok[key] = tok

    id_b = al.takeb(128)
    ind_b = al.takeb(256)
    lt_b = al.takeb(128)
    pre_b = al.takeb(20)[0:16, :]
    onesrow = al.take(128)[0:1, :]
    onescol = al.take(1)
    ones2row = al.takeb(512)[0:2, :]
    hm_f = al.take(2)
    id_f32 = al.take(128)
    id2_f = al.take(2)[0:2, :]
    hm_b = al.takeb(2)
    pre32_b = al.take(20)[0:16, :]
    zrows_b = al.takeb(512)[0:2, :]
    zcol_b = zrows_b[0:1, 0:1]
    zrow_b = zrows_b[0:1, 0:512]
    toks = [
        pl.dma("sync", id_f32, CT["c_idf32"].ap(), ch="const"),
        pl.dma("sync", id2_f, CT["c_id2"].ap(), ch="const"),
        pl.dma("sync", hm_b, CT["c_hmb"].ap(), ch="const"),
        pl.dma("sync", pre32_b, CT["c_pre32"].ap().rearrange("p a b -> p (a b)"), ch="const"),
        pl.dma("sync", zrows_b, CT["c_zeros"].ap(), ch="const"),
        pl.dma("sync", id_b, CT["c_idb"].ap(), ch="const"),
        pl.dma("sync", ind_b, CT["c_ind"].ap().rearrange("p a b -> p (a b)"), ch="const"),
        pl.dma("sync", lt_b, CT["c_lt"].ap(), ch="const"),
        pl.dma("sync", pre_b, CT["c_pre"].ap().rearrange("p a b -> p (a b)"), ch="const"),
        pl.dma("sync", onesrow, CT["c_onesrow"].ap(), ch="const"),
        pl.dma("sync", onescol, CT["c_onescol"].ap(), ch="const"),
        pl.dma("sync", ones2row, CT["c_ones2row"].ap(), ch="const"),
        pl.dma("sync", hm_f, CT["c_hm"].ap(), ch="const"),
    ]
    sel4 = al.takeb(256)[0:4, :]
    toks.append(pl.dma("sync", sel4.rearrange("p (s n) -> p s n", n=128),
                       bass.AP(CT["c_sel"], 0, [[128, 4], [512, 2], [1, 128]]), ch="const"))
    for e in ENGS:
        pl.wait(e, toks[-1])

    def ap_of(x):
        return x if isinstance(x, bass.AP) else x[:]

    def mm(bank_ap, lhsT, rhs, start, stop, skip=False):
        return pl.emit("tensor", lambda e, a=ap_of(bank_ap), l=ap_of(lhsT),
                       r=ap_of(rhs), s=start, t=stop, k=skip:
                       e.matmul(a, l, r, start=s, stop=t, skip_group_check=k))

    def tr(pt_ap, src_b):
        return pl.emit("tensor", lambda e, a=pt_ap, s=src_b:
                       e.transpose(a, s, id_b))

    def ve(fn):
        return pl.emit("vector", fn)

    def sc(fn):
        return pl.emit("scalar", fn)

    def transpose_to(dst, src_blocks, ptsel, src3d=False):
        nblk = len(src_blocks)
        assert nblk <= 8
        pt_b = pt[ptsel]
        bank_wait("tensor", f"pt{ptsel}")
        for k, s in enumerate(src_blocks):
            ttk = tr(pt_b[:, 128 * k:128 * (k + 1)], s)
        tok = pl.mark(ttk)
        pl.wait("vector", tok)
        srcv = pt_b[:, 0:128 * nblk]
        if src3d:
            srcv = srcv.rearrange("p (c n) -> p c n", n=128)
        evk = ve(lambda e, d=dst, s=srcv: e.tensor_copy(d, s))
        etok = pl.mark(evk)
        bank_set(f"pt{ptsel}", etok)
        return etok

    al.push()
    wbuf = [al.take(1024) for _ in range(2)]
    wsum = al.take(1)
    red1 = al.take(1)
    mrow = al.take(1)[0:1, :]
    srow = al.take(1)[0:1, :]
    srep = al.take(1)
    dqk = al.take(3)[0:1, :]
    i32t = al.take(1024)
    wq_full = al.takeb(NF * 1024)
    rowbuf = [al.takeb(4096) for _ in range(2)]

    wspec = [("wg", W["wg"], wqh["wg"], FF, D), ("wu", W["wu"], wqh["wu"], FF, D),
             ("wd", W["wd"], wqh["wd"], D, FF)]
    wq_store = {}
    cons_tok = [None, None]
    st_tok = [None, None]
    for wi, (nm, wt, wqt, RO, CO) in enumerate(wspec):
        nch, ncc = RO // 128, CO // 128
        npc = CO // 1024
        pieces = [(ch, cp) for ch in range(nch) for cp in range(npc)]
        pl.fence("vector")
        ve(lambda e: e.tensor_scalar(wsum, onescol, 0.0, None, ALU.mult))
        lastv = None
        for pi, (ch, cp) in enumerate(pieces):
            bi = pi % 2
            if cons_tok[bi] is not None:
                pl.wait("sync", cons_tok[bi])
            ltok = pl.dma("sync", wbuf[bi],
                          wt.ap()[128 * ch:128 * (ch + 1), 1024 * cp:1024 * (cp + 1)],
                          ch=f"wqp{bi}")
            pl.wait("vector", ltok)
            pl.fence("vector")
            rv = ve(lambda e, b=wbuf[bi]: e.reduce_sum(
                red1, b, axis=AX, apply_absolute_value=True))
            cons_tok[bi] = pl.mark(rv)
            pl.fence("vector")
            lastv = ve(lambda e: e.tensor_tensor(wsum, wsum, red1, ALU.add))
        tok_sums = pl.mark(lastv)
        pl.wait("tensor", tok_sums)
        bank_wait("tensor", "pb0")
        m1 = mm(pb[0][0:1, 0:1], onescol, wsum, True, True)
        pl.wait("vector", pl.mark(m1))
        ve(lambda e, s=1.0 / (RO * CO): e.tensor_scalar(
            mrow, pb[0][0:1, 0:1], s, 1e-5, ALU.mult, ALU.max))
        pl.fence("vector")
        ve(lambda e: e.reciprocal(srow, mrow))
        kv = ve(lambda e, c=dqk[:, wi:wi + 1]: e.tensor_copy(c, mrow))
        tok_s = pl.mark(kv)
        bank_set("pb0", tok_s)
        pl.wait("tensor", tok_s)
        bank_wait("tensor", "pb0")
        m2 = mm(pb[0][:, 0:1], onesrow, srow, True, True)
        pl.wait("vector", pl.mark(m2))
        evs = ve(lambda e: e.tensor_copy(srep, pb[0][:, 0:1]))
        bank_set("pb0", pl.mark(evs))
        bank_wait("vector", "wqfull")
        wq3 = wq_full.rearrange("p (c n) -> p c n", n=CO)
        qv = None
        for pi, (ch, cp) in enumerate(pieces):
            bi = pi % 2
            if cons_tok[bi] is not None:
                pl.wait("sync", cons_tok[bi])
            ltok = pl.dma("sync", wbuf[bi],
                          wt.ap()[128 * ch:128 * (ch + 1), 1024 * cp:1024 * (cp + 1)],
                          ch=f"wqp{bi}")
            pl.wait("vector", ltok)
            pl.fence("vector")
            fv = ve(lambda e, b=wbuf[bi]: e.tensor_scalar(
                i32t.bitcast(I32)[:, 0:1024], b, srep, None, ALU.mult))
            cons_tok[bi] = pl.mark(fv)
            pl.fence("vector")
            qv = ve(lambda e, d=wq3[:, ch, 1024 * cp:1024 * (cp + 1)]:
                    e.tensor_scalar(d, i32t.bitcast(I32)[:, 0:1024],
                                    -1.0, 1.0, ALU.max, ALU.min))
        tok_q = pl.mark(qv)
        pl.wait("tensor", tok_q)
        for rc in range(ncc):
            bo = rc % 2
            if st_tok[bo] is not None:
                pl.wait("vector", st_tok[bo])
            evk_tok = None
            for rnd in range((nch + 7) // 8):
                nblk = min(8, nch - 8 * rnd)
                blocks = [wq3[:, 8 * rnd + k, 128 * rc:128 * rc + 128]
                          for k in range(nblk)]
                evk_tok = transpose_to(
                    rowbuf[bo][:, 1024 * rnd:1024 * rnd + 128 * nblk],
                    blocks, rnd % 2)
            pl.wait("gpsimd", evk_tok)
            st_tok[bo] = pl.dma("gpsimd", wqt.ap()[128 * rc:128 * (rc + 1), :],
                                rowbuf[bo][:, 0:RO], ch=f"wqs{bo}")
        bank_set("wqfull", evk_tok)
        wq_store[nm] = [st_tok[0], st_tok[1]]
    al.pop()

    mrep = al.take(3)
    bank_wait("tensor", "pb0")
    m3 = mm(pb[0][:, 0:3], onesrow, dqk[:, 0:3], True, True)
    pl.wait("vector", pl.mark(m3))
    evm = ve(lambda e: e.tensor_copy(mrep, pb[0][:, 0:3]))
    tok_mrep = pl.mark(evm)
    bank_set("pb0", tok_mrep)

    ln_m = al.take(1)
    ln_sq = al.take(1)
    ln_v = al.take(1)
    ln_rs = al.take(1)
    qa = al.take(1)
    qs = al.take(1)
    i32q = al.take(1024)
    sq_scr = al.take(1024)

    def ln_tile(xt, h_out):
        ve(lambda e: e.reduce_sum(ln_m, xt, axis=AX))
        pl.fence("vector")
        tok = pl.mark(ve(lambda e: e.tensor_scalar(ln_m, ln_m, 1.0 / D, None, ALU.mult)))
        s1 = sc(lambda e: e.activation(sq_scr, xt, ACT.Square, accum_out=ln_sq))
        pl.wait("vector", pl.mark(s1))
        pl.wait("vector", tok)
        ve(lambda e: e.tensor_tensor(ln_v, ln_m, ln_m, ALU.mult))
        pl.fence("vector")
        ve(lambda e: e.tensor_scalar(
            ln_v, ln_sq, 1.0 / D, ln_v, ALU.mult, ALU.subtract))
        pl.fence("vector")
        t2 = pl.mark(ve(lambda e: e.tensor_scalar(
            ln_v, ln_v, 1e-5, None, ALU.add)))
        pl.wait("scalar", t2)
        s2 = sc(lambda e: e.activation(ln_rs, ln_v, ACT.Sqrt))
        pl.wait("vector", pl.mark(s2))
        ve(lambda e: e.reciprocal(ln_rs, ln_rs))
        pl.fence("vector")
        return pl.mark(ve(lambda e: e.tensor_scalar(
            h_out, xt, ln_m, ln_rs, ALU.subtract, ALU.mult)))

    def quant_tok(src, n, qdst_b, dq_col):
        pl.fence("vector")
        ve(lambda e: e.reduce_max(qa, src, axis=AX, apply_absolute_value=True))
        pl.fence("vector")
        ve(lambda e: e.tensor_scalar(qa, qa, 1e-5, 1.0 / 127.0, ALU.max, ALU.mult))
        pl.fence("vector")
        ve(lambda e: e.tensor_copy(dq_col, qa))
        ve(lambda e: e.reciprocal(qs, qa))
        nb = (n + 1023) // 1024
        for b in range(nb):
            c0, c1 = 1024 * b, min(1024 * (b + 1), n)
            pl.fence("vector")
            ve(lambda e, s=src[:, c0:c1], d=i32q.bitcast(I32)[:, 0:c1 - c0]:
               e.tensor_scalar(d, s, qs, None, ALU.mult))
            pl.fence("vector")
            qv = ve(lambda e, s=i32q.bitcast(I32)[:, 0:c1 - c0],
                    d=qdst_b[:, c0:c1]: e.tensor_copy(d, s))
        return pl.mark(qv)


    def ffn_sublayer(src_dram, dst_dram, tag, pre_tok=None):
        al.push()
        xtile = al.take(1024)
        htile = al.take(1024)
        xq_b = al.takeb(1024)
        xqT = al.takeb(ND * 8 * 128)
        xqT4 = xqT.rearrange("p (c t n) -> p c t n", t=8, n=128)
        sg = al.take(8)
        su = al.take(8)
        sd = al.take(8)
        dqx = al.take(1)
        pamax = al.take(8)
        p_region = al.take(8 * 4096)
        p_half = [p_region[:, 4096 * k:4096 * (k + 1)] for k in range(8)]
        wgb = [al.takeb(ND * 512) for _ in range(2)]
        wub = [al.takeb(ND * 512) for _ in range(2)]
        wgbF = [bfv(b).bitcast(F32) for b in wgb]
        ytile = [wgbF[0][:, 0:1024], wgbF[1][:, 0:1024]]
        gsil_t = al.take(512)
        gsil = [gsil_t, gsil_t]

        wgb3 = [b.rearrange("p (c n) -> p c n", n=512) for b in wgb]
        wub3 = [b.rearrange("p (c n) -> p c n", n=512) for b in wub]
        p_regb = bfv(p_region)
        wdb_chunks = [p_regb[:, 8192 * (oc // 4) + 1024 * (oc % 4):
                             8192 * (oc // 4) + 1024 * (oc % 4) + 1024]
                      for oc in range(NF)]
        wdb_dst = [p_regb[:, 8192 * a:8192 * a + 4096]
                   .rearrange("p (b n) -> p b n", n=1024) for a in range(8)]
        wdb_src = [bass.AP(wqh["wd"], 512 * D * a,
                           [[D, 128], [128 * D, 4], [1, 1024]])
                   for a in range(8)]

        xcons = None
        wcons = [None, None]
        wdcons = None
        ycons = [None, None]
        gtok = [None]
        pre_list = pre_tok if isinstance(pre_tok, list) else (
            [pre_tok] if pre_tok is not None else [])
        for tk_ in pre_list:
            for e_ in ("sync", "vector", "scalar", "tensor", "gpsimd"):
                pl.wait(e_, tk_)
        half_tok = None
        out_toks = []
        for half in range(2):
            xqT_tok = None
            for ti in range(8):
                tau = 8 * half + ti
                if xcons is not None:
                    pl.wait("sync", xcons)
                xtok = pl.dma("sync", xtile,
                              src_dram.ap()[128 * tau:128 * (tau + 1), :],
                              ch="xa")
                if half_tok is not None:
                    pl.wait("vector", half_tok)
                    pl.wait("scalar", half_tok)
                    pl.wait("tensor", half_tok)
                pl.wait("vector", xtok)
                pl.fence("vector")
                pl.wait("scalar", xtok)
                htok = ln_tile(xtile, htile)
                xcons = htok
                qtok = quant_tok(htile, 1024, xq_b, dqx)
                pl.wait("vector", tok_mrep)
                pl.fence("vector")
                ve(lambda e, c=sg[:, ti:ti + 1]: e.tensor_scalar(
                    c, dqx, mrep[:, 0:1], None, ALU.mult))
                ve(lambda e, c=su[:, ti:ti + 1]: e.tensor_scalar(
                    c, dqx, mrep[:, 1:2], None, ALU.mult))
                pl.wait("tensor", qtok)
                xqT_tok = transpose_to(
                    xqT4[:, :, ti, :],
                    [xq_b[:, 128 * dc:128 * (dc + 1)] for dc in range(ND)],
                    ti % 2, src3d=True)
            for ob in range(8):
                bo = ob % 2
                if wcons[bo] is not None:
                    pl.wait("sync", wcons[bo])
                if ycons[bo] is not None:
                    pl.wait("sync", ycons[bo])
                pl.waits("sync", wq_store["wg"] + wq_store["wu"])
                wtok1 = pl.dma("sync", wgb3[bo], bass.AP(
                    wqh["wg"], 512 * ob, [[FF, 128], [128 * FF, ND], [1, 512]]),
                    ch=f"wgb{bo}")
                wtok2 = pl.dma("sync", wub3[bo], bass.AP(
                    wqh["wu"], 512 * ob, [[FF, 128], [128 * FF, ND], [1, 512]]),
                    ch=f"wub{bo}")
                pl.wait("tensor", wtok1)
                pl.wait("tensor", wtok2)
                pl.wait("tensor", xqT_tok)
                for ti in range(8):
                    bg, bgk = pb[ti % 2], f"pb{ti % 2}"
                    bu, buk = pb[2 + ti % 2], f"pb{2 + ti % 2}"
                    bank_wait("tensor", bgk)
                    bank_wait("tensor", buk)
                    mku = None
                    for dc in range(ND):
                        lhs = xqT4[:, dc, ti, :]
                        mm(bg, lhs, wgb3[bo][:, dc], dc == 0, dc == ND - 1)
                        mku = mm(bu, lhs, wub3[bo][:, dc], dc == 0, dc == ND - 1)
                    tok_mm = pl.mark(mku)
                    pl.wait("scalar", tok_mm)
                    if gtok[0] is not None:
                        pl.wait("scalar", gtok[0])
                    stok = sc(lambda e, b=bg[:], d=sg[:, ti:ti + 1], g=gsil[ti % 2]:
                              e.activation(g, b, ACT.Silu, scale=d))
                    smk = pl.mark(stok)
                    bank_set(bgk, smk)
                    pl.wait("vector", smk)
                    ptok = ve(lambda e, b=bu[:], d=su[:, ti:ti + 1], g=gsil[ti % 2],
                              dst=p_half[ti][:, 512 * ob:512 * (ob + 1)]:
                              e.scalar_tensor_tensor(dst, b, d, g,
                                                     ALU.mult, ALU.mult))
                    pmk = pl.mark(ptok)
                    bank_set(buk, pmk)
                    gtok[0] = pmk
                    pl.fence("vector")
                    ve(lambda e, p=p_half[ti][:, 512 * ob:512 * (ob + 1)]:
                       e.reduce_max(qa, p, axis=AX, apply_absolute_value=True))
                    pl.fence("vector")
                    if ob == 0:
                        mxv = ve(lambda e, a=pamax[:, ti:ti + 1]: e.tensor_copy(a, qa))
                    else:
                        mxv = ve(lambda e, a=pamax[:, ti:ti + 1]:
                                 e.tensor_tensor(a, a, qa, ALU.max))
                    wcons[bo] = pl.mark(mxv) if ti == 7 else wcons[bo]
            xq2T_tok = [None] * 8
            for ti in range(8):
                pl.fence("vector")
                ve(lambda e, a=pamax[:, ti:ti + 1]: e.tensor_scalar(
                    qa, a, 1e-5, 1.0 / 127.0, ALU.max, ALU.mult))
                pl.fence("vector")
                ve(lambda e, c=sd[:, ti:ti + 1]: e.tensor_scalar(
                    c, qa, mrep[:, 2:3], 0.5, ALU.mult, ALU.mult))
                ve(lambda e: e.reciprocal(qs, qa))
                xq2 = bfv(p_half[ti])[:, 0:4096]
                qv = None
                for b2 in range(4):
                    pl.fence("vector")
                    ve(lambda e, s=p_half[ti][:, 1024 * b2:1024 * (b2 + 1)],
                       d=i32q.bitcast(I32)[:, 0:1024]:
                       e.tensor_scalar(d, s, qs, None, ALU.mult))
                    pl.fence("vector")
                    qv = ve(lambda e, s=i32q.bitcast(I32)[:, 0:1024],
                            d=xq2[:, 1024 * b2:1024 * (b2 + 1)]: e.tensor_copy(d, s))
                qtok = pl.mark(qv)
                pl.wait("tensor", qtok)
                xq2T = bfv(p_half[ti])[:, 4096:8192]
                for g4 in range(4):
                    xq2T_tok[ti] = transpose_to(
                        xq2T[:, g4 * 1024:g4 * 1024 + 1024],
                        [xq2[:, g4 * 1024 + 128 * k:g4 * 1024 + 128 * (k + 1)]
                         for k in range(8)], g4 % 2)
            pl.waits("sync", wq_store["wd"])
            for ti in range(8):
                pl.wait("sync", xq2T_tok[ti])
            for a in range(8):
                wdtok = pl.dma("sync", wdb_dst[a], wdb_src[a], ch="wd")
            pl.wait("tensor", wdtok)
            pl.wait("gpsimd", wcons[0])
            pl.wait("gpsimd", wcons[1])
            dlast = None
            for ti in range(8):
                tau = 8 * half + ti
                yb = ytile[ti % 2]
                if ycons[ti % 2] is not None:
                    pl.wait("gpsimd", ycons[ti % 2])
                xtok = pl.dma("gpsimd", yb,
                              src_dram.ap()[128 * tau:128 * (tau + 1), :],
                              ch=f"yx{ti % 2}")
                xq2T = bfv(p_half[ti])[:, 4096:8192]
                pl.wait("tensor", xq2T_tok[ti])
                evk = None
                for db in range(2):
                    bank_wait("tensor", "pb4")
                    mkd = None
                    for oc in range(NF):
                        mkd = mm(pb[4], xq2T[:, 128 * oc:128 * (oc + 1)],
                                 wdb_chunks[oc][:, 512 * db:512 * (db + 1)],
                                 oc == 0, oc == NF - 1)
                    dlast = mkd
                    tok_mm = pl.mark(mkd)
                    pl.wait("vector", tok_mm)
                    pl.wait("vector", xtok)
                    evk = ve(lambda e, b=pb[4][:], d=sd[:, ti:ti + 1],
                             y=yb[:, 512 * db:512 * (db + 1)]:
                             e.scalar_tensor_tensor(y, b, d, y, ALU.mult, ALU.add))
                    bank_set("pb4", pl.mark(evk))
                ytok = pl.mark(evk)
                pl.wait("gpsimd", ytok)
                stok = pl.dma("gpsimd",
                              dst_dram.ap()[128 * tau:128 * (tau + 1), :], yb,
                              ch=f"ys{ti % 2}")
                ycons[ti % 2] = stok
                out_toks.append(stok)
            wdcons = pl.mark(dlast)
            half_tok = wdcons
        al.pop()
        out_toks = out_toks[-2:] + [pl.mark((e_, pl.last_real[e_]))
                                    for e_ in ("tensor", "vector", "scalar")]
        return out_toks


    def attention_sublayer(src_dram, dst_dram, with_rel=True, pre_tok=None):
        al.push()
        att0 = al.top
        xtile_a = al.take(1024)
        htile_a = al.take(1024)
        hh_t = al.takeb(1024)
        hl_t = al.takeb(1024)
        hT0 = al.top
        hTh = al.takeb(ND * NT * 128)
        hTl = al.takeb(ND * NT * 128)
        hTh4 = hTh.rearrange("p (c t n) -> p c t n", t=NT, n=128)
        hTl4 = hTl.rearrange("p (c t n) -> p c t n", t=NT, n=128)
        nhl = [al.takeb(512) for _ in range(4)]
        rkx_h = al.takeb(512)
        rkx_l = al.takeb(512)
        rkt0 = al.top
        rkt32 = al.take(768)
        rvp_h = al.takeb(2 * 64)
        rvp_l = al.takeb(2 * 64)
        rvd_f = al.take(64)[0:1, :]
        rkm = al.take(1)[0:1, :]
        r256rep = al.take(64)
        rvcat_h = al.takeb(2)[:, 0:1]
        rvcat_l = al.takeb(2)[:, 1:2]
        rvc32 = al.take(1)
        wpb = [al.take(1024) for _ in range(2)]
        WT = {w: (al.takeb(ND * 128), al.takeb(ND * 128)) for w in ("q", "k", "v")}
        bias2 = {w: al.takeb(256)[0:2, 0:128] for w in ("q", "k", "v")}
        brow = al.take(128)[0:1, :]
        bc2 = al.takeb(2)
        bcol_s = al.take(1)
        qk0 = al.top
        Qh, Ql = al.takeb(T), al.takeb(T)
        Kh, Kl = al.takeb(T), al.takeb(T)
        Vx_h = al.takeb(NT * 160)
        Vx_l = al.takeb(NT * 160)
        Vx_h3 = Vx_h.rearrange("p (t n) -> p t n", n=160)
        Vx_l3 = Vx_l.rearrange("p (t n) -> p t n", n=160)
        Kcat = bfv(AAP[:, att0:att0 + 2048])
        Kcat3 = Kcat.rearrange("p (c n) -> p c n", n=T)
        qdup2 = [al.takeb(512) for _ in range(2)]
        sqb = al.takeb(T)
        mrow4 = al.take(4)[0:1, :]
        macc = al.take(2)
        negB2 = al.take(2)
        tb_s = al.take(2)[0:1, :]
        ring0 = al.top
        ring_h = [al.takeb(512) for _ in range(6)]
        ring_l = [al.takeb(512) for _ in range(6)]
        pf32 = [al.take(512) for _ in range(2)]
        e_hi = [al.takeb(512) for _ in range(2)]
        e_lo = [al.takeb(512) for _ in range(2)]
        rs_h = [al.takeb(384) for _ in range(4)]
        rs_l = [al.takeb(384) for _ in range(4)]
        stp_h = [al.takeb(384) for _ in range(4)]
        stp_l = [al.takeb(384) for _ in range(4)]
        psk_sl = [(al.takeb(256), al.takeb(256)) for _ in range(2)]
        psk_sl = psk_sl + [
            (bfv(AAP[:, rkt0:rkt0 + 128]), bfv(AAP[:, rkt0 + 128:rkt0 + 256])),
            (bfv(AAP[:, rkt0 + 256:rkt0 + 384]), bfv(AAP[:, rkt0 + 384:rkt0 + 512]))]
        psk_rdy = [None] * 4
        pskT_h = al.takeb(256)
        pskT_l = al.takeb(256)
        tmp4 = al.takeb(16)
        bm4 = [al.takeb(256)[0:4, 0:128] for _ in range(4)]
        cs_sb = al.take(512)[0:16, :]
        tri4p = [al.take(4), al.take(4)]
        triT_s = al.take(128)[0:4, :]
        c0r4 = al.take(128)[0:4, :]
        c0flat = al.take(512)[0:1, :]
        trflat = al.take(512)[0:1, :]
        lflat = al.take(512)[0:1, :]
        lrec = al.take(4)
        pfx_sb = al.take(512)[0:5, :]
        nsc = al.take(512)
        WoT8 = [(bfv(AAP[:, hT0 + 1024 * oc:hT0 + 1024 * oc + 512]),
                 bfv(AAP[:, hT0 + 1024 * oc + 512:hT0 + 1024 * oc + 1024]))
                for oc in range(8)]
        boext8 = al.take(8)
        aosc = [AAP[:, qk0 + 512 * i:qk0 + 512 * (i + 1)] for i in range(8)]
        ytile_a = [AAP[:, qk0 + 4096 + 1024 * i:qk0 + 4096 + 1024 * (i + 1)]
                   for i in range(2)]
        attb_h = bfv(AAP[:, ring0:ring0 + 2048])
        attb_l = bfv(AAP[:, ring0 + 2048:ring0 + 4096])
        attb_h3 = attb_h.rearrange("p (c n) -> p c n", n=512)
        attb_l3 = attb_l.rearrange("p (c n) -> p c n", n=512)

        pre_list = pre_tok if isinstance(pre_tok, list) else (
            [pre_tok] if pre_tok is not None else [])
        for tk_ in pre_list:
            for e_ in ENGS:
                pl.wait(e_, tk_)

        xcons = None
        hT_tok = None
        for tau in range(NT):
            if xcons is not None:
                pl.wait("sync", xcons)
            xtok = pl.dma("sync", xtile_a,
                          src_dram.ap()[128 * tau:128 * (tau + 1), :], ch="xa")
            pl.wait("vector", xtok)
            pl.wait("scalar", xtok)
            pl.fence("vector")
            htok = ln_tile(xtile_a, htile_a)
            xcons = htok
            pl.fence("vector")
            ve(lambda e: e.tensor_copy(hh_t, htile_a))
            pl.fence("vector")
            ve(lambda e: e.tensor_tensor(hl_t, htile_a, hh_t, ALU.subtract))
            sp_tok = pl.fence("vector")
            pl.wait("tensor", sp_tok)
            transpose_to(hTh4[:, :, tau, :],
                         [hh_t[:, 128 * dc:128 * (dc + 1)] for dc in range(ND)],
                         0, src3d=True)
            hT_tok = transpose_to(hTl4[:, :, tau, :],
                                  [hl_t[:, 128 * dc:128 * (dc + 1)] for dc in range(ND)],
                                  1, src3d=True)

        if with_rel:
            rk_sb = wpb[0]
            pl.dma("sync", rk_sb[:, 0:64], relk.ap()[0:128, :], ch="rk")
            t2 = pl.dma("sync", rk_sb[:, 64:128], relk.ap()[128:256, :], ch="rk")
            t3 = pl.dma("sync", rkt32[0:64, 256:257],
                        bass.AP(relk, 256 * HD, [[1, 64], [1, 1]]), ch="rk")
            pl.wait("vector", t3)
            pl.fence("vector")
            ve(lambda e: e.tensor_scalar(rk_sb[:, 128:256], rk_sb[:, 0:128],
                                         0.125, None, ALU.mult))
            ve(lambda e: e.tensor_scalar(rkt32[0:64, 256:257],
                                         rkt32[0:64, 256:257], 0.125, None, ALU.mult))
            sctok = pl.fence("vector")
            pl.wait("tensor", sctok)
            bank_wait("tensor", "pb0")
            pl.emit("tensor", lambda e: e.transpose(
                pb[0][0:64, 0:128], rk_sb[:, 128:192], id_f32))
            tt1 = pl.emit("tensor", lambda e: e.transpose(
                pb[0][0:64, 128:256], rk_sb[:, 192:256], id_f32))
            tt1_tok = pl.mark(tt1)
            pl.wait("vector", tt1_tok)
            ev = ve(lambda e: e.tensor_copy(rkt32[0:64, 0:256], pb[0][0:64, 0:256]))
            bank_set("pb0", pl.mark(ev))
            pl.fence("vector")
            ve(lambda e: e.tensor_tensor(rkt32[0:64, 260:517], rkt32[0:64, 0:257],
                                         rkt32[0:64, 0:257], ALU.mult))
            sqtok = pl.fence("vector")
            pl.wait("tensor", sqtok)
            bank_wait("tensor", "pb0")
            nmm = mm(pb[0][0:1, 0:257], onescol[0:64, :], rkt32[0:64, 260:517],
                     True, True)
            pl.wait("vector", pl.mark(nmm))
            rmv = ve(lambda e: e.reduce_max(rkm, pb[0][0:1, 0:257], axis=AX))
            rmvt = pl.mark(rmv)
            bank_set("pb0", rmvt)
            pl.fence("vector")
            pl.wait("scalar", rmvt)
            sq2 = sc(lambda e: e.activation(rkm, rkm, ACT.Sqrt))
            pl.wait("vector", pl.mark(sq2))
            pl.fence("vector")
            ve(lambda e: e.tensor_copy(rkx_h[0:64, 127:384], rkt32[0:64, 0:257]))
            ve(lambda e, s=rkt32[0:64, 0:1].to_broadcast((64, 127)):
               e.tensor_copy(rkx_h[0:64, 0:127], s))
            ve(lambda e, s=rkt32[0:64, 256:257].to_broadcast((64, 128)):
               e.tensor_copy(rkx_h[0:64, 384:512], s))
            pl.fence("vector")
            ve(lambda e: e.tensor_tensor(rkx_l[0:64, 127:384], rkt32[0:64, 0:257],
                                         rkx_h[0:64, 127:384], ALU.subtract))
            ve(lambda e, s=rkt32[0:64, 0:1].to_broadcast((64, 127)),
               h=rkx_h[0:64, 0:127]:
               e.tensor_tensor(rkx_l[0:64, 0:127], s, h, ALU.subtract))
            ve(lambda e, s=rkt32[0:64, 256:257].to_broadcast((64, 128)),
               h=rkx_h[0:64, 384:512]:
               e.tensor_tensor(rkx_l[0:64, 384:512], s, h, ALU.subtract))
            rkx_tok0 = pl.fence("vector")
            pl.wait("gpsimd", rkx_tok0)
            pl.dma("gpsimd", rkx_h[64:128, :], rkx_h[0:64, :], ch="rkrep")
            rep_tok = pl.dma("gpsimd", rkx_l[64:128, :], rkx_l[0:64, :], ch="rkrep")
            rkx_tok = rep_tok

            rv_sb = wpb[1]
            pl.dma("sync", rv_sb[:, 0:64], relv.ap()[0:128, :], ch="rv")
            v2 = pl.dma("sync", rv_sb[:, 64:128], relv.ap()[128:256, :], ch="rv")
            v3 = pl.dma("sync", rv_sb[0:1, 128:192], relv.ap()[256:257, :], ch="rv")
            pl.wait("tensor", v3)
            bank_wait("tensor", "pb1")
            rmm = mm(pb[1][:, 0:64], onesrow, rv_sb[0:1, 128:192], True, True)
            pl.wait("vector", pl.mark(rmm))
            rrv = ve(lambda e: e.tensor_copy(r256rep, pb[1][:, 0:64]))
            bank_set("pb1", pl.mark(rrv))
            pl.wait("vector", v2)
            pl.fence("vector")
            ve(lambda e: e.tensor_tensor(rvd_f, rv_sb[0:1, 0:64],
                                         r256rep[0:1, :], ALU.subtract))
            ve(lambda e: e.tensor_tensor(rv_sb[:, 192:256], rv_sb[:, 0:64],
                                         r256rep, ALU.subtract))
            ve(lambda e: e.tensor_tensor(rv_sb[:, 256:320], rv_sb[:, 64:128],
                                         r256rep, ALU.subtract))
            pl.fence("vector")
            ve(lambda e: e.tensor_scalar(rv_sb[0:1, 192:256], rv_sb[0:1, 192:256],
                                         0.0, None, ALU.mult))
            pl.fence("vector")
            ve(lambda e: e.tensor_copy(rvp_h[:, 0:64], rv_sb[:, 192:256]))
            ve(lambda e: e.tensor_copy(rvp_h[:, 64:128], rv_sb[:, 256:320]))
            pl.fence("vector")
            ve(lambda e: e.tensor_tensor(rvp_l[:, 0:64], rv_sb[:, 192:256],
                                         rvp_h[:, 0:64], ALU.subtract))
            ve(lambda e: e.tensor_tensor(rvp_l[:, 64:128], rv_sb[:, 256:320],
                                         rvp_h[:, 64:128], ALU.subtract))
            bank_wait("tensor", "pb1")
            mm(pb[1][0:64, 0:1], rv_sb[0:1, 128:192], onesrow[0:1, 0:1], True, True)
            cmm2 = mm(pb[1][64:128, 0:1], rv_sb[0:1, 128:192], onesrow[0:1, 0:1],
                      True, True)
            pl.wait("vector", pl.mark(cmm2))
            rcv = ve(lambda e: e.tensor_copy(rvc32, pb[1][:, 0:1]))
            bank_set("pb1", pl.mark(rcv))
            pl.fence("vector")
            ve(lambda e: e.tensor_copy(rvcat_h, rvc32))
            pl.fence("vector")
            ve(lambda e: e.tensor_tensor(rvcat_l, rvc32, rvcat_h, ALU.subtract))
            rv_tok = pl.fence("vector")
        else:
            ve(lambda e: e.tensor_scalar(rkm, onescol[0:1, :], 0.0, None, ALU.mult))
            ve(lambda e: e.tensor_scalar(rvc32, onescol, 0.0, None, ALU.mult))
            pl.fence("vector")
            ve(lambda e: e.tensor_copy(rvcat_h, rvc32))
            ve(lambda e: e.tensor_copy(rvcat_l, rvc32))
            rkx_tok = pl.fence("vector")
            rv_tok = rkx_tok

        ring_cons = [None] * 6
        plo_tok = [None] * 6
        tri_done = [None] * 4
        tilestate = [None]
        pf_cons = [None, None]
        qd_cons = [None, None]
        qd_tok = [None, None]
        rot_gate = [None]
        rotK_tok = [None]
        att_wcons = [None, None, None, None]
        att_wtoks = []
        eW_tok = [None, None]
        eR_tok = [None, None]
        psW_tok = [None, None]
        psR_tok = [None, None]
        rs_ready = [None] * 4
        rs_cons = [None] * 4
        bm_tok = [None] * 4
        stp_cons = [None] * 4
        psk_cons = [None]
        c0_tok = [None]

        bm_pend = [None] * 4

        def emit_strips(h, po, hh, blk, qts):
            def p2(j):
                pl.wait("tensor", bm_pend[j])
                if j == 0:
                    bank_wait("tensor", "pt0")
                bmt = pl.emit("tensor", lambda e, c=128 * j, s=tmp4[:, 4 * j:4 * j + 4]:
                              e.transpose(pt[0][0:4, c:c + 128], s, id_b))
                pl.wait("vector", pl.mark(bmt))
                bme = ve(lambda e, d=bm4[j], c=128 * j:
                         e.tensor_copy(d, pt[0][0:4, c:c + 128]))
                bm_tok[j] = pl.mark(bme)
                if j == 3:
                    bank_set("pt0", bm_tok[j])

            def p1(j, qt):
                es = qt % 2
                rbk = "pb2" if j % 2 == 0 else "pb0"
                rb = pb[2] if j % 2 == 0 else pb[0]
                bank_wait("tensor", rbk)
                pl.wait("tensor", rkx_tok)
                mm(rb, Qh[po:po + 64, 128 * qt:128 * qt + 128],
                   rkx_h[po:po + 64, :], True, False)
                mm(rb, Qh[po:po + 64, 128 * qt:128 * qt + 128],
                   rkx_l[po:po + 64, :], False, False)
                rmk = mm(rb, Ql[po:po + 64, 128 * qt:128 * qt + 128],
                         rkx_h[po:po + 64, :], False, True)
                rtok = pl.mark(rmk)
                pl.wait("vector", rtok)
                pl.fence("vector")
                ve(lambda e, s=rb[:, 127:384:256], d=tmp4[:, 4 * j:4 * j + 4:2]:
                   e.tensor_copy(d, s))
                pl.fence("vector")
                ve(lambda e, s=rb[:, 127:384:256],
                   hsb=tmp4[:, 4 * j:4 * j + 4:2], d=tmp4[:, 4 * j + 1:4 * j + 4:2]:
                   e.tensor_tensor(d, s, hsb, ALU.subtract))
                bm_pend[j] = pl.fence("vector")
                pl.wait("vector", eW_tok[es])
                ve(lambda e, d=e_hi[es][:, 0:511]: e.tensor_copy(
                    d, rb[:, 0:511]))
                pl.fence("vector")
                elo_v = ve(lambda e, d=e_lo[es][:, 0:511], hsb=e_hi[es][:, 0:511]:
                           e.tensor_tensor(d, rb[:, 0:511], hsb, ALU.subtract))
                ebuild = pl.mark(elo_v)
                bank_set(rbk, ebuild)
                pl.wait("gpsimd", ebuild)
                pl.wait("gpsimd", eR_tok[es])
                pl.dma("gpsimd", bass.AP(e_h, (es * 2 + 0) * 128 * 511,
                                         [[511, 128], [1, 511]]),
                       e_hi[es][:, 0:511], ch=f"ehW{es}")
                wtk = pl.dma("gpsimd", bass.AP(e_h, (es * 2 + 1) * 128 * 511,
                                               [[511, 128], [1, 511]]),
                             e_lo[es][:, 0:511], ch=f"ehW{es}")
                eW_tok[es] = wtk
                pl.wait("sync", wtk)
                pl.wait("sync", rs_cons[j])
                pl.dma("sync", rs_h[j], bass.AP(
                    e_h, (es * 2 + 0) * 128 * 511 + 127, [[510, 128], [1, 384]]),
                    ch=f"ehR{es}")
                rtk = pl.dma("sync", rs_l[j], bass.AP(
                    e_h, (es * 2 + 1) * 128 * 511 + 127, [[510, 128], [1, 384]]),
                    ch=f"ehR{es}")
                eR_tok[es] = rtk
                rs_ready[j] = rtk

            for j, qt in enumerate(qts):
                p1(j, qt)
                p2(j)

        def emit_tiles(h, po, hh, blk, par, qts, with_rel, pend, carry_out):
            cols = slice(512 * blk, 512 * (blk + 1))
            bank = pb[4 + par]
            bkey = f"pb{4 + par}"
            csb = 64 if po == 0 else 32
            bst = {"first": True, "ztok": None}

            def bank_mm(rows, lhsT, rhs, stop=False):
                st = bst["first"]
                if st:
                    pl.wait("tensor", bst["ztok"])
                bst["first"] = False
                return mm(bank[rows[0]:rows[1], :] if rows[2] is None else
                          bank[rows[0]:rows[1], rows[2]:rows[3]],
                          lhsT, rhs, st, stop, skip=True)

            def psk_part2(qt):
                j = qt - 4 * blk
                psk_h, psk_l = psk_sl[j]
                pl.wait("tensor", psk_rdy[j])
                bank_wait("tensor", "pt0")
                pl.emit("tensor", lambda e, a=psk_h: e.transpose(
                    pt[0][:, 0:128], a[:, 0:128], id_b))
                pl.emit("tensor", lambda e, a=psk_h: e.transpose(
                    pt[0][:, 128:256], a[:, 128:256], id_b))
                pl.emit("tensor", lambda e, a=psk_l: e.transpose(
                    pt[0][:, 256:384], a[:, 0:128], id_b))
                ptk = pl.emit("tensor", lambda e, a=psk_l: e.transpose(
                    pt[0][:, 384:512], a[:, 128:256], id_b))
                pl.wait("vector", pl.mark(ptk))
                pl.fence("vector")
                ve(lambda e: e.tensor_copy(pskT_h, pt[0][:, 0:256]))
                pve = ve(lambda e: e.tensor_copy(pskT_l, pt[0][:, 256:512]))
                pskt = pl.mark(pve)
                bank_set("pt0", pskt)
                psk_cons[0] = pskt
                pl.wait("tensor", pskt)
                pl.wait("tensor", rv_tok)
                for ck in range(2):
                    bank_mm((po, po + 64, 128 * j, 128 * j + 128),
                            rvp_h[:, 64 * ck:64 * ck + 64],
                            pskT_h[:, 128 * ck:128 * ck + 128])
                    bank_mm((po, po + 64, 128 * j, 128 * j + 128),
                            rvp_l[:, 64 * ck:64 * ck + 64],
                            pskT_h[:, 128 * ck:128 * ck + 128])
                    bank_mm((po, po + 64, 128 * j, 128 * j + 128),
                            rvp_h[:, 64 * ck:64 * ck + 64],
                            pskT_l[:, 128 * ck:128 * ck + 128])

            bst["psk2"] = psk_part2
            tilestate[0] = bst
            base = 4 * blk
            far = [(base + 5 + k) % NT for k in range(10)]
            diag = [(base + 15 + k) % NT for k in range(6)]
            order = far[0:3] + diag + far[3:10]
            done_tiles = set()
            strips_emitted = set()
            strip_ready = {}
            psk_pend = {}
            slot_of = {}
            sbanks = (0, 1, 3)
            vlo, vhi = (0, 80) if hh == 0 else (32, 160)
            ro, rh = (0, 80) if hh == 0 else (0, 128)
            for oi in range(NT):
                tt = order[oi]
                if pend:
                    for sl_, fn_ in pend:
                        if sl_ == oi:
                            fn_()
                sb_ = pb[sbanks[oi % 3]]
                sbk = f"pb{sbanks[oi % 3]}"
                bank_wait("tensor", sbk)
                if oi == 0:
                    bank_wait("tensor", bkey)
                    bank_wait("vector", bkey)
                    pl.fence("vector")
                    zb = ve(lambda e: e.tensor_scalar(
                        bank[:], Qh[:, 0:512], 0.0, None, ALU.mult))
                    bst["ztok"] = pl.mark(zb)
                if oi == 0:
                    pl.wait("tensor", qd_tok[blk % 2])
                    pl.wait("tensor", rotK_tok[0])
                mm(sb_, Kcat3[:, hh, 128 * tt:128 * tt + 128],
                   qdup2[blk % 2], True, False)
                if with_rel:
                    for j, qt in enumerate(qts):
                        dqt = tt - qt
                        if -1 <= dqt <= 1:
                            pl.wait("tensor", rs_ready[j])
                            co = 128 * (dqt + 1)
                            mm(sb_[:, 128 * j:128 * j + 128],
                               rs_h[j][:, co:co + 128], id_b, False, False)
                            rmm2 = mm(sb_[:, 128 * j:128 * j + 128],
                                      rs_l[j][:, co:co + 128], id_b, False, False)
                            rs_cons[j] = pl.mark(rmm2)
                        elif dqt < -1:
                            pl.wait("tensor", bm_tok[j])
                            mm(sb_[:, 128 * j:128 * j + 128],
                               sel4[0:4, 0:128], bm4[j], False, False)
                        else:
                            pl.wait("tensor", bm_tok[j])
                            mm(sb_[:, 128 * j:128 * j + 128],
                               sel4[0:4, 128:256], bm4[j], False, False)
                lmk = mm(sb_, Kl[po:po + 64, 128 * tt:128 * tt + 128],
                         Qh[po:po + 64, cols], False, True)
                smk = pl.mark(lmk)
                s6 = oi % 6
                slot_of[tt] = s6
                pl.wait("scalar", smk)
                pl.wait("scalar", pf_cons[oi % 2])
                petok = pl.mark(sc(lambda e, d=pf32[oi % 2], b=sb_[:],
                                   c=negB2[:, hh:hh + 1]:
                                   e.activation(d, b, ACT.Exp, bias=c)))
                bank_set(sbk, petok)
                pl.wait("scalar", ring_cons[s6])
                pl.fence("scalar")
                phi_t = pl.mark(sc(lambda e, d=ring_h[s6], p=pf32[oi % 2]:
                                   e.activation(d, p, ACT.Copy)))
                pl.wait("gpsimd", phi_t)
                pl.fence("gpsimd")
                pltok = pl.mark(pl.emit("gpsimd",
                                lambda e, d=ring_l[s6], p=pf32[oi % 2],
                                hsb=ring_h[s6]:
                                e.tensor_tensor(d, p, hsb, ALU.subtract)))
                plo_tok[s6] = pltok
                pf_cons[oi % 2] = pltok
                lagged = []
                if oi >= 5:
                    lagged.append(order[oi - 5])
                if oi == NT - 1 and not carry_out:
                    lagged += [order[NT - 5 + k3] for k3 in range(5)]
                for lt in lagged:
                    l6 = slot_of[lt]
                    pl.wait("tensor", plo_tok[l6])
                    bank_mm((ro, rh, None, None),
                            Vx_h3[:, lt, vlo:vhi], ring_h[l6])
                    bank_mm((ro, rh, None, None),
                            Vx_h3[:, lt, vlo:vhi], ring_l[l6])
                    pvk = bank_mm((ro, rh, None, None),
                                  Vx_l3[:, lt, vlo:vhi], ring_h[l6],
                                  stop=(oi == NT - 1 and lt == lagged[-1]
                                        and not with_rel and not carry_out))
                    ring_cons[l6] = pl.mark(pvk)
                done_tiles.add(tt)
                if with_rel:
                    for qt in qts:
                        if qt in strip_ready or qt in strips_emitted:
                            continue
                        need = {x for x in (qt - 1, qt, qt + 1) if 0 <= x < NT}
                        if need <= done_tiles:
                            strip_ready[qt] = oi
                    strip_qts = [qt for qt, ro in sorted(strip_ready.items())
                                 if ro + 3 <= oi or oi == NT - 1]
                    for qt in strip_qts:
                        strips_emitted.add(qt)
                        del strip_ready[qt]
                        j = qt - 4 * blk
                        jj = j
                        pl.wait("vector", stp_cons[j])
                        if 0 < qt < NT - 1:
                            bank_wait("tensor", "pt1")
                            tl = None
                            for kk, st_t in enumerate((qt - 1, qt, qt + 1)):
                                s6b = slot_of[st_t]
                                pl.wait("tensor", plo_tok[s6b])
                                pl.emit("tensor", lambda e, c=256 * kk,
                                        s=ring_h[s6b][:, 128 * jj:128 * jj + 128]:
                                        e.transpose(pt[1][:, c:c + 128], s, id_b))
                                tl = pl.emit("tensor", lambda e, c=256 * kk + 128,
                                             s=ring_l[s6b][:, 128 * jj:128 * jj + 128]:
                                             e.transpose(pt[1][:, c:c + 128], s, id_b))
                            pl.wait("vector", pl.mark(tl))
                            sev = None
                            for kk in range(3):
                                ve(lambda e, d=stp_h[j][:, 128 * kk:128 * kk + 128],
                                   c=256 * kk: e.tensor_copy(d, pt[1][:, c:c + 128]))
                                sev = ve(lambda e, d=stp_l[j][:, 128 * kk:128 * kk + 128],
                                         c=256 * kk + 128:
                                         e.tensor_copy(d, pt[1][:, c:c + 128]))
                            bank_set("pt1", pl.mark(sev))
                        else:
                            for kk, st_t in enumerate((qt - 1, qt, qt + 1)):
                                if 0 <= st_t < NT:
                                    s6b = slot_of[st_t]
                                    bank_wait("tensor", "pt1")
                                    pl.wait("tensor", plo_tok[s6b])
                                    pl.emit("tensor", lambda e, s=ring_h[s6b][:, 128 * jj:128 * jj + 128]:
                                            e.transpose(pt[1][:, 0:128], s, id_b))
                                    tl = pl.emit("tensor", lambda e, s=ring_l[s6b][:, 128 * jj:128 * jj + 128]:
                                                 e.transpose(pt[1][:, 128:256], s, id_b))
                                    pl.wait("vector", pl.mark(tl))
                                    ve(lambda e, d=stp_h[j][:, 128 * kk:128 * kk + 128]:
                                       e.tensor_copy(d, pt[1][:, 0:128]))
                                    sev = ve(lambda e, d=stp_l[j][:, 128 * kk:128 * kk + 128]:
                                             e.tensor_copy(d, pt[1][:, 128:256]))
                                    bank_set("pt1", pl.mark(sev))
                                else:
                                    pl.fence("vector")
                                    ve(lambda e, d=stp_h[j][:, 128 * kk:128 * kk + 128]:
                                       e.tensor_scalar(d, id_b, 0.0, None, ALU.mult))
                                    sev = ve(lambda e, d=stp_l[j][:, 128 * kk:128 * kk + 128]:
                                             e.tensor_scalar(d, id_b, 0.0, None, ALU.mult))
                                    pl.mark(sev)
                        stp_cons[j] = pl.fence("vector")
                        t4 = tri4p[par]
                        tcs = 128 + 8 * par
                        if qt == 0:
                            ve(lambda e, d=t4[:, 0:1]: e.tensor_scalar(
                                d, onescol, 0.0, None, ALU.mult))
                        else:
                            ve(lambda e, d=t4[:, j:j + 1], s=stp_h[j][:, 0:128]:
                               e.scalar_tensor_tensor(nsc[:, 0:64].bitcast(BF16),
                                                      s, 1.0, lt_b, ALU.mult,
                                                      ALU.mult, accum_out=d))
                            ve(lambda e, d=nsc[:, tcs + j:tcs + j + 1],
                               s=stp_l[j][:, 0:128]:
                               e.scalar_tensor_tensor(nsc[:, 64:128].bitcast(BF16),
                                                      s, 1.0, lt_b, ALU.mult,
                                                      ALU.mult, accum_out=d))
                        pl.fence("vector")
                        if qt != 0:
                            ve(lambda e, d=t4[:, j:j + 1], a=nsc[:, tcs + j:tcs + j + 1]:
                               e.tensor_tensor(d, d, a, ALU.add))
                        tri_done[j] = pl.fence("vector")
                        ps = j % 2
                        pl.wait("gpsimd", stp_cons[j])
                        pl.wait("gpsimd", psR_tok[ps])
                        pl.dma("gpsimd",
                               bass.AP(s_h, (ps * 2 + 0) * 128 * 384,
                                       [[384, 128], [1, 384]]),
                               stp_h[j], ch=f"psW{ps}")
                        pwt = pl.dma("gpsimd",
                                     bass.AP(s_h, (ps * 2 + 1) * 128 * 384,
                                             [[384, 128], [1, 384]]),
                                     stp_l[j], ch=f"psW{ps}")
                        psW_tok[ps] = pwt
                        stp_cons[j] = pwt
                        pl.wait("sync", pwt)
                        pl.wait("sync", psk_cons[0])
                        pl.dma("sync", psk_sl[j][0], bass.AP(
                            s_h, (ps * 2 + 0) * 128 * 384,
                            [[385, 128], [1, 256]]), ch=f"psR{ps}")
                        prt = pl.dma("sync", psk_sl[j][1], bass.AP(
                            s_h, (ps * 2 + 1) * 128 * 384,
                            [[385, 128], [1, 256]]), ch=f"psR{ps}")
                        psR_tok[ps] = prt
                        psk_rdy[j] = prt
                        psk_pend[qt] = oi
                if with_rel:
                    for qt in list(psk_pend):
                        if psk_pend[qt] + 6 <= oi:
                            bst["psk2"](qt)
                            del psk_pend[qt]
            if pend:
                for sl_, fn_ in pend:
                    if sl_ >= NT:
                        fn_()
            carries = []
            if carry_out:
                for k2 in range(5):
                    lt_c = order[NT - 5 + k2]
                    l6_c = slot_of[lt_c]
                    tok_c = plo_tok[l6_c]

                    def cfn(lt=lt_c, l6=l6_c, tok=tok_c, last=(k2 == 0)):
                        pl.wait("tensor", tok)
                        bank_mm((ro, rh, None, None),
                                Vx_h3[:, lt, vlo:vhi], ring_h[l6])
                        bank_mm((ro, rh, None, None),
                                Vx_h3[:, lt, vlo:vhi], ring_l[l6])
                        pvk2 = bank_mm((ro, rh, None, None),
                                       Vx_l3[:, lt, vlo:vhi], ring_h[l6],
                                       stop=(last and not with_rel))
                        ring_cons[l6] = pl.mark(pvk2)
                    carries.append(((NT - 5 + k2) % 6, cfn))
                if with_rel:
                    for ip, qt in enumerate(sorted(psk_pend)):
                        def pfn(qt=qt):
                            bst["psk2"](qt)
                        carries.append(((1 + ip) % 4, pfn))
                    psk_pend.clear()
            elif with_rel:
                for qt in sorted(psk_pend):
                    bst["psk2"](qt)
                psk_pend.clear()
            return carries

        def make_post(h, po, hh, blk, par, qts, with_rel, rsnap, bst):
            bank = pb[4 + par]
            bkey = f"pb{4 + par}"
            csb = 64 if po == 0 else 32

            S = {}

            def st0():
                pl.wait("vector", rsnap[(NT - 1) % 6])
                pl.wait("vector", rsnap[(NT - 5) % 6])
                pl.fence("vector")
                csv = ve(lambda e: e.tensor_copy(cs_sb, bank[csb:csb + 16, :]))
                cst = pl.mark(csv)
                pl.wait("tensor", cst)
                bank_wait("tensor", "pb2")
                pfm = mm(pb[2][0:5, :], pre32_b[:, 5 * blk:5 * blk + 5], cs_sb,
                         True, True)
                pl.wait("vector", pl.mark(pfm))
                pfe = ve(lambda e: e.tensor_copy(pfx_sb, pb[2][0:5, :]))
                S["pfe"] = pl.mark(pfe)
                bank_set("pb2", S["pfe"])
                pl.wait("gpsimd", S["pfe"])
                ldm = pl.dma("gpsimd", lrec, pfx_sb[4:5, :], ch="c0")
                pl.wait("vector", ldm)
                pl.fence("vector")
                ve(lambda e: e.reciprocal(lrec, lrec))
                rv_t = pl.fence("vector")
                pl.wait("gpsimd", rv_t)
                S["lr"] = pl.dma("gpsimd", lflat, lrec, ch="c0")

            def st1():
                pfe_tok = S["pfe"]
                if with_rel:
                    for j in range(4):
                        pl.wait("tensor", tri_done[j])
                    bank_wait("tensor", "pb2")
                    trt = pl.emit("tensor", lambda e: e.transpose(
                        pb[2][0:4, 0:128], tri4p[par], id_f32))
                    pl.wait("vector", pl.mark(trt))
                    pl.fence("vector")
                    tre = ve(lambda e: e.tensor_copy(triT_s, pb[2][0:4, 0:128]))
                    tre_t = pl.mark(tre)
                    bank_set("pb2", tre_t)
                    pl.wait("gpsimd", tre_t)
                    pl.wait("gpsimd", pfe_tok)
                    pl.wait("gpsimd", c0_tok[0])
                    pl.dma("gpsimd", c0flat,
                           bass.AP(arena, pfx_sb.offset, [[ACOLS + 128, 4], [1, 128]]),
                           ch="c0")
                    c0d = pl.dma("gpsimd", trflat,
                                 bass.AP(arena, triT_s.offset, [[ACOLS, 4], [1, 128]]),
                                 ch="c0")
                    pl.wait("vector", c0d)
                    pl.fence("vector")
                    ve(lambda e: e.tensor_tensor(c0flat, c0flat, trflat, ALU.add))
                    c0t = pl.fence("vector")
                    c0_tok[0] = c0t

            def st2():
                if with_rel:
                    pl.wait("tensor", c0_tok[0])
                    for j in range(4):
                        mm(bank[po:po + 64, 128 * j:128 * j + 128],
                           rvd_f, c0flat[0:1, 128 * j:128 * j + 128],
                           False, j == 3, skip=True)
                pl.wait("tensor", S["lr"])
                bank_wait("tensor", "pb2")
                lrm = mm(pb[2], onesrow, lflat, True, True)
                S["lrt"] = pl.mark(lrm)

            def st3():
                pl.wait("vector", S["lrt"])
                pl.wait("vector", rsnap[(NT - 1) % 6])
                pl.wait("vector", rsnap[(NT - 5) % 6])
                pl.fence("vector")
                ve(lambda e, d=nsc[po:po + 64, :]: e.tensor_copy(
                    d, bank[po:po + 64, :]))
                pl.fence("vector")
                nv = ve(lambda e, d=nsc[po:po + 64, :]: e.tensor_tensor(
                    d, d, pb[2][po:po + 64, :], ALU.mult))
                ntok = pl.mark(nv)
                bank_set("pb2", ntok)
                bank_set(bkey, ntok)
                ns2 = (blk % 2) * 2
                pl.wait("vector", att_wcons[ns2])
                pl.fence("vector")
                ve(lambda e, d=nhl[ns2][po:po + 64, :], s=nsc[po:po + 64, :]:
                   e.tensor_copy(d, s))
                pl.fence("vector")
                ve(lambda e, d=nhl[ns2 + 1][po:po + 64, :],
                   s=nsc[po:po + 64, :], hb=nhl[ns2][po:po + 64, :]:
                   e.tensor_tensor(d, s, hb, ALU.subtract))
                awt = pl.fence("vector")
                pl.wait("gpsimd", awt)
                row = (h // 2) * 128 + 64 * (h % 2)
                pl.dma("gpsimd", bass.AP(att_hbm, row * T + 512 * blk,
                                         [[T, 64], [1, 512]]),
                       nhl[ns2][po:po + 64, :], ch=f"atw{ns2}")
                att_wcons[ns2] = pl.dma(
                    "gpsimd", bass.AP(att_hbm, (ND * 128 + row) * T + 512 * blk,
                                      [[T, 64], [1, 512]]),
                    nhl[ns2 + 1][po:po + 64, :], ch=f"atw{ns2}")
                att_wtoks.append(att_wcons[ns2])
            return [st0, st1, st2, st3]

        par_ctr = [0]

        def attn_head(h, po, hh, with_rel, pending):
            for blk in range(4):
                qts = [4 * blk + j for j in range(4)]
                par = par_ctr[0] % 2
                par_ctr[0] += 1
                b2 = blk % 2
                qsrc = Qh[64 * hh:64 * hh + 64, 512 * blk:512 * (blk + 1)]
                pl.wait("gpsimd", qd_cons[b2])
                pl.wait("gpsimd", rot_gate[0])
                pl.dma("gpsimd", qdup2[b2][0:64, :], qsrc, ch="rot0")
                qd_tok[b2] = pl.dma("gpsimd", qdup2[b2][64:128, :], qsrc,
                                    ch="rot0")
                if with_rel:
                    emit_strips(h, po, hh, blk, qts)
                cross = (hh == 1 and blk == 3)
                carries = emit_tiles(h, po, hh, blk, par, qts, with_rel,
                                     pending, not cross)
                qd_cons[b2] = pl.mark(("tensor", pl.last_real["tensor"]))
                stages = make_post(h, po, hh, blk, par, qts, with_rel,
                                   ring_cons, tilestate[0])
                pending = carries + [(6, stages[0]), (8, stages[1]),
                                     (11, stages[2]), (13, stages[3])]
            return pending

        wo_out = []

        bo_cons = [None]

        def wo_phase():
            barr = [psW_tok[0], psW_tok[1], psR_tok[0], psR_tok[1],
                    eW_tok[0], eW_tok[1], eR_tok[0], eR_tok[1],
                    rotK_tok[0], qd_tok[0], qd_tok[1]] +                    [ring_cons[i] for i in range(6)] +                    [pl.mark((e_, pl.last_real[e_]))
                    for e_ in ("tensor", "vector", "scalar")]
            for t_ in barr:
                for e_ in ("sync", "gpsimd", "tensor", "vector", "scalar"):
                    pl.wait(e_, t_)
            wp2 = [None, None]
            wrv_toks = []
            for oc in range(8):
                bslot = oc % 2
                if wp2[bslot] is not None:
                    pl.wait("sync", wp2[bslot])
                wtok = pl.dma("sync", wpb[bslot],
                              W["wo"].ap()[128 * oc:128 * (oc + 1), :], ch=f"wp{bslot}")
                pl.wait("vector", wtok)
                pl.fence("vector")
                ve(lambda e, b=wpb[bslot]: e.tensor_copy(hh_t, b))
                pl.fence("vector")
                spv = ve(lambda e, b=wpb[bslot]: e.tensor_tensor(
                    hl_t, b, hh_t, ALU.subtract))
                wp2[bslot] = pl.mark(spv)
                sp_tok = pl.fence("vector")
                pl.wait("tensor", sp_tok)
                transpose_to(WoT8[oc][0].rearrange("p (c n) -> p c n", n=128),
                             [hh_t[:, 128 * dc:128 * (dc + 1)] for dc in range(ND)],
                             0, src3d=True)
                wotok = transpose_to(WoT8[oc][1].rearrange("p (c n) -> p c n", n=128),
                                     [hl_t[:, 128 * dc:128 * (dc + 1)] for dc in range(ND)],
                                     1, src3d=True)
                WoTh3 = WoT8[oc][0].rearrange("p (c n) -> p c n", n=128)
                WoTl3 = WoT8[oc][1].rearrange("p (c n) -> p c n", n=128)
                pl.wait("tensor", wotok)
                pl.wait("tensor", rv_tok)
                bank_wait("tensor", "pb5")
                for dc in range(ND):
                    mm(pb[5][:, 0:1], WoTh3[:, dc], rvcat_h, dc == 0, False)
                    mm(pb[5][:, 0:1], WoTh3[:, dc], rvcat_l, False, False)
                    wrm = mm(pb[5][:, 0:1], WoTl3[:, dc], rvcat_h, False,
                             dc == ND - 1)
                pl.wait("sync", bo_cons[0])
                bctok = pl.dma("sync", nsc[:, 0:1],
                               bass.AP(Bi["bo"], 128 * oc, [[1, 128], [1, 1]]),
                               ch="boc")
                pl.wait("vector", pl.mark(wrm))
                pl.wait("vector", bctok)
                pl.fence("vector")
                bev = ve(lambda e, d=boext8[:, oc:oc + 1]: e.tensor_tensor(
                    d, pb[5][:, 0:1], nsc[:, 0:1], ALU.add))
                bo_cons[0] = pl.mark(bev)
                bank_set("pb5", bo_cons[0])
            ycons_a = [None, None]
            attb_cons = None
            for blk in range(4):
                cols = slice(512 * blk, 512 * (blk + 1))
                if attb_cons is not None:
                    pl.wait("sync", attb_cons)
                for t_ in att_wtoks[-4:]:
                    pl.wait("sync", t_)
                pl.dma("sync", attb_h3, bass.AP(
                    att_hbm, 512 * blk, [[T, 128], [128 * T, ND], [1, 512]]),
                    ch="atr")
                abtok = pl.dma("sync", attb_l3, bass.AP(
                    att_hbm, ND * 128 * T + 512 * blk,
                    [[T, 128], [128 * T, ND], [1, 512]]), ch="atr")
                pl.wait("tensor", abtok)
                ao_toks = []
                for oc in range(8):
                    WoTh3 = WoT8[oc][0].rearrange("p (c n) -> p c n", n=128)
                    WoTl3 = WoT8[oc][1].rearrange("p (c n) -> p c n", n=128)
                    bk_ = oc % 2
                    bank_wait("tensor", f"pb{bk_}")
                    for dc in range(ND):
                        mm(pb[bk_], WoTh3[:, dc], attb_h3[:, dc], dc == 0, False)
                        mm(pb[bk_], WoTh3[:, dc], attb_l3[:, dc], False, False)
                        prm = mm(pb[bk_], WoTl3[:, dc], attb_h3[:, dc],
                                 False, dc == ND - 1)
                    prt = pl.mark(prm)
                    attb_cons = prt
                    pl.wait("vector", prt)
                    pl.fence("vector")
                    aov = ve(lambda e, d=aosc[oc], b=pb[bk_][:], c=boext8[:, oc:oc + 1]:
                             e.tensor_scalar(d, b, c, None, ALU.add))
                    bank_set(f"pb{bk_}", pl.mark(aov))
                    ao_toks.append(pl.mark(aov))
                for jt in range(4):
                    tau = 4 * blk + jt
                    yb = ytile_a[jt % 2]
                    if ycons_a[jt % 2] is not None:
                        pl.wait("gpsimd", ycons_a[jt % 2])
                    xt2 = pl.dma("gpsimd", yb,
                                 src_dram.ap()[128 * tau:128 * (tau + 1), :],
                                 ch=f"ya{jt % 2}")
                    for oc in range(8):
                        pl.wait("tensor", ao_toks[oc])
                        bank_wait("tensor", f"pb{2 + oc % 2}")
                        trm = pl.emit("tensor", lambda e, s=aosc[oc][:, 128 * jt:128 * jt + 128],
                                      b=pb[2 + oc % 2][:, 0:128]:
                                      e.transpose(b, s, id_f32))
                        trt2 = pl.mark(trm)
                        pl.wait("vector", trt2)
                        pl.wait("vector", xt2)
                        pl.fence("vector")
                        yv = ve(lambda e, d=yb[:, 128 * oc:128 * oc + 128],
                                b=pb[2 + oc % 2][:, 0:128]:
                                e.tensor_tensor(d, d, b, ALU.add))
                        bank_set(f"pb{2 + oc % 2}", pl.mark(yv))
                    ytk = pl.fence("vector")
                    pl.wait("gpsimd", ytk)
                    stk = pl.dma("gpsimd", dst_dram.ap()[128 * tau:128 * (tau + 1), :],
                                 yb, ch=f"ys{jt % 2}")
                    ycons_a[jt % 2] = stk
                    wo_out.append(stk)

        pending_post = [None]
        if with_rel:
            pl.wait("scalar", rv_tok)
            pl.wait("scalar", rkx_tok0)
            wp_cons = [tt1_tok, rv_tok]
        else:
            wp_cons = [None, None]
        brow_cons = [None]
        for hp in range(8):
            po2 = 128 * hp
            for iw, (w, wt_d, bi_d, kscale) in enumerate([
                    ("q", W["wq"], Bi["bq"], None),
                    ("k", W["wk"], Bi["bk"], 0.125),
                    ("v", W["wv"], Bi["bv"], None)]):
                bslot = iw % 2
                if wp_cons[bslot] is not None:
                    pl.wait("sync", wp_cons[bslot])
                wtok = pl.dma("sync", wpb[bslot],
                              wt_d.ap()[po2:po2 + 128, :], ch=f"wp{bslot}")
                pl.wait("sync", brow_cons[0])
                btk = pl.dma("sync", brow, bass.AP(bi_d, po2, [[1, 1], [1, 128]]),
                             ch="br")
                pl.wait("vector", wtok)
                pl.wait("vector", btk)
                pl.fence("vector")
                if kscale is not None:
                    ve(lambda e, b=wpb[bslot], s=kscale: e.tensor_scalar(
                        b, b, s, None, ALU.mult))
                    ve(lambda e, s=kscale: e.tensor_scalar(brow, brow, s, None, ALU.mult))
                    pl.fence("vector")
                brt = pl.fence("vector")
                pl.wait("tensor", brt)
                pl.wait("tensor", btk)
                bank_wait("tensor", "pb2")
                btr = pl.emit("tensor", lambda e: e.transpose(
                    pb[2][:, 0:1], brow, onesrow[0:1, 0:1]))
                brow_cons[0] = pl.mark(btr)
                pl.wait("vector", brow_cons[0])
                ve(lambda e: e.tensor_copy(bcol_s, pb[2][:, 0:1]))
                bank_set("pb2", pl.fence("vector") or None)
                pl.fence("vector")
                ve(lambda e, d=bc2[:, 0:1]: e.tensor_copy(d, bcol_s))
                pl.fence("vector")
                ve(lambda e, d=bc2[:, 1:2], hb=bc2[:, 0:1]:
                   e.tensor_tensor(d, bcol_s, hb, ALU.subtract))
                bct = pl.fence("vector")
                pl.wait("tensor", bct)
                bank_wait("tensor", "pt0")
                bt2 = pl.emit("tensor", lambda e: e.transpose(
                    pt[0][0:2, 0:128], bc2, id_b))
                pl.wait("vector", pl.mark(bt2))
                b2v = ve(lambda e, d=bias2[w]: e.tensor_copy(d, pt[0][0:2, 0:128]))
                bank_set("pt0", pl.mark(b2v))
                ve(lambda e, b=wpb[bslot]: e.tensor_copy(hh_t, b))
                pl.fence("vector")
                spv = ve(lambda e, b=wpb[bslot]: e.tensor_tensor(
                    hl_t, b, hh_t, ALU.subtract))
                wp_cons[bslot] = pl.mark(spv)
                sp_tok = pl.fence("vector")
                pl.wait("tensor", sp_tok)
                transpose_to(WT[w][0].rearrange("p (c n) -> p c n", n=128),
                             [hh_t[:, 128 * dc:128 * (dc + 1)] for dc in range(ND)],
                             0, src3d=True)
                transpose_to(WT[w][1].rearrange("p (c n) -> p c n", n=128),
                             [hl_t[:, 128 * dc:128 * (dc + 1)] for dc in range(ND)],
                             1, src3d=True)
            WTq3 = [WT["q"][k].rearrange("p (c n) -> p c n", n=128) for k in (0, 1)]
            WTk3 = [WT["k"][k].rearrange("p (c n) -> p c n", n=128) for k in (0, 1)]
            WTv3 = [WT["v"][k].rearrange("p (c n) -> p c n", n=128) for k in (0, 1)]
            pl.wait("tensor", hT_tok)
            for blk in range(4):
                cols = slice(512 * blk, 512 * (blk + 1))
                for w, WT3, Dh, Dl in (("q", WTq3, Qh, Ql), ("k", WTk3, Kh, Kl)):
                    bk_ = blk % 2
                    bank_wait("tensor", f"pb{bk_}")
                    for dc in range(ND):
                        rhsh = hTh4[:, dc, 4 * blk:4 * blk + 4, :]
                        rhsl = hTl4[:, dc, 4 * blk:4 * blk + 4, :]
                        mm(pb[bk_], WT3[0][:, dc], rhsh, dc == 0, False)
                        mm(pb[bk_], WT3[0][:, dc], rhsl, False, False)
                        mm(pb[bk_], WT3[1][:, dc], rhsh, False, False)
                    mmk = mm(pb[bk_], bias2[w], ones2row, False, True)
                    tok_mm = pl.mark(mmk)
                    pl.wait("vector", tok_mm)
                    pl.fence("vector")
                    ve(lambda e, b=pb[bk_][:], d=Dh[:, cols]: e.tensor_copy(d, b))
                    pl.fence("vector")
                    evv = ve(lambda e, b=pb[bk_][:], hb=Dh[:, cols], d=Dl[:, cols]:
                             e.tensor_tensor(d, b, hb, ALU.subtract))
                    bank_set(f"pb{bk_}", pl.mark(evv))
            for tau in range(NT):
                bk_ = 2 + tau % 2
                bank_wait("tensor", f"pb{bk_}")
                for dc in range(ND):
                    lh = hTh4[:, dc, tau, :]
                    ll = hTl4[:, dc, tau, :]
                    mm(pb[bk_][:, 0:128], lh, WTv3[0][:, dc], dc == 0, False)
                    mm(pb[bk_][:, 0:128], lh, WTv3[1][:, dc], False, False)
                    mm(pb[bk_][:, 0:128], ll, WTv3[0][:, dc], False, False)
                mmk = mm(pb[bk_][:, 0:128], ones2row[:, 0:128], bias2["v"],
                         False, True)
                tok_mm = pl.mark(mmk)
                pl.wait("vector", tok_mm)
                pl.fence("vector")
                ve(lambda e, b=pb[bk_][:, 0:64],
                   d=Vx_h3[:, tau, 0:64]: e.tensor_copy(d, b))
                ve(lambda e, b=pb[bk_][:, 64:128],
                   d=Vx_h3[:, tau, 96:160]: e.tensor_copy(d, b))
                pl.fence("vector")
                ve(lambda e, b=pb[bk_][:, 0:64], hb=Vx_h3[:, tau, 0:64],
                   d=Vx_l3[:, tau, 0:64]:
                   e.tensor_tensor(d, b, hb, ALU.subtract))
                evv = ve(lambda e, b=pb[bk_][:, 64:128], hb=Vx_h3[:, tau, 96:160],
                         d=Vx_l3[:, tau, 96:160]:
                         e.tensor_tensor(d, b, hb, ALU.subtract))
                bank_set(f"pb{bk_}", pl.mark(evv))
            ind3 = ind_b.rearrange("p (t n) -> p t n", n=16)
            pl.fence("vector")
            ve(lambda e, d=Vx_h3[:, :, 64:80]: e.tensor_copy(d, ind3))
            ve(lambda e, d=Vx_h3[:, :, 80:96]: e.tensor_scalar(
                d, ind3, 0.0, None, ALU.mult))
            ve(lambda e, d=Vx_l3[:, :, 64:80]: e.tensor_scalar(
                d, ind3, 0.0, None, ALU.mult))
            ve(lambda e, d=Vx_l3[:, :, 80:96]: e.tensor_scalar(
                d, ind3, 0.0, None, ALU.mult))
            for src_b, col0 in ((Qh, 0), (Kh, 2)):
                pl.fence("vector")
                ve(lambda e, s=src_b: e.tensor_tensor(sqb, s, s, ALU.mult))
                sq_tok = pl.fence("vector")
                pl.wait("tensor", sq_tok)
                for blkn in range(4):
                    bank_wait("tensor", "pb2")
                    nm2 = mm(pb[2][0:2, 0:512], hm_b,
                             sqb[:, 512 * blkn:512 * (blkn + 1)], True, True)
                    pl.wait("vector", pl.mark(nm2))
                    pl.fence("vector")
                    rv2 = ve(lambda e: e.reduce_max(macc[0:2, 1:2],
                                                    pb[2][0:2, 0:512], axis=AX))
                    bank_set("pb2", pl.mark(rv2))
                    pl.fence("vector")
                    if blkn == 0:
                        ve(lambda e: e.tensor_copy(macc[0:2, 0:1], macc[0:2, 1:2]))
                    else:
                        ve(lambda e: e.tensor_tensor(macc[0:2, 0:1], macc[0:2, 0:1],
                                                     macc[0:2, 1:2], ALU.max))
                tp_tok = pl.fence("vector")
                pl.wait("tensor", tp_tok)
                bank_wait("tensor", "pb2")
                tmm = mm(pb[2][0:1, col0:col0 + 2], macc[0:2, 0:1], id2_f, True, True)
                pl.wait("vector", pl.mark(tmm))
                mvv = ve(lambda e, c=col0: e.tensor_copy(mrow4[:, c:c + 2],
                                                         pb[2][0:1, c:c + 2]))
                bank_set("pb2", pl.mark(mvv))
                mvt = pl.fence("vector")
            pl.wait("scalar", mvt)
            sq3 = sc(lambda e: e.activation(mrow4, mrow4, ACT.Sqrt))
            pl.wait("vector", pl.mark(sq3))
            pl.fence("vector")
            ve(lambda e: e.tensor_scalar(tb_s, mrow4[:, 2:4], rkm, None, ALU.add))
            pl.fence("vector")
            nbv = ve(lambda e: e.scalar_tensor_tensor(
                tb_s, tb_s, -1.05, mrow4[:, 0:2], ALU.mult, ALU.mult))
            nb_tok = pl.fence("vector")
            pl.wait("tensor", nb_tok)
            bank_wait("tensor", "pb2")
            nbm = mm(pb[2][:, 0:2], onesrow, tb_s, True, True)
            pl.wait("vector", pl.mark(nbm))
            nbe = ve(lambda e: e.tensor_copy(negB2, pb[2][:, 0:2]))
            bank_set("pb2", pl.mark(nbe))
            nbe_t = pl.fence("vector")
            rot_gate[0] = nbe_t
            pl.wait("sync", nbe_t)
            for hh2 in range(2):
                pl.dma("sync", Kcat3[0:64, hh2, :],
                       Kh[64 * hh2:64 * hh2 + 64, :], ch="rot1")
                rotK_tok[0] = pl.dma("sync", Kcat3[64:128, hh2, :],
                                     Kl[64 * hh2:64 * hh2 + 64, :], ch="rot1")
            for hh in range(2):
                pending_post[0] = attn_head(2 * hp + hh, 64 * hh, hh, with_rel,
                                            pending_post[0])
        if pending_post[0] is not None:
            for sl_, fn_ in sorted(pending_post[0], key=lambda x: x[0]):
                fn_()
            pending_post[0] = None
        wo_phase()
        out_toks = wo_out[-2:] + [pl.mark((e_, pl.last_real[e_]))
                                  for e_ in ("tensor", "vector", "scalar")]
        al.pop()
        return out_toks


    prep_done = [tok_mrep, btok.get("wqfull"),
                 wq_store["wd"][0], wq_store["wd"][1]]
    if stage == 1:
        final_toks = ffn_sublayer(x_in, y_out, "ffn1", pre_tok=prep_done)
    elif stage == 3:
        final_toks = attention_sublayer(x_in, y_out, with_rel=False,
                                        pre_tok=prep_done)
    elif stage == 4:
        final_toks = attention_sublayer(x_in, y_out, with_rel=True,
                                        pre_tok=prep_done)
    elif stage == 5:
        t1 = ffn_sublayer(x_in, xres, "ffn1", pre_tok=prep_done)
        t2 = attention_sublayer(xres, xres, with_rel=True, pre_tok=t1)
        final_toks = ffn_sublayer(xres, y_out, "ffn2", pre_tok=t2)
    else:
        raise NotImplementedError

    for t in final_toks:
        teng, kind, sem, val = t
        if kind == "dma":
            pl.emit("gpsimd", lambda e, s=sem, v=val: e.wait_ge(s, v))

    pl.replay()
    return nc


_NC_CACHE = {}


def _get_nc(stage):
    if stage not in _NC_CACHE:
        _NC_CACHE[stage] = build_program(stage)
    return _NC_CACHE[stage]


def kernel(L0_attn_x, L0_attn_Wq, L0_attn_bq, L0_attn_Wk, L0_attn_bk,
           L0_attn_Wv, L0_attn_bv, L0_attn_Wo, L0_attn_bo,
           L0_attn_rel_k, L0_attn_rel_v,
           L0_ffn_w_gate, L0_ffn_w_up, L0_ffn_w_down, _stage=5):
    nc = _get_nc(_stage)
    shared = dict(
        wq=np.ascontiguousarray(L0_attn_Wq, dtype=np.float32),
        wk=np.ascontiguousarray(L0_attn_Wk, dtype=np.float32),
        wv=np.ascontiguousarray(L0_attn_Wv, dtype=np.float32),
        wo=np.ascontiguousarray(L0_attn_Wo, dtype=np.float32),
        bq=np.ascontiguousarray(L0_attn_bq, dtype=np.float32),
        bk=np.ascontiguousarray(L0_attn_bk, dtype=np.float32),
        bv=np.ascontiguousarray(L0_attn_bv, dtype=np.float32),
        bo=np.ascontiguousarray(L0_attn_bo, dtype=np.float32),
        relk=np.ascontiguousarray(L0_attn_rel_k, dtype=np.float32),
        relv=np.ascontiguousarray(L0_attn_rel_v, dtype=np.float32),
        wg=np.ascontiguousarray(L0_ffn_w_gate, dtype=np.float32),
        wu=np.ascontiguousarray(L0_ffn_w_up, dtype=np.float32),
        wd=np.ascontiguousarray(L0_ffn_w_down, dtype=np.float32),
    )
    shared.update({k: v for k, v in CONSTS.items()})
    xb = np.ascontiguousarray(np.asarray(L0_attn_x, dtype=np.float32))
    in_maps = []
    for c in range(8):
        m = dict(shared)
        m["x_in"] = xb[c]
        in_maps.append(m)
    res = run_bass_kernel_spmd(nc, in_maps, core_ids=list(range(8)))
    return np.stack([r["y_out"] for r in res.results], axis=0)


# version 21
# speedup vs baseline: 1.0623x; 1.0226x over previous
import numpy as np
import ml_dtypes
import concourse.bass as bass
import concourse.mybir as mybir
from concourse.bass_utils import run_bass_kernel_spmd

F32 = mybir.dt.float32
BF16 = mybir.dt.bfloat16
I32 = mybir.dt.int32
AX = mybir.AxisListType.X
ALU = mybir.AluOpType
ACT = mybir.ActivationFunctionType

T, D, FF, H, HD = 2048, 1024, 4096, 16, 64
NT, ND, NF = T // 128, D // 128, FF // 128
ENGS = ("sync", "scalar", "tensor", "vector", "gpsimd")
BB = ml_dtypes.bfloat16


class Op:
    __slots__ = ("fn", "incs")

    def __init__(self, fn):
        self.fn = fn
        self.incs = []


class Plan:
    def __init__(self, nc):
        self.nc = nc
        self.q = {e: [] for e in ENGS}
        self.ctr = {e: [nc.alloc_semaphore(name=f"c_{e}"), 0] for e in ENGS}
        self.dch = {}
        self.waited = {e: {} for e in ENGS}

    def emit(self, eng, fn, is_wait=False, is_dma=False):
        op = Op(fn)
        self.q[eng].append(op)
        if not (is_wait or is_dma):
            self.last_real = getattr(self, "last_real", {})
            self.last_real[eng] = op
        return (eng, op)

    def dma(self, eng, out, in_, ch="d0"):
        if ch not in self.dch:
            self.dch[ch] = [self.nc.alloc_semaphore(name=f"dm_{ch}"), 0]
        ent = self.dch[ch]
        sem = ent[0]
        if ent[1] > 0:
            self.wait(eng, (eng, "dma", sem, ent[1]))
        ent[1] += 16
        self.emit(eng, lambda e, o=out, i=in_, s=sem:
                  e.dma_start(out=o, in_=i).then_inc(s, 16), is_dma=True)
        return (eng, "dma", sem, ent[1])

    def mark(self, emitted):
        eng, op = emitted
        if op.incs:
            return (eng, "c", op.incs[0][0], op.incs[0][1])
        sem, v = self.ctr[eng]
        self.ctr[eng][1] = v + 1
        op.incs.append((sem, v + 1))
        return (eng, "c", sem, v + 1)

    def wait(self, eng, tok):
        if tok is None:
            return
        teng, kind, sem, val = tok
        seen = self.waited[eng].get(id(sem), 0)
        if val <= seen:
            return
        self.waited[eng][id(sem)] = val
        self.emit(eng, lambda e, s=sem, v=val: e.wait_ge(s, v), is_wait=True)

    def fence(self, eng):
        op = getattr(self, "last_real", {}).get(eng)
        if op is None:
            return None
        tok = self.mark((eng, op))
        self.wait(eng, tok)
        return tok

    def waits(self, eng, toks):
        for t in toks:
            self.wait(eng, t)

    def replay(self):
        with self.nc.Block() as blk:
            for name in ENGS:
                ops = self.q[name]

                def mk(ops):
                    def body(eng):
                        for op in ops:
                            ins = op.fn(eng)
                            for sem, val in op.incs:
                                ins.then_inc(sem, 1)
                    return body

                getattr(blk, name)(mk(ops))


def bfv(ap):
    return ap.bitcast(BF16)


def build_consts():
    c = {}
    c["c_idb"] = np.eye(128).astype(BB)
    ind = np.zeros((128, 16, 16), dtype=np.float32)
    for t in range(16):
        ind[:, t, t] = 1.0
    c["c_ind"] = ind.astype(BB)
    p = np.arange(128)
    c["c_lt"] = (p[None, :] <= p[:, None]).astype(BB)
    pre = np.zeros((16, 4, 5), dtype=np.float32)
    for b in range(4):
        for j in range(4):
            pre[: max(4 * b + j - 1, 0), b, j] = 1.0
        pre[:, b, 4] = 1.0
    c["c_pre"] = pre.astype(BB)
    c["c_onesrow"] = np.ones((1, 128), dtype=np.float32)
    c["c_idf32"] = np.eye(128, dtype=np.float32)
    c["c_id2"] = np.eye(2, dtype=np.float32)
    c["c_hmb"] = c["c_hm"] if False else None
    c["c_pre32"] = None
    c["c_zeros"] = np.zeros((2, 512)).astype(BB)
    c["c_onescol"] = np.ones((128, 1), dtype=np.float32)
    c["c_ones2row"] = np.ones((2, 512)).astype(BB)
    sel = np.zeros((2, 4, 128), dtype=np.float32)
    sel[0, 0:2] = 1.0
    sel[1, 2:4] = 1.0
    c["c_sel"] = sel.astype(BB)
    hm = np.zeros((128, 2), dtype=np.float32)
    hm[0:64, 0] = 1.0
    hm[64:128, 1] = 1.0
    c["c_hm"] = hm
    c["c_hmb"] = hm.astype(BB)
    c["c_pre32"] = np.ascontiguousarray(c["c_pre"].astype(np.float32))
    return c


CONSTS = {k: v for k, v in build_consts().items() if v is not None}


def build_program(stage=3):
    nc = bass.Bass("TRN2", target_bir_lowering=False, debug=False)
    pl = Plan(nc)

    x_in = nc.dram_tensor("x_in", [T, D], F32, kind="ExternalInput")
    W = {n: nc.dram_tensor(n, list(s), F32, kind="ExternalInput") for n, s in
         [("wq", (D, D)), ("wk", (D, D)), ("wv", (D, D)), ("wo", (D, D)),
          ("wg", (FF, D)), ("wu", (FF, D)), ("wd", (D, FF))]}
    Bi = {n: nc.dram_tensor(n, [D], F32, kind="ExternalInput")
          for n in ("bq", "bk", "bv", "bo")}
    relk = nc.dram_tensor("relk", [257, HD], F32, kind="ExternalInput")
    relv = nc.dram_tensor("relv", [257, HD], F32, kind="ExternalInput")
    CT = {k: nc.dram_tensor(k, list(v.shape),
                            F32 if v.dtype == np.float32 else BF16,
                            kind="ExternalInput") for k, v in CONSTS.items()}
    y_out = nc.dram_tensor("y_out", [T, D], F32, kind="ExternalOutput")

    xres = nc.dram_tensor("xres", [T, D], F32, kind="Internal")
    wqh = {"wg": nc.dram_tensor("wgq_h", [D, FF], BF16, kind="Internal"),
           "wu": nc.dram_tensor("wuq_h", [D, FF], BF16, kind="Internal"),
           "wd": nc.dram_tensor("wdq_h", [FF, D], BF16, kind="Internal")}
    e_h = nc.dram_tensor("e_h", [4, 2, 128 * 511], BF16, kind="Internal")
    att_hbm = nc.dram_tensor("att_hbm", [2 * ND * 128 * T], BF16, kind="Internal")
    s_h = nc.dram_tensor("s_h", [2, 2, 128 * 384], BF16, kind="Internal")

    ACOLS = 53056
    arena = nc.alloc_sbuf_tensor("arena", [128, ACOLS], F32)
    AAP = arena.ap()

    class Alloc:
        def __init__(self):
            self.top = 0
            self.marks = []
            self.peak = 0

        def take(self, ncols):
            a = self.top
            assert a + ncols <= ACOLS, f"arena overflow {a}+{ncols}"
            self.top = a + ncols
            self.peak = max(self.peak, self.top)
            return AAP[:, a:a + ncols]

        def takeb(self, nb):
            return bfv(self.take((nb + 1) // 2))[:, 0:nb]

        def push(self):
            self.marks.append(self.top)

        def pop(self):
            self.top = self.marks.pop()

    al = Alloc()

    pb = [nc.alloc_psum_tensor(f"pb{i}", [128, 512], F32) for i in range(6)]
    pt = [nc.alloc_psum_tensor(f"pt{i}", [128, 1024], BF16) for i in range(2)]
    btok = {}

    def bank_wait(eng, key):
        pl.wait(eng, btok.get(key))

    def bank_set(key, tok):
        btok[key] = tok

    id_b = al.takeb(128)
    ind_b = al.takeb(256)
    lt_b = al.takeb(128)
    pre_b = al.takeb(20)[0:16, :]
    onesrow = al.take(128)[0:1, :]
    onescol = al.take(1)
    ones2row = al.takeb(512)[0:2, :]
    hm_f = al.take(2)
    id_f32 = al.take(128)
    id2_f = al.take(2)[0:2, :]
    hm_b = al.takeb(2)
    pre32_b = al.take(20)[0:16, :]
    zrows_b = al.takeb(512)[0:2, :]
    zcol_b = zrows_b[0:1, 0:1]
    zrow_b = zrows_b[0:1, 0:512]
    toks = [
        pl.dma("sync", id_f32, CT["c_idf32"].ap(), ch="const"),
        pl.dma("sync", id2_f, CT["c_id2"].ap(), ch="const"),
        pl.dma("sync", hm_b, CT["c_hmb"].ap(), ch="const"),
        pl.dma("sync", pre32_b, CT["c_pre32"].ap().rearrange("p a b -> p (a b)"), ch="const"),
        pl.dma("sync", zrows_b, CT["c_zeros"].ap(), ch="const"),
        pl.dma("sync", id_b, CT["c_idb"].ap(), ch="const"),
        pl.dma("sync", ind_b, CT["c_ind"].ap().rearrange("p a b -> p (a b)"), ch="const"),
        pl.dma("sync", lt_b, CT["c_lt"].ap(), ch="const"),
        pl.dma("sync", pre_b, CT["c_pre"].ap().rearrange("p a b -> p (a b)"), ch="const"),
        pl.dma("sync", onesrow, CT["c_onesrow"].ap(), ch="const"),
        pl.dma("sync", onescol, CT["c_onescol"].ap(), ch="const"),
        pl.dma("sync", ones2row, CT["c_ones2row"].ap(), ch="const"),
        pl.dma("sync", hm_f, CT["c_hm"].ap(), ch="const"),
    ]
    sel4 = al.takeb(256)[0:4, :]
    toks.append(pl.dma("sync", sel4.rearrange("p (s n) -> p s n", n=128),
                       bass.AP(CT["c_sel"], 0, [[128, 4], [512, 2], [1, 128]]), ch="const"))
    for e in ENGS:
        pl.wait(e, toks[-1])

    def ap_of(x):
        return x if isinstance(x, bass.AP) else x[:]

    def mm(bank_ap, lhsT, rhs, start, stop, skip=False):
        return pl.emit("tensor", lambda e, a=ap_of(bank_ap), l=ap_of(lhsT),
                       r=ap_of(rhs), s=start, t=stop, k=skip:
                       e.matmul(a, l, r, start=s, stop=t, skip_group_check=k))

    def tr(pt_ap, src_b):
        return pl.emit("tensor", lambda e, a=pt_ap, s=src_b:
                       e.transpose(a, s, id_b))

    def ve(fn):
        return pl.emit("vector", fn)

    def sc(fn):
        return pl.emit("scalar", fn)

    def transpose_to(dst, src_blocks, ptsel, src3d=False):
        nblk = len(src_blocks)
        assert nblk <= 8
        pt_b = pt[ptsel]
        bank_wait("tensor", f"pt{ptsel}")
        for k, s in enumerate(src_blocks):
            ttk = tr(pt_b[:, 128 * k:128 * (k + 1)], s)
        tok = pl.mark(ttk)
        pl.wait("vector", tok)
        srcv = pt_b[:, 0:128 * nblk]
        if src3d:
            srcv = srcv.rearrange("p (c n) -> p c n", n=128)
        evk = ve(lambda e, d=dst, s=srcv: e.tensor_copy(d, s))
        etok = pl.mark(evk)
        bank_set(f"pt{ptsel}", etok)
        return etok

    al.push()
    wbuf = [al.take(1024) for _ in range(2)]
    wsum = al.take(1)
    red1 = al.take(1)
    mrow = al.take(1)[0:1, :]
    srow = al.take(1)[0:1, :]
    srep = al.take(1)
    dqk = al.take(3)[0:1, :]
    i32t = al.take(1024)
    wq_full = al.takeb(NF * 1024)
    rowbuf = [al.takeb(4096) for _ in range(2)]

    wspec = [("wg", W["wg"], wqh["wg"], FF, D), ("wu", W["wu"], wqh["wu"], FF, D),
             ("wd", W["wd"], wqh["wd"], D, FF)]
    wq_store = {}
    cons_tok = [None, None]
    st_tok = [None, None]
    for wi, (nm, wt, wqt, RO, CO) in enumerate(wspec):
        nch, ncc = RO // 128, CO // 128
        npc = CO // 1024
        pieces = [(ch, cp) for ch in range(nch) for cp in range(npc)]
        pl.fence("vector")
        ve(lambda e: e.tensor_scalar(wsum, onescol, 0.0, None, ALU.mult))
        lastv = None
        for pi, (ch, cp) in enumerate(pieces):
            bi = pi % 2
            if cons_tok[bi] is not None:
                pl.wait("sync", cons_tok[bi])
            ltok = pl.dma("sync", wbuf[bi],
                          wt.ap()[128 * ch:128 * (ch + 1), 1024 * cp:1024 * (cp + 1)],
                          ch=f"wqp{bi}")
            pl.wait("vector", ltok)
            pl.fence("vector")
            rv = ve(lambda e, b=wbuf[bi]: e.reduce_sum(
                red1, b, axis=AX, apply_absolute_value=True))
            cons_tok[bi] = pl.mark(rv)
            pl.fence("vector")
            lastv = ve(lambda e: e.tensor_tensor(wsum, wsum, red1, ALU.add))
        tok_sums = pl.mark(lastv)
        pl.wait("tensor", tok_sums)
        bank_wait("tensor", "pb0")
        m1 = mm(pb[0][0:1, 0:1], onescol, wsum, True, True)
        pl.wait("vector", pl.mark(m1))
        ve(lambda e, s=1.0 / (RO * CO): e.tensor_scalar(
            mrow, pb[0][0:1, 0:1], s, 1e-5, ALU.mult, ALU.max))
        pl.fence("vector")
        ve(lambda e: e.reciprocal(srow, mrow))
        kv = ve(lambda e, c=dqk[:, wi:wi + 1]: e.tensor_copy(c, mrow))
        tok_s = pl.mark(kv)
        bank_set("pb0", tok_s)
        pl.wait("tensor", tok_s)
        bank_wait("tensor", "pb0")
        m2 = mm(pb[0][:, 0:1], onesrow, srow, True, True)
        pl.wait("vector", pl.mark(m2))
        evs = ve(lambda e: e.tensor_copy(srep, pb[0][:, 0:1]))
        bank_set("pb0", pl.mark(evs))
        bank_wait("vector", "wqfull")
        wq3 = wq_full.rearrange("p (c n) -> p c n", n=CO)
        qv = None
        for pi, (ch, cp) in enumerate(pieces):
            bi = pi % 2
            if cons_tok[bi] is not None:
                pl.wait("sync", cons_tok[bi])
            ltok = pl.dma("sync", wbuf[bi],
                          wt.ap()[128 * ch:128 * (ch + 1), 1024 * cp:1024 * (cp + 1)],
                          ch=f"wqp{bi}")
            pl.wait("vector", ltok)
            pl.fence("vector")
            fv = ve(lambda e, b=wbuf[bi]: e.tensor_scalar(
                i32t.bitcast(I32)[:, 0:1024], b, srep, None, ALU.mult))
            cons_tok[bi] = pl.mark(fv)
            pl.fence("vector")
            qv = ve(lambda e, d=wq3[:, ch, 1024 * cp:1024 * (cp + 1)]:
                    e.tensor_scalar(d, i32t.bitcast(I32)[:, 0:1024],
                                    -1.0, 1.0, ALU.max, ALU.min))
        tok_q = pl.mark(qv)
        pl.wait("tensor", tok_q)
        for rc in range(ncc):
            bo = rc % 2
            if st_tok[bo] is not None:
                pl.wait("vector", st_tok[bo])
            evk_tok = None
            for rnd in range((nch + 7) // 8):
                nblk = min(8, nch - 8 * rnd)
                blocks = [wq3[:, 8 * rnd + k, 128 * rc:128 * rc + 128]
                          for k in range(nblk)]
                evk_tok = transpose_to(
                    rowbuf[bo][:, 1024 * rnd:1024 * rnd + 128 * nblk],
                    blocks, rnd % 2)
            pl.wait("gpsimd", evk_tok)
            st_tok[bo] = pl.dma("gpsimd", wqt.ap()[128 * rc:128 * (rc + 1), :],
                                rowbuf[bo][:, 0:RO], ch=f"wqs{bo}")
        bank_set("wqfull", evk_tok)
        wq_store[nm] = [st_tok[0], st_tok[1]]
    al.pop()

    mrep = al.take(3)
    bank_wait("tensor", "pb0")
    m3 = mm(pb[0][:, 0:3], onesrow, dqk[:, 0:3], True, True)
    pl.wait("vector", pl.mark(m3))
    evm = ve(lambda e: e.tensor_copy(mrep, pb[0][:, 0:3]))
    tok_mrep = pl.mark(evm)
    bank_set("pb0", tok_mrep)

    ln_m = al.take(1)
    ln_sq = al.take(1)
    ln_v = al.take(1)
    ln_rs = al.take(1)
    qa = al.take(1)
    qs = al.take(1)
    i32q = al.take(1024)
    sq_scr = al.take(1024)

    def ln_tile(xt, h_out):
        ve(lambda e: e.reduce_sum(ln_m, xt, axis=AX))
        pl.fence("vector")
        tok = pl.mark(ve(lambda e: e.tensor_scalar(ln_m, ln_m, 1.0 / D, None, ALU.mult)))
        s1 = sc(lambda e: e.activation(sq_scr, xt, ACT.Square, accum_out=ln_sq))
        pl.wait("vector", pl.mark(s1))
        pl.wait("vector", tok)
        ve(lambda e: e.tensor_tensor(ln_v, ln_m, ln_m, ALU.mult))
        pl.fence("vector")
        ve(lambda e: e.tensor_scalar(
            ln_v, ln_sq, 1.0 / D, ln_v, ALU.mult, ALU.subtract))
        pl.fence("vector")
        t2 = pl.mark(ve(lambda e: e.tensor_scalar(
            ln_v, ln_v, 1e-5, None, ALU.add)))
        pl.wait("scalar", t2)
        s2 = sc(lambda e: e.activation(ln_rs, ln_v, ACT.Sqrt))
        pl.wait("vector", pl.mark(s2))
        ve(lambda e: e.reciprocal(ln_rs, ln_rs))
        pl.fence("vector")
        return pl.mark(ve(lambda e: e.tensor_scalar(
            h_out, xt, ln_m, ln_rs, ALU.subtract, ALU.mult)))

    def quant_tok(src, n, qdst_b, dq_col):
        pl.fence("vector")
        ve(lambda e: e.reduce_max(qa, src, axis=AX, apply_absolute_value=True))
        pl.fence("vector")
        ve(lambda e: e.tensor_scalar(qa, qa, 1e-5, 1.0 / 127.0, ALU.max, ALU.mult))
        pl.fence("vector")
        ve(lambda e: e.tensor_copy(dq_col, qa))
        ve(lambda e: e.reciprocal(qs, qa))
        nb = (n + 1023) // 1024
        for b in range(nb):
            c0, c1 = 1024 * b, min(1024 * (b + 1), n)
            pl.fence("vector")
            ve(lambda e, s=src[:, c0:c1], d=i32q.bitcast(I32)[:, 0:c1 - c0]:
               e.tensor_scalar(d, s, qs, None, ALU.mult))
            pl.fence("vector")
            qv = ve(lambda e, s=i32q.bitcast(I32)[:, 0:c1 - c0],
                    d=qdst_b[:, c0:c1]: e.tensor_copy(d, s))
        return pl.mark(qv)


    def ffn_sublayer(src_dram, dst_dram, tag, pre_tok=None):
        al.push()
        xtile = al.take(1024)
        htile = al.take(1024)
        xq_b = al.takeb(1024)
        xqT = al.takeb(ND * 8 * 128)
        xqT4 = xqT.rearrange("p (c t n) -> p c t n", t=8, n=128)
        sg = al.take(8)
        su = al.take(8)
        sd = al.take(8)
        dqx = al.take(1)
        pamax = al.take(8)
        p_region = al.take(8 * 4096)
        p_half = [p_region[:, 4096 * k:4096 * (k + 1)] for k in range(8)]
        wgb = [al.takeb(ND * 512) for _ in range(2)]
        wub = [al.takeb(ND * 512) for _ in range(2)]
        wgbF = [bfv(b).bitcast(F32) for b in wgb]
        ytile = [wgbF[0][:, 0:1024], wgbF[1][:, 0:1024]]
        gsil_t = al.take(512)
        gsil = [gsil_t, gsil_t]

        wgb3 = [b.rearrange("p (c n) -> p c n", n=512) for b in wgb]
        wub3 = [b.rearrange("p (c n) -> p c n", n=512) for b in wub]
        p_regb = bfv(p_region)
        wdb_chunks = [p_regb[:, 8192 * (oc // 4) + 1024 * (oc % 4):
                             8192 * (oc // 4) + 1024 * (oc % 4) + 1024]
                      for oc in range(NF)]
        wdb_dst = [p_regb[:, 8192 * a:8192 * a + 4096]
                   .rearrange("p (b n) -> p b n", n=1024) for a in range(8)]
        wdb_src = [bass.AP(wqh["wd"], 512 * D * a,
                           [[D, 128], [128 * D, 4], [1, 1024]])
                   for a in range(8)]

        xcons = None
        wcons = [None, None]
        wdcons = None
        ycons = [None, None]
        gtok = [None]
        pre_list = pre_tok if isinstance(pre_tok, list) else (
            [pre_tok] if pre_tok is not None else [])
        for tk_ in pre_list:
            for e_ in ("sync", "vector", "scalar", "tensor", "gpsimd"):
                pl.wait(e_, tk_)
        half_tok = None
        out_toks = []
        for half in range(2):
            xqT_tok = None
            for ti in range(8):
                tau = 8 * half + ti
                if xcons is not None:
                    pl.wait("sync", xcons)
                xtok = pl.dma("sync", xtile,
                              src_dram.ap()[128 * tau:128 * (tau + 1), :],
                              ch="xa")
                if half_tok is not None:
                    pl.wait("vector", half_tok)
                    pl.wait("scalar", half_tok)
                    pl.wait("tensor", half_tok)
                pl.wait("vector", xtok)
                pl.fence("vector")
                pl.wait("scalar", xtok)
                htok = ln_tile(xtile, htile)
                xcons = htok
                qtok = quant_tok(htile, 1024, xq_b, dqx)
                pl.wait("vector", tok_mrep)
                pl.fence("vector")
                ve(lambda e, c=sg[:, ti:ti + 1]: e.tensor_scalar(
                    c, dqx, mrep[:, 0:1], None, ALU.mult))
                ve(lambda e, c=su[:, ti:ti + 1]: e.tensor_scalar(
                    c, dqx, mrep[:, 1:2], None, ALU.mult))
                pl.wait("tensor", qtok)
                xqT_tok = transpose_to(
                    xqT4[:, :, ti, :],
                    [xq_b[:, 128 * dc:128 * (dc + 1)] for dc in range(ND)],
                    ti % 2, src3d=True)
            for ob in range(8):
                bo = ob % 2
                if wcons[bo] is not None:
                    pl.wait("sync", wcons[bo])
                if ycons[bo] is not None:
                    pl.wait("sync", ycons[bo])
                pl.waits("sync", wq_store["wg"] + wq_store["wu"])
                wtok1 = pl.dma("sync", wgb3[bo], bass.AP(
                    wqh["wg"], 512 * ob, [[FF, 128], [128 * FF, ND], [1, 512]]),
                    ch=f"wgb{bo}")
                wtok2 = pl.dma("sync", wub3[bo], bass.AP(
                    wqh["wu"], 512 * ob, [[FF, 128], [128 * FF, ND], [1, 512]]),
                    ch=f"wub{bo}")
                pl.wait("tensor", wtok1)
                pl.wait("tensor", wtok2)
                pl.wait("tensor", xqT_tok)
                for ti in range(8):
                    bg, bgk = pb[ti % 2], f"pb{ti % 2}"
                    bu, buk = pb[2 + ti % 2], f"pb{2 + ti % 2}"
                    bank_wait("tensor", bgk)
                    bank_wait("tensor", buk)
                    mku = None
                    for dc in range(ND):
                        lhs = xqT4[:, dc, ti, :]
                        mm(bg, lhs, wgb3[bo][:, dc], dc == 0, dc == ND - 1)
                        mku = mm(bu, lhs, wub3[bo][:, dc], dc == 0, dc == ND - 1)
                    tok_mm = pl.mark(mku)
                    pl.wait("scalar", tok_mm)
                    if gtok[0] is not None:
                        pl.wait("scalar", gtok[0])
                    stok = sc(lambda e, b=bg[:], d=sg[:, ti:ti + 1], g=gsil[ti % 2]:
                              e.activation(g, b, ACT.Silu, scale=d))
                    smk = pl.mark(stok)
                    bank_set(bgk, smk)
                    pl.wait("vector", smk)
                    ptok = ve(lambda e, b=bu[:], d=su[:, ti:ti + 1], g=gsil[ti % 2],
                              dst=p_half[ti][:, 512 * ob:512 * (ob + 1)]:
                              e.scalar_tensor_tensor(dst, b, d, g,
                                                     ALU.mult, ALU.mult))
                    pmk = pl.mark(ptok)
                    bank_set(buk, pmk)
                    gtok[0] = pmk
                    pl.fence("vector")
                    ve(lambda e, p=p_half[ti][:, 512 * ob:512 * (ob + 1)]:
                       e.reduce_max(qa, p, axis=AX, apply_absolute_value=True))
                    pl.fence("vector")
                    if ob == 0:
                        mxv = ve(lambda e, a=pamax[:, ti:ti + 1]: e.tensor_copy(a, qa))
                    else:
                        mxv = ve(lambda e, a=pamax[:, ti:ti + 1]:
                                 e.tensor_tensor(a, a, qa, ALU.max))
                    wcons[bo] = pl.mark(mxv) if ti == 7 else wcons[bo]
            xq2T_tok = [None] * 8
            for ti in range(8):
                pl.fence("vector")
                ve(lambda e, a=pamax[:, ti:ti + 1]: e.tensor_scalar(
                    qa, a, 1e-5, 1.0 / 127.0, ALU.max, ALU.mult))
                pl.fence("vector")
                ve(lambda e, c=sd[:, ti:ti + 1]: e.tensor_scalar(
                    c, qa, mrep[:, 2:3], 0.5, ALU.mult, ALU.mult))
                ve(lambda e: e.reciprocal(qs, qa))
                xq2 = bfv(p_half[ti])[:, 0:4096]
                qv = None
                for b2 in range(4):
                    pl.fence("vector")
                    ve(lambda e, s=p_half[ti][:, 1024 * b2:1024 * (b2 + 1)],
                       d=i32q.bitcast(I32)[:, 0:1024]:
                       e.tensor_scalar(d, s, qs, None, ALU.mult))
                    pl.fence("vector")
                    qv = ve(lambda e, s=i32q.bitcast(I32)[:, 0:1024],
                            d=xq2[:, 1024 * b2:1024 * (b2 + 1)]: e.tensor_copy(d, s))
                qtok = pl.mark(qv)
                pl.wait("tensor", qtok)
                xq2T = bfv(p_half[ti])[:, 4096:8192]
                for g4 in range(4):
                    xq2T_tok[ti] = transpose_to(
                        xq2T[:, g4 * 1024:g4 * 1024 + 1024],
                        [xq2[:, g4 * 1024 + 128 * k:g4 * 1024 + 128 * (k + 1)]
                         for k in range(8)], g4 % 2)
            pl.waits("sync", wq_store["wd"])
            for ti in range(8):
                pl.wait("sync", xq2T_tok[ti])
            for a in range(8):
                wdtok = pl.dma("sync", wdb_dst[a], wdb_src[a], ch="wd")
            pl.wait("tensor", wdtok)
            pl.wait("gpsimd", wcons[0])
            pl.wait("gpsimd", wcons[1])
            dlast = None
            for ti in range(8):
                tau = 8 * half + ti
                yb = ytile[ti % 2]
                if ycons[ti % 2] is not None:
                    pl.wait("gpsimd", ycons[ti % 2])
                xtok = pl.dma("gpsimd", yb,
                              src_dram.ap()[128 * tau:128 * (tau + 1), :],
                              ch=f"yx{ti % 2}")
                xq2T = bfv(p_half[ti])[:, 4096:8192]
                pl.wait("tensor", xq2T_tok[ti])
                evk = None
                for db in range(2):
                    bank_wait("tensor", "pb4")
                    mkd = None
                    for oc in range(NF):
                        mkd = mm(pb[4], xq2T[:, 128 * oc:128 * (oc + 1)],
                                 wdb_chunks[oc][:, 512 * db:512 * (db + 1)],
                                 oc == 0, oc == NF - 1)
                    dlast = mkd
                    tok_mm = pl.mark(mkd)
                    pl.wait("vector", tok_mm)
                    pl.wait("vector", xtok)
                    evk = ve(lambda e, b=pb[4][:], d=sd[:, ti:ti + 1],
                             y=yb[:, 512 * db:512 * (db + 1)]:
                             e.scalar_tensor_tensor(y, b, d, y, ALU.mult, ALU.add))
                    bank_set("pb4", pl.mark(evk))
                ytok = pl.mark(evk)
                pl.wait("gpsimd", ytok)
                stok = pl.dma("gpsimd",
                              dst_dram.ap()[128 * tau:128 * (tau + 1), :], yb,
                              ch=f"ys{ti % 2}")
                ycons[ti % 2] = stok
                out_toks.append(stok)
            wdcons = pl.mark(dlast)
            half_tok = wdcons
        al.pop()
        out_toks = out_toks[-2:] + [pl.mark((e_, pl.last_real[e_]))
                                    for e_ in ("tensor", "vector", "scalar")]
        return out_toks


    def attention_sublayer(src_dram, dst_dram, with_rel=True, pre_tok=None):
        al.push()
        att0 = al.top
        xtile_a = al.take(1024)
        htile_a = al.take(1024)
        hh_t = al.takeb(1024)
        hl_t = al.takeb(1024)
        hT0 = al.top
        hTh = al.takeb(ND * NT * 128)
        hTl = al.takeb(ND * NT * 128)
        hTh4 = hTh.rearrange("p (c t n) -> p c t n", t=NT, n=128)
        hTl4 = hTl.rearrange("p (c t n) -> p c t n", t=NT, n=128)
        nhl = [al.takeb(512) for _ in range(4)]
        rkx_h = al.takeb(512)
        rkx_l = al.takeb(512)
        rkt0 = al.top
        rkt32 = al.take(768)
        rvp_h = al.takeb(2 * 64)
        rvp_l = al.takeb(2 * 64)
        rvd_f = al.take(64)[0:1, :]
        rkm = al.take(1)[0:1, :]
        r256rep = al.take(64)
        rvcat_h = al.takeb(2)[:, 0:1]
        rvcat_l = al.takeb(2)[:, 1:2]
        rvc32 = al.take(1)
        wpb = [al.take(1024) for _ in range(2)]
        WT = {w: (al.takeb(ND * 128), al.takeb(ND * 128)) for w in ("q", "k", "v")}
        bias2 = {w: al.takeb(256)[0:2, 0:128] for w in ("q", "k", "v")}
        brow = al.take(128)[0:1, :]
        bc2 = al.takeb(2)
        bcol_s = al.take(1)
        qk0 = al.top
        Qh, Ql = al.takeb(T), al.takeb(T)
        Kh, Kl = al.takeb(T), al.takeb(T)
        Vx_h = al.takeb(NT * 160)
        Vx_l = al.takeb(NT * 160)
        Vx_h3 = Vx_h.rearrange("p (t n) -> p t n", n=160)
        Vx_l3 = Vx_l.rearrange("p (t n) -> p t n", n=160)
        Kcat = bfv(AAP[:, att0:att0 + 2048])
        Kcat3 = Kcat.rearrange("p (c n) -> p c n", n=T)
        qdup2 = [al.takeb(512) for _ in range(2)]
        sqb = al.takeb(T)
        mrow4 = al.take(4)[0:1, :]
        macc = al.take(2)
        negB2 = al.take(2)
        tb_s = al.take(2)[0:1, :]
        ring0 = al.top
        ring_h = [al.takeb(512) for _ in range(6)]
        ring_l = [al.takeb(512) for _ in range(6)]
        pf32 = [al.take(512) for _ in range(2)]
        e_hi = [al.takeb(512) for _ in range(2)]
        e_lo = [al.takeb(512) for _ in range(2)]
        rs_h = [al.takeb(384) for _ in range(4)]
        rs_l = [al.takeb(384) for _ in range(4)]
        stp_h = [al.takeb(384) for _ in range(4)]
        stp_l = [al.takeb(384) for _ in range(4)]
        psk_sl = [(al.takeb(256), al.takeb(256)) for _ in range(2)]
        psk_sl = psk_sl + [
            (bfv(AAP[:, rkt0:rkt0 + 128]), bfv(AAP[:, rkt0 + 128:rkt0 + 256])),
            (bfv(AAP[:, rkt0 + 256:rkt0 + 384]), bfv(AAP[:, rkt0 + 384:rkt0 + 512]))]
        psk_rdy = [None] * 4
        pskT_h = al.takeb(256)
        pskT_l = al.takeb(256)
        tmp4 = al.takeb(16)
        bm4 = [al.takeb(256)[0:4, 0:128] for _ in range(4)]
        cs_sb = al.take(512)[0:16, :]
        tri4p = [al.take(4), al.take(4)]
        triT_s = al.take(128)[0:4, :]
        c0r4 = al.take(128)[0:4, :]
        c0flat = al.take(512)[0:1, :]
        trflat = al.take(512)[0:1, :]
        lflat = al.take(512)[0:1, :]
        lrec = al.take(4)
        pfx_sb = al.take(512)[0:5, :]
        nsc = al.take(512)
        WoT8 = [(bfv(AAP[:, hT0 + 1024 * oc:hT0 + 1024 * oc + 512]),
                 bfv(AAP[:, hT0 + 1024 * oc + 512:hT0 + 1024 * oc + 1024]))
                for oc in range(8)]
        boext8 = al.take(8)
        aosc = [AAP[:, qk0 + 512 * i:qk0 + 512 * (i + 1)] for i in range(8)]
        ytile_a = [AAP[:, qk0 + 4096 + 1024 * i:qk0 + 4096 + 1024 * (i + 1)]
                   for i in range(2)]
        attb_h = bfv(AAP[:, ring0:ring0 + 2048])
        attb_l = bfv(AAP[:, ring0 + 2048:ring0 + 4096])
        attb_h3 = attb_h.rearrange("p (c n) -> p c n", n=512)
        attb_l3 = attb_l.rearrange("p (c n) -> p c n", n=512)

        pre_list = pre_tok if isinstance(pre_tok, list) else (
            [pre_tok] if pre_tok is not None else [])
        for tk_ in pre_list:
            for e_ in ENGS:
                pl.wait(e_, tk_)

        xcons = None
        hT_tok = None
        for tau in range(NT):
            if xcons is not None:
                pl.wait("sync", xcons)
            xtok = pl.dma("sync", xtile_a,
                          src_dram.ap()[128 * tau:128 * (tau + 1), :], ch="xa")
            pl.wait("vector", xtok)
            pl.wait("scalar", xtok)
            pl.fence("vector")
            htok = ln_tile(xtile_a, htile_a)
            xcons = htok
            pl.fence("vector")
            ve(lambda e: e.tensor_copy(hh_t, htile_a))
            pl.fence("vector")
            ve(lambda e: e.tensor_tensor(hl_t, htile_a, hh_t, ALU.subtract))
            sp_tok = pl.fence("vector")
            pl.wait("tensor", sp_tok)
            transpose_to(hTh4[:, :, tau, :],
                         [hh_t[:, 128 * dc:128 * (dc + 1)] for dc in range(ND)],
                         0, src3d=True)
            hT_tok = transpose_to(hTl4[:, :, tau, :],
                                  [hl_t[:, 128 * dc:128 * (dc + 1)] for dc in range(ND)],
                                  1, src3d=True)

        if with_rel:
            rk_sb = wpb[0]
            pl.dma("sync", rk_sb[:, 0:64], relk.ap()[0:128, :], ch="rk")
            t2 = pl.dma("sync", rk_sb[:, 64:128], relk.ap()[128:256, :], ch="rk")
            t3 = pl.dma("sync", rkt32[0:64, 256:257],
                        bass.AP(relk, 256 * HD, [[1, 64], [1, 1]]), ch="rk")
            pl.wait("vector", t3)
            pl.fence("vector")
            ve(lambda e: e.tensor_scalar(rk_sb[:, 128:256], rk_sb[:, 0:128],
                                         0.125, None, ALU.mult))
            ve(lambda e: e.tensor_scalar(rkt32[0:64, 256:257],
                                         rkt32[0:64, 256:257], 0.125, None, ALU.mult))
            sctok = pl.fence("vector")
            pl.wait("tensor", sctok)
            bank_wait("tensor", "pb0")
            pl.emit("tensor", lambda e: e.transpose(
                pb[0][0:64, 0:128], rk_sb[:, 128:192], id_f32))
            tt1 = pl.emit("tensor", lambda e: e.transpose(
                pb[0][0:64, 128:256], rk_sb[:, 192:256], id_f32))
            tt1_tok = pl.mark(tt1)
            pl.wait("vector", tt1_tok)
            ev = ve(lambda e: e.tensor_copy(rkt32[0:64, 0:256], pb[0][0:64, 0:256]))
            bank_set("pb0", pl.mark(ev))
            pl.fence("vector")
            ve(lambda e: e.tensor_tensor(rkt32[0:64, 260:517], rkt32[0:64, 0:257],
                                         rkt32[0:64, 0:257], ALU.mult))
            sqtok = pl.fence("vector")
            pl.wait("tensor", sqtok)
            bank_wait("tensor", "pb0")
            nmm = mm(pb[0][0:1, 0:257], onescol[0:64, :], rkt32[0:64, 260:517],
                     True, True)
            pl.wait("vector", pl.mark(nmm))
            rmv = ve(lambda e: e.reduce_max(rkm, pb[0][0:1, 0:257], axis=AX))
            rmvt = pl.mark(rmv)
            bank_set("pb0", rmvt)
            pl.fence("vector")
            pl.wait("scalar", rmvt)
            sq2 = sc(lambda e: e.activation(rkm, rkm, ACT.Sqrt))
            pl.wait("vector", pl.mark(sq2))
            pl.fence("vector")
            ve(lambda e: e.tensor_copy(rkx_h[0:64, 127:384], rkt32[0:64, 0:257]))
            ve(lambda e, s=rkt32[0:64, 0:1].to_broadcast((64, 127)):
               e.tensor_copy(rkx_h[0:64, 0:127], s))
            ve(lambda e, s=rkt32[0:64, 256:257].to_broadcast((64, 128)):
               e.tensor_copy(rkx_h[0:64, 384:512], s))
            pl.fence("vector")
            ve(lambda e: e.tensor_tensor(rkx_l[0:64, 127:384], rkt32[0:64, 0:257],
                                         rkx_h[0:64, 127:384], ALU.subtract))
            ve(lambda e, s=rkt32[0:64, 0:1].to_broadcast((64, 127)),
               h=rkx_h[0:64, 0:127]:
               e.tensor_tensor(rkx_l[0:64, 0:127], s, h, ALU.subtract))
            ve(lambda e, s=rkt32[0:64, 256:257].to_broadcast((64, 128)),
               h=rkx_h[0:64, 384:512]:
               e.tensor_tensor(rkx_l[0:64, 384:512], s, h, ALU.subtract))
            rkx_tok0 = pl.fence("vector")
            pl.wait("gpsimd", rkx_tok0)
            pl.dma("gpsimd", rkx_h[64:128, :], rkx_h[0:64, :], ch="rkrep")
            rep_tok = pl.dma("gpsimd", rkx_l[64:128, :], rkx_l[0:64, :], ch="rkrep")
            rkx_tok = rep_tok

            rv_sb = wpb[1]
            pl.dma("sync", rv_sb[:, 0:64], relv.ap()[0:128, :], ch="rv")
            v2 = pl.dma("sync", rv_sb[:, 64:128], relv.ap()[128:256, :], ch="rv")
            v3 = pl.dma("sync", rv_sb[0:1, 128:192], relv.ap()[256:257, :], ch="rv")
            pl.wait("tensor", v3)
            bank_wait("tensor", "pb1")
            rmm = mm(pb[1][:, 0:64], onesrow, rv_sb[0:1, 128:192], True, True)
            pl.wait("vector", pl.mark(rmm))
            rrv = ve(lambda e: e.tensor_copy(r256rep, pb[1][:, 0:64]))
            bank_set("pb1", pl.mark(rrv))
            pl.wait("vector", v2)
            pl.fence("vector")
            ve(lambda e: e.tensor_tensor(rvd_f, rv_sb[0:1, 0:64],
                                         r256rep[0:1, :], ALU.subtract))
            ve(lambda e: e.tensor_tensor(rv_sb[:, 192:256], rv_sb[:, 0:64],
                                         r256rep, ALU.subtract))
            ve(lambda e: e.tensor_tensor(rv_sb[:, 256:320], rv_sb[:, 64:128],
                                         r256rep, ALU.subtract))
            pl.fence("vector")
            ve(lambda e: e.tensor_scalar(rv_sb[0:1, 192:256], rv_sb[0:1, 192:256],
                                         0.0, None, ALU.mult))
            pl.fence("vector")
            ve(lambda e: e.tensor_copy(rvp_h[:, 0:64], rv_sb[:, 192:256]))
            ve(lambda e: e.tensor_copy(rvp_h[:, 64:128], rv_sb[:, 256:320]))
            pl.fence("vector")
            ve(lambda e: e.tensor_tensor(rvp_l[:, 0:64], rv_sb[:, 192:256],
                                         rvp_h[:, 0:64], ALU.subtract))
            ve(lambda e: e.tensor_tensor(rvp_l[:, 64:128], rv_sb[:, 256:320],
                                         rvp_h[:, 64:128], ALU.subtract))
            bank_wait("tensor", "pb1")
            mm(pb[1][0:64, 0:1], rv_sb[0:1, 128:192], onesrow[0:1, 0:1], True, True)
            cmm2 = mm(pb[1][64:128, 0:1], rv_sb[0:1, 128:192], onesrow[0:1, 0:1],
                      True, True)
            pl.wait("vector", pl.mark(cmm2))
            rcv = ve(lambda e: e.tensor_copy(rvc32, pb[1][:, 0:1]))
            bank_set("pb1", pl.mark(rcv))
            pl.fence("vector")
            ve(lambda e: e.tensor_copy(rvcat_h, rvc32))
            pl.fence("vector")
            ve(lambda e: e.tensor_tensor(rvcat_l, rvc32, rvcat_h, ALU.subtract))
            rv_tok = pl.fence("vector")
        else:
            ve(lambda e: e.tensor_scalar(rkm, onescol[0:1, :], 0.0, None, ALU.mult))
            ve(lambda e: e.tensor_scalar(rvc32, onescol, 0.0, None, ALU.mult))
            pl.fence("vector")
            ve(lambda e: e.tensor_copy(rvcat_h, rvc32))
            ve(lambda e: e.tensor_copy(rvcat_l, rvc32))
            rkx_tok = pl.fence("vector")
            rv_tok = rkx_tok

        ring_cons = [None] * 6
        plo_tok = [None] * 6
        tri_done = [None] * 4
        tilestate = [None]
        pf_cons = [None, None]
        qd_cons = [None, None]
        qd_tok = [None, None]
        rot_gate = [None]
        rotK_tok = [None]
        att_wcons = [None, None, None, None]
        att_wtoks = []
        eW_tok = [None, None]
        eR_tok = [None, None]
        psW_tok = [None, None]
        psR_tok = [None, None]
        rs_ready = [None] * 4
        rs_cons = [None] * 4
        bm_tok = [None] * 4
        stp_cons = [None] * 4
        psk_cons = [None]
        c0_tok = [None]

        bm_pend = [None] * 4

        def emit_strips(h, po, hh, blk, qts):
            def p2(j):
                pl.wait("tensor", bm_pend[j])
                if j == 0:
                    bank_wait("tensor", "pt0")
                bmt = pl.emit("tensor", lambda e, c=128 * j, s=tmp4[:, 4 * j:4 * j + 4]:
                              e.transpose(pt[0][0:4, c:c + 128], s, id_b))
                pl.wait("vector", pl.mark(bmt))
                bme = ve(lambda e, d=bm4[j], c=128 * j:
                         e.tensor_copy(d, pt[0][0:4, c:c + 128]))
                bm_tok[j] = pl.mark(bme)
                if j == 3:
                    bank_set("pt0", bm_tok[j])

            def p1(j, qt):
                es = qt % 2
                rbk = "pb2" if j % 2 == 0 else "pb0"
                rb = pb[2] if j % 2 == 0 else pb[0]
                bank_wait("tensor", rbk)
                pl.wait("tensor", rkx_tok)
                mm(rb, Qh[po:po + 64, 128 * qt:128 * qt + 128],
                   rkx_h[po:po + 64, :], True, False)
                mm(rb, Qh[po:po + 64, 128 * qt:128 * qt + 128],
                   rkx_l[po:po + 64, :], False, False)
                rmk = mm(rb, Ql[po:po + 64, 128 * qt:128 * qt + 128],
                         rkx_h[po:po + 64, :], False, True)
                rtok = pl.mark(rmk)
                pl.wait("vector", rtok)
                pl.fence("vector")
                ve(lambda e, s=rb[:, 127:384:256], d=tmp4[:, 4 * j:4 * j + 4:2]:
                   e.tensor_copy(d, s))
                pl.fence("vector")
                ve(lambda e, s=rb[:, 127:384:256],
                   hsb=tmp4[:, 4 * j:4 * j + 4:2], d=tmp4[:, 4 * j + 1:4 * j + 4:2]:
                   e.tensor_tensor(d, s, hsb, ALU.subtract))
                bm_pend[j] = pl.fence("vector")
                pl.wait("vector", eW_tok[es])
                ve(lambda e, d=e_hi[es][:, 0:511]: e.tensor_copy(
                    d, rb[:, 0:511]))
                pl.fence("vector")
                elo_v = ve(lambda e, d=e_lo[es][:, 0:511], hsb=e_hi[es][:, 0:511]:
                           e.tensor_tensor(d, rb[:, 0:511], hsb, ALU.subtract))
                ebuild = pl.mark(elo_v)
                bank_set(rbk, ebuild)
                pl.wait("gpsimd", ebuild)
                pl.wait("gpsimd", eR_tok[es])
                pl.dma("gpsimd", bass.AP(e_h, (es * 2 + 0) * 128 * 511,
                                         [[511, 128], [1, 511]]),
                       e_hi[es][:, 0:511], ch=f"ehW{es}")
                wtk = pl.dma("gpsimd", bass.AP(e_h, (es * 2 + 1) * 128 * 511,
                                               [[511, 128], [1, 511]]),
                             e_lo[es][:, 0:511], ch=f"ehW{es}")
                eW_tok[es] = wtk
                pl.wait("sync", wtk)
                pl.wait("sync", rs_cons[j])
                pl.dma("sync", rs_h[j], bass.AP(
                    e_h, (es * 2 + 0) * 128 * 511 + 127, [[510, 128], [1, 384]]),
                    ch=f"ehR{es}")
                rtk = pl.dma("sync", rs_l[j], bass.AP(
                    e_h, (es * 2 + 1) * 128 * 511 + 127, [[510, 128], [1, 384]]),
                    ch=f"ehR{es}")
                eR_tok[es] = rtk
                rs_ready[j] = rtk

            for j, qt in enumerate(qts):
                p1(j, qt)
                p2(j)

        def emit_tiles(h, po, hh, blk, par, qts, with_rel, pend, carry_out):
            cols = slice(512 * blk, 512 * (blk + 1))
            bank = pb[4 + par]
            bkey = f"pb{4 + par}"
            csb = 64 if po == 0 else 32
            bst = {"first": True, "ztok": None}

            def bank_mm(rows, lhsT, rhs, stop=False):
                st = bst["first"]
                if st:
                    pl.wait("tensor", bst["ztok"])
                bst["first"] = False
                return mm(bank[rows[0]:rows[1], :] if rows[2] is None else
                          bank[rows[0]:rows[1], rows[2]:rows[3]],
                          lhsT, rhs, st, stop, skip=True)

            def psk_part2(qt):
                j = qt - 4 * blk
                psk_h, psk_l = psk_sl[j]
                pl.wait("tensor", psk_rdy[j])
                bank_wait("tensor", "pt0")
                pl.emit("tensor", lambda e, a=psk_h: e.transpose(
                    pt[0][:, 0:128], a[:, 0:128], id_b))
                pl.emit("tensor", lambda e, a=psk_h: e.transpose(
                    pt[0][:, 128:256], a[:, 128:256], id_b))
                pl.emit("tensor", lambda e, a=psk_l: e.transpose(
                    pt[0][:, 256:384], a[:, 0:128], id_b))
                ptk = pl.emit("tensor", lambda e, a=psk_l: e.transpose(
                    pt[0][:, 384:512], a[:, 128:256], id_b))
                pl.wait("vector", pl.mark(ptk))
                pl.fence("vector")
                ve(lambda e: e.tensor_copy(pskT_h, pt[0][:, 0:256]))
                pve = ve(lambda e: e.tensor_copy(pskT_l, pt[0][:, 256:512]))
                pskt = pl.mark(pve)
                bank_set("pt0", pskt)
                psk_cons[0] = pskt
                pl.wait("tensor", pskt)
                pl.wait("tensor", rv_tok)
                for ck in range(2):
                    bank_mm((po, po + 64, 128 * j, 128 * j + 128),
                            rvp_h[:, 64 * ck:64 * ck + 64],
                            pskT_h[:, 128 * ck:128 * ck + 128])
                    bank_mm((po, po + 64, 128 * j, 128 * j + 128),
                            rvp_l[:, 64 * ck:64 * ck + 64],
                            pskT_h[:, 128 * ck:128 * ck + 128])
                    bank_mm((po, po + 64, 128 * j, 128 * j + 128),
                            rvp_h[:, 64 * ck:64 * ck + 64],
                            pskT_l[:, 128 * ck:128 * ck + 128])

            bst["psk2"] = psk_part2
            tilestate[0] = bst
            base = 4 * blk
            far = [(base + 5 + k) % NT for k in range(10)]
            diag = [(base + 15 + k) % NT for k in range(6)]
            order = far[0:3] + diag + far[3:10]
            done_tiles = set()
            strips_emitted = set()
            strip_ready = {}
            psk_pend = {}
            slot_of = {}
            sbanks = (0, 1, 3)
            vlo, vhi = (0, 80) if hh == 0 else (32, 160)
            ro, rh = (0, 80) if hh == 0 else (0, 128)
            for oi in range(NT):
                tt = order[oi]
                if pend:
                    for sl_, fn_ in pend:
                        if sl_ == oi:
                            fn_()
                sb_ = pb[sbanks[oi % 3]]
                sbk = f"pb{sbanks[oi % 3]}"
                bank_wait("tensor", sbk)
                if oi == 0:
                    bank_wait("tensor", bkey)
                    bank_wait("vector", bkey)
                    pl.fence("vector")
                    zb = ve(lambda e: e.tensor_scalar(
                        bank[:], Qh[:, 0:512], 0.0, None, ALU.mult))
                    bst["ztok"] = pl.mark(zb)
                if oi == 0:
                    pl.wait("tensor", qd_tok[blk % 2])
                    pl.wait("tensor", rotK_tok[0])
                mm(sb_, Kcat3[:, hh, 128 * tt:128 * tt + 128],
                   qdup2[blk % 2], True, False)
                if with_rel:
                    for j, qt in enumerate(qts):
                        dqt = tt - qt
                        if -1 <= dqt <= 1:
                            pl.wait("tensor", rs_ready[j])
                            co = 128 * (dqt + 1)
                            mm(sb_[:, 128 * j:128 * j + 128],
                               rs_h[j][:, co:co + 128], id_b, False, False)
                            rmm2 = mm(sb_[:, 128 * j:128 * j + 128],
                                      rs_l[j][:, co:co + 128], id_b, False, False)
                            rs_cons[j] = pl.mark(rmm2)
                        elif dqt < -1:
                            pl.wait("tensor", bm_tok[j])
                            mm(sb_[:, 128 * j:128 * j + 128],
                               sel4[0:4, 0:128], bm4[j], False, False)
                        else:
                            pl.wait("tensor", bm_tok[j])
                            mm(sb_[:, 128 * j:128 * j + 128],
                               sel4[0:4, 128:256], bm4[j], False, False)
                lmk = mm(sb_, Kl[po:po + 64, 128 * tt:128 * tt + 128],
                         Qh[po:po + 64, cols], False, True)
                smk = pl.mark(lmk)
                s6 = oi % 6
                slot_of[tt] = s6
                pl.wait("scalar", smk)
                pl.wait("scalar", pf_cons[oi % 2])
                petok = pl.mark(sc(lambda e, d=pf32[oi % 2], b=sb_[:],
                                   c=negB2[:, hh:hh + 1]:
                                   e.activation(d, b, ACT.Exp, bias=c)))
                bank_set(sbk, petok)
                pl.wait("scalar", ring_cons[s6])
                pl.fence("scalar")
                phi_t = pl.mark(sc(lambda e, d=ring_h[s6], p=pf32[oi % 2]:
                                   e.activation(d, p, ACT.Copy)))
                pl.wait("gpsimd", phi_t)
                pl.fence("gpsimd")
                pltok = pl.mark(pl.emit("gpsimd",
                                lambda e, d=ring_l[s6], p=pf32[oi % 2],
                                hsb=ring_h[s6]:
                                e.tensor_tensor(d, p, hsb, ALU.subtract)))
                plo_tok[s6] = pltok
                pf_cons[oi % 2] = pltok
                lagged = []
                if oi >= 5:
                    lagged.append(order[oi - 5])
                if oi == NT - 1 and not carry_out:
                    lagged += [order[NT - 5 + k3] for k3 in range(5)]
                for lt in lagged:
                    l6 = slot_of[lt]
                    pl.wait("tensor", plo_tok[l6])
                    bank_mm((ro, rh, None, None),
                            Vx_h3[:, lt, vlo:vhi], ring_h[l6])
                    bank_mm((ro, rh, None, None),
                            Vx_h3[:, lt, vlo:vhi], ring_l[l6])
                    pvk = bank_mm((ro, rh, None, None),
                                  Vx_l3[:, lt, vlo:vhi], ring_h[l6],
                                  stop=(oi == NT - 1 and lt == lagged[-1]
                                        and not with_rel and not carry_out))
                    ring_cons[l6] = pl.mark(pvk)
                done_tiles.add(tt)
                if with_rel:
                    for qt in qts:
                        if qt in strip_ready or qt in strips_emitted:
                            continue
                        need = {x for x in (qt - 1, qt, qt + 1) if 0 <= x < NT}
                        if need <= done_tiles:
                            strip_ready[qt] = oi
                    strip_qts = [qt for qt, ro in sorted(strip_ready.items())
                                 if ro + 3 <= oi or oi == NT - 1]
                    for qt in strip_qts:
                        strips_emitted.add(qt)
                        del strip_ready[qt]
                        j = qt - 4 * blk
                        jj = j
                        pl.wait("vector", stp_cons[j])
                        if 0 < qt < NT - 1:
                            bank_wait("tensor", "pt1")
                            tl = None
                            for kk, st_t in enumerate((qt - 1, qt, qt + 1)):
                                s6b = slot_of[st_t]
                                pl.wait("tensor", plo_tok[s6b])
                                pl.emit("tensor", lambda e, c=256 * kk,
                                        s=ring_h[s6b][:, 128 * jj:128 * jj + 128]:
                                        e.transpose(pt[1][:, c:c + 128], s, id_b))
                                tl = pl.emit("tensor", lambda e, c=256 * kk + 128,
                                             s=ring_l[s6b][:, 128 * jj:128 * jj + 128]:
                                             e.transpose(pt[1][:, c:c + 128], s, id_b))
                            pl.wait("vector", pl.mark(tl))
                            sev = None
                            for kk in range(3):
                                ve(lambda e, d=stp_h[j][:, 128 * kk:128 * kk + 128],
                                   c=256 * kk: e.tensor_copy(d, pt[1][:, c:c + 128]))
                                sev = ve(lambda e, d=stp_l[j][:, 128 * kk:128 * kk + 128],
                                         c=256 * kk + 128:
                                         e.tensor_copy(d, pt[1][:, c:c + 128]))
                            bank_set("pt1", pl.mark(sev))
                        else:
                            for kk, st_t in enumerate((qt - 1, qt, qt + 1)):
                                if 0 <= st_t < NT:
                                    s6b = slot_of[st_t]
                                    bank_wait("tensor", "pt1")
                                    pl.wait("tensor", plo_tok[s6b])
                                    pl.emit("tensor", lambda e, s=ring_h[s6b][:, 128 * jj:128 * jj + 128]:
                                            e.transpose(pt[1][:, 0:128], s, id_b))
                                    tl = pl.emit("tensor", lambda e, s=ring_l[s6b][:, 128 * jj:128 * jj + 128]:
                                                 e.transpose(pt[1][:, 128:256], s, id_b))
                                    pl.wait("vector", pl.mark(tl))
                                    ve(lambda e, d=stp_h[j][:, 128 * kk:128 * kk + 128]:
                                       e.tensor_copy(d, pt[1][:, 0:128]))
                                    sev = ve(lambda e, d=stp_l[j][:, 128 * kk:128 * kk + 128]:
                                             e.tensor_copy(d, pt[1][:, 128:256]))
                                    bank_set("pt1", pl.mark(sev))
                                else:
                                    pl.fence("vector")
                                    ve(lambda e, d=stp_h[j][:, 128 * kk:128 * kk + 128]:
                                       e.tensor_scalar(d, id_b, 0.0, None, ALU.mult))
                                    sev = ve(lambda e, d=stp_l[j][:, 128 * kk:128 * kk + 128]:
                                             e.tensor_scalar(d, id_b, 0.0, None, ALU.mult))
                                    pl.mark(sev)
                        stp_cons[j] = pl.fence("vector")
                        t4 = tri4p[par]
                        tcs = 128 + 8 * par
                        if qt == 0:
                            ve(lambda e, d=t4[:, 0:1]: e.tensor_scalar(
                                d, onescol, 0.0, None, ALU.mult))
                        else:
                            ve(lambda e, d=t4[:, j:j + 1], s=stp_h[j][:, 0:128]:
                               e.scalar_tensor_tensor(nsc[:, 0:64].bitcast(BF16),
                                                      s, 1.0, lt_b, ALU.mult,
                                                      ALU.mult, accum_out=d))
                            ve(lambda e, d=nsc[:, tcs + j:tcs + j + 1],
                               s=stp_l[j][:, 0:128]:
                               e.scalar_tensor_tensor(nsc[:, 64:128].bitcast(BF16),
                                                      s, 1.0, lt_b, ALU.mult,
                                                      ALU.mult, accum_out=d))
                        pl.fence("vector")
                        if qt != 0:
                            ve(lambda e, d=t4[:, j:j + 1], a=nsc[:, tcs + j:tcs + j + 1]:
                               e.tensor_tensor(d, d, a, ALU.add))
                        tri_done[j] = pl.fence("vector")
                        ps = j % 2
                        pl.wait("gpsimd", stp_cons[j])
                        pl.wait("gpsimd", psR_tok[ps])
                        pl.dma("gpsimd",
                               bass.AP(s_h, (ps * 2 + 0) * 128 * 384,
                                       [[384, 128], [1, 384]]),
                               stp_h[j], ch=f"psW{ps}")
                        pwt = pl.dma("gpsimd",
                                     bass.AP(s_h, (ps * 2 + 1) * 128 * 384,
                                             [[384, 128], [1, 384]]),
                                     stp_l[j], ch=f"psW{ps}")
                        psW_tok[ps] = pwt
                        stp_cons[j] = pwt
                        pl.wait("sync", pwt)
                        pl.wait("sync", psk_cons[0])
                        pl.dma("sync", psk_sl[j][0], bass.AP(
                            s_h, (ps * 2 + 0) * 128 * 384,
                            [[385, 128], [1, 256]]), ch=f"psR{ps}")
                        prt = pl.dma("sync", psk_sl[j][1], bass.AP(
                            s_h, (ps * 2 + 1) * 128 * 384,
                            [[385, 128], [1, 256]]), ch=f"psR{ps}")
                        psR_tok[ps] = prt
                        psk_rdy[j] = prt
                        psk_pend[qt] = oi
                if with_rel:
                    for qt in list(psk_pend):
                        if psk_pend[qt] + 6 <= oi:
                            bst["psk2"](qt)
                            del psk_pend[qt]
            if pend:
                for sl_, fn_ in pend:
                    if sl_ >= NT:
                        fn_()
            carries = []
            if carry_out:
                for k2 in range(5):
                    lt_c = order[NT - 5 + k2]
                    l6_c = slot_of[lt_c]
                    tok_c = plo_tok[l6_c]

                    def cfn(lt=lt_c, l6=l6_c, tok=tok_c, last=(k2 == 0)):
                        pl.wait("tensor", tok)
                        bank_mm((ro, rh, None, None),
                                Vx_h3[:, lt, vlo:vhi], ring_h[l6])
                        bank_mm((ro, rh, None, None),
                                Vx_h3[:, lt, vlo:vhi], ring_l[l6])
                        pvk2 = bank_mm((ro, rh, None, None),
                                       Vx_l3[:, lt, vlo:vhi], ring_h[l6],
                                       stop=(last and not with_rel))
                        ring_cons[l6] = pl.mark(pvk2)
                    carries.append(((NT - 5 + k2) % 6, cfn))
                if with_rel:
                    for ip, qt in enumerate(sorted(psk_pend)):
                        def pfn(qt=qt):
                            bst["psk2"](qt)
                        carries.append(((1 + ip) % 4, pfn))
                    psk_pend.clear()
            elif with_rel:
                for qt in sorted(psk_pend):
                    bst["psk2"](qt)
                psk_pend.clear()
            return carries

        def make_post(h, po, hh, blk, par, qts, with_rel, rsnap, bst):
            bank = pb[4 + par]
            bkey = f"pb{4 + par}"
            csb = 64 if po == 0 else 32

            S = {}

            def st0():
                pl.wait("vector", rsnap[(NT - 1) % 6])
                pl.wait("vector", rsnap[(NT - 5) % 6])
                pl.fence("vector")
                csv = ve(lambda e: e.tensor_copy(cs_sb, bank[csb:csb + 16, :]))
                cst = pl.mark(csv)
                pl.wait("tensor", cst)
                bank_wait("tensor", "pb2")
                pfm = mm(pb[2][0:5, :], pre32_b[:, 5 * blk:5 * blk + 5], cs_sb,
                         True, True)
                pl.wait("vector", pl.mark(pfm))
                pfe = ve(lambda e: e.tensor_copy(pfx_sb, pb[2][0:5, :]))
                S["pfe"] = pl.mark(pfe)
                bank_set("pb2", S["pfe"])
                pl.wait("gpsimd", S["pfe"])
                ldm = pl.dma("gpsimd", lrec, pfx_sb[4:5, :], ch="c0")
                pl.wait("vector", ldm)
                pl.fence("vector")
                ve(lambda e: e.reciprocal(lrec, lrec))
                rv_t = pl.fence("vector")
                pl.wait("gpsimd", rv_t)
                S["lr"] = pl.dma("gpsimd", lflat, lrec, ch="c0")

            def st1():
                pfe_tok = S["pfe"]
                if with_rel:
                    for j in range(4):
                        pl.wait("tensor", tri_done[j])
                    bank_wait("tensor", "pb2")
                    trt = pl.emit("tensor", lambda e: e.transpose(
                        pb[2][0:4, 0:128], tri4p[par], id_f32))
                    pl.wait("vector", pl.mark(trt))
                    pl.fence("vector")
                    tre = ve(lambda e: e.tensor_copy(triT_s, pb[2][0:4, 0:128]))
                    tre_t = pl.mark(tre)
                    bank_set("pb2", tre_t)
                    pl.wait("gpsimd", tre_t)
                    pl.wait("gpsimd", pfe_tok)
                    pl.wait("gpsimd", c0_tok[0])
                    pl.dma("gpsimd", c0flat,
                           bass.AP(arena, pfx_sb.offset, [[ACOLS + 128, 4], [1, 128]]),
                           ch="c0")
                    c0d = pl.dma("gpsimd", trflat,
                                 bass.AP(arena, triT_s.offset, [[ACOLS, 4], [1, 128]]),
                                 ch="c0")
                    pl.wait("vector", c0d)
                    pl.fence("vector")
                    ve(lambda e: e.tensor_tensor(c0flat, c0flat, trflat, ALU.add))
                    c0t = pl.fence("vector")
                    c0_tok[0] = c0t

            def st2():
                if with_rel:
                    pl.wait("tensor", c0_tok[0])
                    for j in range(4):
                        mm(bank[po:po + 64, 128 * j:128 * j + 128],
                           rvd_f, c0flat[0:1, 128 * j:128 * j + 128],
                           False, j == 3, skip=True)
                pl.wait("tensor", S["lr"])
                bank_wait("tensor", "pb2")
                lrm = mm(pb[2], onesrow, lflat, True, True)
                S["lrt"] = pl.mark(lrm)

            def st3():
                pl.wait("vector", S["lrt"])
                pl.wait("vector", rsnap[(NT - 1) % 6])
                pl.wait("vector", rsnap[(NT - 5) % 6])
                pl.fence("vector")
                ve(lambda e, d=nsc[po:po + 64, :]: e.tensor_copy(
                    d, bank[po:po + 64, :]))
                pl.fence("vector")
                nv = ve(lambda e, d=nsc[po:po + 64, :]: e.tensor_tensor(
                    d, d, pb[2][po:po + 64, :], ALU.mult))
                ntok = pl.mark(nv)
                bank_set("pb2", ntok)
                bank_set(bkey, ntok)
                ns2 = (blk % 2) * 2
                pl.wait("vector", att_wcons[ns2])
                pl.fence("vector")
                ve(lambda e, d=nhl[ns2][po:po + 64, :], s=nsc[po:po + 64, :]:
                   e.tensor_copy(d, s))
                pl.fence("vector")
                ve(lambda e, d=nhl[ns2 + 1][po:po + 64, :],
                   s=nsc[po:po + 64, :], hb=nhl[ns2][po:po + 64, :]:
                   e.tensor_tensor(d, s, hb, ALU.subtract))
                awt = pl.fence("vector")
                pl.wait("gpsimd", awt)
                row = (h // 2) * 128 + 64 * (h % 2)
                pl.dma("gpsimd", bass.AP(att_hbm, row * T + 512 * blk,
                                         [[T, 64], [1, 512]]),
                       nhl[ns2][po:po + 64, :], ch=f"atw{ns2}")
                att_wcons[ns2] = pl.dma(
                    "gpsimd", bass.AP(att_hbm, (ND * 128 + row) * T + 512 * blk,
                                      [[T, 64], [1, 512]]),
                    nhl[ns2 + 1][po:po + 64, :], ch=f"atw{ns2}")
                att_wtoks.append(att_wcons[ns2])
            return [st0, st1, st2, st3]

        par_ctr = [0]

        def attn_head(h, po, hh, with_rel, pending):
            for blk in range(4):
                qts = [4 * blk + j for j in range(4)]
                par = par_ctr[0] % 2
                par_ctr[0] += 1
                b2 = blk % 2
                qsrc = Qh[64 * hh:64 * hh + 64, 512 * blk:512 * (blk + 1)]
                pl.wait("gpsimd", qd_cons[b2])
                pl.wait("gpsimd", rot_gate[0])
                pl.dma("gpsimd", qdup2[b2][0:64, :], qsrc, ch="rot0")
                qd_tok[b2] = pl.dma("gpsimd", qdup2[b2][64:128, :], qsrc,
                                    ch="rot0")
                if with_rel:
                    emit_strips(h, po, hh, blk, qts)
                cross = (hh == 1 and blk == 3)
                carries = emit_tiles(h, po, hh, blk, par, qts, with_rel,
                                     pending, not cross)
                qd_cons[b2] = pl.mark(("tensor", pl.last_real["tensor"]))
                stages = make_post(h, po, hh, blk, par, qts, with_rel,
                                   ring_cons, tilestate[0])
                pending = carries + [(5, stages[0]), (7, stages[1]),
                                     (10, stages[2]), (13, stages[3])]
            return pending

        wo_out = []

        bo_cons = [None]

        def wo_phase():
            barr = [psW_tok[0], psW_tok[1], psR_tok[0], psR_tok[1],
                    eW_tok[0], eW_tok[1], eR_tok[0], eR_tok[1],
                    rotK_tok[0], qd_tok[0], qd_tok[1]] +                    [ring_cons[i] for i in range(6)] +                    [pl.mark((e_, pl.last_real[e_]))
                    for e_ in ("tensor", "vector", "scalar")]
            for t_ in barr:
                for e_ in ("sync", "gpsimd", "tensor", "vector", "scalar"):
                    pl.wait(e_, t_)
            wp2 = [None, None]
            wrv_toks = []
            for oc in range(8):
                bslot = oc % 2
                if wp2[bslot] is not None:
                    pl.wait("sync", wp2[bslot])
                wtok = pl.dma("sync", wpb[bslot],
                              W["wo"].ap()[128 * oc:128 * (oc + 1), :], ch=f"wp{bslot}")
                pl.wait("vector", wtok)
                pl.fence("vector")
                ve(lambda e, b=wpb[bslot]: e.tensor_copy(hh_t, b))
                pl.fence("vector")
                spv = ve(lambda e, b=wpb[bslot]: e.tensor_tensor(
                    hl_t, b, hh_t, ALU.subtract))
                wp2[bslot] = pl.mark(spv)
                sp_tok = pl.fence("vector")
                pl.wait("tensor", sp_tok)
                transpose_to(WoT8[oc][0].rearrange("p (c n) -> p c n", n=128),
                             [hh_t[:, 128 * dc:128 * (dc + 1)] for dc in range(ND)],
                             0, src3d=True)
                wotok = transpose_to(WoT8[oc][1].rearrange("p (c n) -> p c n", n=128),
                                     [hl_t[:, 128 * dc:128 * (dc + 1)] for dc in range(ND)],
                                     1, src3d=True)
                WoTh3 = WoT8[oc][0].rearrange("p (c n) -> p c n", n=128)
                WoTl3 = WoT8[oc][1].rearrange("p (c n) -> p c n", n=128)
                pl.wait("tensor", wotok)
                pl.wait("tensor", rv_tok)
                bank_wait("tensor", "pb5")
                for dc in range(ND):
                    mm(pb[5][:, 0:1], WoTh3[:, dc], rvcat_h, dc == 0, False)
                    mm(pb[5][:, 0:1], WoTh3[:, dc], rvcat_l, False, False)
                    wrm = mm(pb[5][:, 0:1], WoTl3[:, dc], rvcat_h, False,
                             dc == ND - 1)
                pl.wait("sync", bo_cons[0])
                bctok = pl.dma("sync", nsc[:, 0:1],
                               bass.AP(Bi["bo"], 128 * oc, [[1, 128], [1, 1]]),
                               ch="boc")
                pl.wait("vector", pl.mark(wrm))
                pl.wait("vector", bctok)
                pl.fence("vector")
                bev = ve(lambda e, d=boext8[:, oc:oc + 1]: e.tensor_tensor(
                    d, pb[5][:, 0:1], nsc[:, 0:1], ALU.add))
                bo_cons[0] = pl.mark(bev)
                bank_set("pb5", bo_cons[0])
            ycons_a = [None, None]
            attb_cons = None
            for blk in range(4):
                cols = slice(512 * blk, 512 * (blk + 1))
                if attb_cons is not None:
                    pl.wait("sync", attb_cons)
                for t_ in att_wtoks[-4:]:
                    pl.wait("sync", t_)
                pl.dma("sync", attb_h3, bass.AP(
                    att_hbm, 512 * blk, [[T, 128], [128 * T, ND], [1, 512]]),
                    ch="atr")
                abtok = pl.dma("sync", attb_l3, bass.AP(
                    att_hbm, ND * 128 * T + 512 * blk,
                    [[T, 128], [128 * T, ND], [1, 512]]), ch="atr")
                pl.wait("tensor", abtok)
                ao_toks = []
                for oc in range(8):
                    WoTh3 = WoT8[oc][0].rearrange("p (c n) -> p c n", n=128)
                    WoTl3 = WoT8[oc][1].rearrange("p (c n) -> p c n", n=128)
                    bk_ = oc % 2
                    bank_wait("tensor", f"pb{bk_}")
                    for dc in range(ND):
                        mm(pb[bk_], WoTh3[:, dc], attb_h3[:, dc], dc == 0, False)
                        mm(pb[bk_], WoTh3[:, dc], attb_l3[:, dc], False, False)
                        prm = mm(pb[bk_], WoTl3[:, dc], attb_h3[:, dc],
                                 False, dc == ND - 1)
                    prt = pl.mark(prm)
                    attb_cons = prt
                    pl.wait("vector", prt)
                    pl.fence("vector")
                    aov = ve(lambda e, d=aosc[oc], b=pb[bk_][:], c=boext8[:, oc:oc + 1]:
                             e.tensor_scalar(d, b, c, None, ALU.add))
                    bank_set(f"pb{bk_}", pl.mark(aov))
                    ao_toks.append(pl.mark(aov))
                for jt in range(4):
                    tau = 4 * blk + jt
                    yb = ytile_a[jt % 2]
                    if ycons_a[jt % 2] is not None:
                        pl.wait("gpsimd", ycons_a[jt % 2])
                    xt2 = pl.dma("gpsimd", yb,
                                 src_dram.ap()[128 * tau:128 * (tau + 1), :],
                                 ch=f"ya{jt % 2}")
                    for oc in range(8):
                        pl.wait("tensor", ao_toks[oc])
                        bank_wait("tensor", f"pb{2 + oc % 2}")
                        trm = pl.emit("tensor", lambda e, s=aosc[oc][:, 128 * jt:128 * jt + 128],
                                      b=pb[2 + oc % 2][:, 0:128]:
                                      e.transpose(b, s, id_f32))
                        trt2 = pl.mark(trm)
                        pl.wait("vector", trt2)
                        pl.wait("vector", xt2)
                        pl.fence("vector")
                        yv = ve(lambda e, d=yb[:, 128 * oc:128 * oc + 128],
                                b=pb[2 + oc % 2][:, 0:128]:
                                e.tensor_tensor(d, d, b, ALU.add))
                        bank_set(f"pb{2 + oc % 2}", pl.mark(yv))
                    ytk = pl.fence("vector")
                    pl.wait("gpsimd", ytk)
                    stk = pl.dma("gpsimd", dst_dram.ap()[128 * tau:128 * (tau + 1), :],
                                 yb, ch=f"ys{jt % 2}")
                    ycons_a[jt % 2] = stk
                    wo_out.append(stk)

        pending_post = [None]
        if with_rel:
            pl.wait("scalar", rv_tok)
            pl.wait("scalar", rkx_tok0)
            wp_cons = [tt1_tok, rv_tok]
        else:
            wp_cons = [None, None]
        brow_cons = [None]
        for hp in range(8):
            po2 = 128 * hp
            for iw, (w, wt_d, bi_d, kscale) in enumerate([
                    ("q", W["wq"], Bi["bq"], None),
                    ("k", W["wk"], Bi["bk"], 0.125),
                    ("v", W["wv"], Bi["bv"], None)]):
                bslot = iw % 2
                if wp_cons[bslot] is not None:
                    pl.wait("sync", wp_cons[bslot])
                wtok = pl.dma("sync", wpb[bslot],
                              wt_d.ap()[po2:po2 + 128, :], ch=f"wp{bslot}")
                pl.wait("sync", brow_cons[0])
                btk = pl.dma("sync", brow, bass.AP(bi_d, po2, [[1, 1], [1, 128]]),
                             ch="br")
                pl.wait("vector", wtok)
                pl.wait("vector", btk)
                pl.fence("vector")
                if kscale is not None:
                    ve(lambda e, b=wpb[bslot], s=kscale: e.tensor_scalar(
                        b, b, s, None, ALU.mult))
                    ve(lambda e, s=kscale: e.tensor_scalar(brow, brow, s, None, ALU.mult))
                    pl.fence("vector")
                brt = pl.fence("vector")
                pl.wait("tensor", brt)
                pl.wait("tensor", btk)
                bank_wait("tensor", "pb2")
                btr = pl.emit("tensor", lambda e: e.transpose(
                    pb[2][:, 0:1], brow, onesrow[0:1, 0:1]))
                brow_cons[0] = pl.mark(btr)
                pl.wait("vector", brow_cons[0])
                ve(lambda e: e.tensor_copy(bcol_s, pb[2][:, 0:1]))
                bank_set("pb2", pl.fence("vector") or None)
                pl.fence("vector")
                ve(lambda e, d=bc2[:, 0:1]: e.tensor_copy(d, bcol_s))
                pl.fence("vector")
                ve(lambda e, d=bc2[:, 1:2], hb=bc2[:, 0:1]:
                   e.tensor_tensor(d, bcol_s, hb, ALU.subtract))
                bct = pl.fence("vector")
                pl.wait("tensor", bct)
                bank_wait("tensor", "pt0")
                bt2 = pl.emit("tensor", lambda e: e.transpose(
                    pt[0][0:2, 0:128], bc2, id_b))
                pl.wait("vector", pl.mark(bt2))
                b2v = ve(lambda e, d=bias2[w]: e.tensor_copy(d, pt[0][0:2, 0:128]))
                bank_set("pt0", pl.mark(b2v))
                ve(lambda e, b=wpb[bslot]: e.tensor_copy(hh_t, b))
                pl.fence("vector")
                spv = ve(lambda e, b=wpb[bslot]: e.tensor_tensor(
                    hl_t, b, hh_t, ALU.subtract))
                wp_cons[bslot] = pl.mark(spv)
                sp_tok = pl.fence("vector")
                pl.wait("tensor", sp_tok)
                transpose_to(WT[w][0].rearrange("p (c n) -> p c n", n=128),
                             [hh_t[:, 128 * dc:128 * (dc + 1)] for dc in range(ND)],
                             0, src3d=True)
                transpose_to(WT[w][1].rearrange("p (c n) -> p c n", n=128),
                             [hl_t[:, 128 * dc:128 * (dc + 1)] for dc in range(ND)],
                             1, src3d=True)
            WTq3 = [WT["q"][k].rearrange("p (c n) -> p c n", n=128) for k in (0, 1)]
            WTk3 = [WT["k"][k].rearrange("p (c n) -> p c n", n=128) for k in (0, 1)]
            WTv3 = [WT["v"][k].rearrange("p (c n) -> p c n", n=128) for k in (0, 1)]
            pl.wait("tensor", hT_tok)
            for blk in range(4):
                cols = slice(512 * blk, 512 * (blk + 1))
                for w, WT3, Dh, Dl in (("q", WTq3, Qh, Ql), ("k", WTk3, Kh, Kl)):
                    bk_ = blk % 2
                    bank_wait("tensor", f"pb{bk_}")
                    for dc in range(ND):
                        rhsh = hTh4[:, dc, 4 * blk:4 * blk + 4, :]
                        rhsl = hTl4[:, dc, 4 * blk:4 * blk + 4, :]
                        mm(pb[bk_], WT3[0][:, dc], rhsh, dc == 0, False)
                        mm(pb[bk_], WT3[0][:, dc], rhsl, False, False)
                        mm(pb[bk_], WT3[1][:, dc], rhsh, False, False)
                    mmk = mm(pb[bk_], bias2[w], ones2row, False, True)
                    tok_mm = pl.mark(mmk)
                    pl.wait("vector", tok_mm)
                    pl.fence("vector")
                    ve(lambda e, b=pb[bk_][:], d=Dh[:, cols]: e.tensor_copy(d, b))
                    pl.fence("vector")
                    evv = ve(lambda e, b=pb[bk_][:], hb=Dh[:, cols], d=Dl[:, cols]:
                             e.tensor_tensor(d, b, hb, ALU.subtract))
                    bank_set(f"pb{bk_}", pl.mark(evv))
            for tau in range(NT):
                bk_ = 2 + tau % 2
                bank_wait("tensor", f"pb{bk_}")
                for dc in range(ND):
                    lh = hTh4[:, dc, tau, :]
                    ll = hTl4[:, dc, tau, :]
                    mm(pb[bk_][:, 0:128], lh, WTv3[0][:, dc], dc == 0, False)
                    mm(pb[bk_][:, 0:128], lh, WTv3[1][:, dc], False, False)
                    mm(pb[bk_][:, 0:128], ll, WTv3[0][:, dc], False, False)
                mmk = mm(pb[bk_][:, 0:128], ones2row[:, 0:128], bias2["v"],
                         False, True)
                tok_mm = pl.mark(mmk)
                pl.wait("vector", tok_mm)
                pl.fence("vector")
                ve(lambda e, b=pb[bk_][:, 0:64],
                   d=Vx_h3[:, tau, 0:64]: e.tensor_copy(d, b))
                ve(lambda e, b=pb[bk_][:, 64:128],
                   d=Vx_h3[:, tau, 96:160]: e.tensor_copy(d, b))
                pl.fence("vector")
                ve(lambda e, b=pb[bk_][:, 0:64], hb=Vx_h3[:, tau, 0:64],
                   d=Vx_l3[:, tau, 0:64]:
                   e.tensor_tensor(d, b, hb, ALU.subtract))
                evv = ve(lambda e, b=pb[bk_][:, 64:128], hb=Vx_h3[:, tau, 96:160],
                         d=Vx_l3[:, tau, 96:160]:
                         e.tensor_tensor(d, b, hb, ALU.subtract))
                bank_set(f"pb{bk_}", pl.mark(evv))
            ind3 = ind_b.rearrange("p (t n) -> p t n", n=16)
            pl.fence("vector")
            ve(lambda e, d=Vx_h3[:, :, 64:80]: e.tensor_copy(d, ind3))
            ve(lambda e, d=Vx_h3[:, :, 80:96]: e.tensor_scalar(
                d, ind3, 0.0, None, ALU.mult))
            ve(lambda e, d=Vx_l3[:, :, 64:80]: e.tensor_scalar(
                d, ind3, 0.0, None, ALU.mult))
            ve(lambda e, d=Vx_l3[:, :, 80:96]: e.tensor_scalar(
                d, ind3, 0.0, None, ALU.mult))
            for src_b, col0 in ((Qh, 0), (Kh, 2)):
                pl.fence("vector")
                ve(lambda e, s=src_b: e.tensor_tensor(sqb, s, s, ALU.mult))
                sq_tok = pl.fence("vector")
                pl.wait("tensor", sq_tok)
                for blkn in range(4):
                    bank_wait("tensor", "pb2")
                    nm2 = mm(pb[2][0:2, 0:512], hm_b,
                             sqb[:, 512 * blkn:512 * (blkn + 1)], True, True)
                    pl.wait("vector", pl.mark(nm2))
                    pl.fence("vector")
                    rv2 = ve(lambda e: e.reduce_max(macc[0:2, 1:2],
                                                    pb[2][0:2, 0:512], axis=AX))
                    bank_set("pb2", pl.mark(rv2))
                    pl.fence("vector")
                    if blkn == 0:
                        ve(lambda e: e.tensor_copy(macc[0:2, 0:1], macc[0:2, 1:2]))
                    else:
                        ve(lambda e: e.tensor_tensor(macc[0:2, 0:1], macc[0:2, 0:1],
                                                     macc[0:2, 1:2], ALU.max))
                tp_tok = pl.fence("vector")
                pl.wait("tensor", tp_tok)
                bank_wait("tensor", "pb2")
                tmm = mm(pb[2][0:1, col0:col0 + 2], macc[0:2, 0:1], id2_f, True, True)
                pl.wait("vector", pl.mark(tmm))
                mvv = ve(lambda e, c=col0: e.tensor_copy(mrow4[:, c:c + 2],
                                                         pb[2][0:1, c:c + 2]))
                bank_set("pb2", pl.mark(mvv))
                mvt = pl.fence("vector")
            pl.wait("scalar", mvt)
            sq3 = sc(lambda e: e.activation(mrow4, mrow4, ACT.Sqrt))
            pl.wait("vector", pl.mark(sq3))
            pl.fence("vector")
            ve(lambda e: e.tensor_scalar(tb_s, mrow4[:, 2:4], rkm, None, ALU.add))
            pl.fence("vector")
            nbv = ve(lambda e: e.scalar_tensor_tensor(
                tb_s, tb_s, -1.05, mrow4[:, 0:2], ALU.mult, ALU.mult))
            nb_tok = pl.fence("vector")
            pl.wait("tensor", nb_tok)
            bank_wait("tensor", "pb2")
            nbm = mm(pb[2][:, 0:2], onesrow, tb_s, True, True)
            pl.wait("vector", pl.mark(nbm))
            nbe = ve(lambda e: e.tensor_copy(negB2, pb[2][:, 0:2]))
            bank_set("pb2", pl.mark(nbe))
            nbe_t = pl.fence("vector")
            rot_gate[0] = nbe_t
            pl.wait("sync", nbe_t)
            for hh2 in range(2):
                pl.dma("sync", Kcat3[0:64, hh2, :],
                       Kh[64 * hh2:64 * hh2 + 64, :], ch="rot1")
                rotK_tok[0] = pl.dma("sync", Kcat3[64:128, hh2, :],
                                     Kl[64 * hh2:64 * hh2 + 64, :], ch="rot1")
            for hh in range(2):
                pending_post[0] = attn_head(2 * hp + hh, 64 * hh, hh, with_rel,
                                            pending_post[0])
        if pending_post[0] is not None:
            for sl_, fn_ in sorted(pending_post[0], key=lambda x: x[0]):
                fn_()
            pending_post[0] = None
        wo_phase()
        out_toks = wo_out[-2:] + [pl.mark((e_, pl.last_real[e_]))
                                  for e_ in ("tensor", "vector", "scalar")]
        al.pop()
        return out_toks


    prep_done = [tok_mrep, btok.get("wqfull"),
                 wq_store["wd"][0], wq_store["wd"][1]]
    if stage == 1:
        final_toks = ffn_sublayer(x_in, y_out, "ffn1", pre_tok=prep_done)
    elif stage == 3:
        final_toks = attention_sublayer(x_in, y_out, with_rel=False,
                                        pre_tok=prep_done)
    elif stage == 4:
        final_toks = attention_sublayer(x_in, y_out, with_rel=True,
                                        pre_tok=prep_done)
    elif stage == 5:
        t1 = ffn_sublayer(x_in, xres, "ffn1", pre_tok=prep_done)
        t2 = attention_sublayer(xres, xres, with_rel=True, pre_tok=t1)
        final_toks = ffn_sublayer(xres, y_out, "ffn2", pre_tok=t2)
    else:
        raise NotImplementedError

    for t in final_toks:
        teng, kind, sem, val = t
        if kind == "dma":
            pl.emit("gpsimd", lambda e, s=sem, v=val: e.wait_ge(s, v))

    pl.replay()
    return nc


_NC_CACHE = {}


def _get_nc(stage):
    if stage not in _NC_CACHE:
        _NC_CACHE[stage] = build_program(stage)
    return _NC_CACHE[stage]


def kernel(L0_attn_x, L0_attn_Wq, L0_attn_bq, L0_attn_Wk, L0_attn_bk,
           L0_attn_Wv, L0_attn_bv, L0_attn_Wo, L0_attn_bo,
           L0_attn_rel_k, L0_attn_rel_v,
           L0_ffn_w_gate, L0_ffn_w_up, L0_ffn_w_down, _stage=5):
    nc = _get_nc(_stage)
    shared = dict(
        wq=np.ascontiguousarray(L0_attn_Wq, dtype=np.float32),
        wk=np.ascontiguousarray(L0_attn_Wk, dtype=np.float32),
        wv=np.ascontiguousarray(L0_attn_Wv, dtype=np.float32),
        wo=np.ascontiguousarray(L0_attn_Wo, dtype=np.float32),
        bq=np.ascontiguousarray(L0_attn_bq, dtype=np.float32),
        bk=np.ascontiguousarray(L0_attn_bk, dtype=np.float32),
        bv=np.ascontiguousarray(L0_attn_bv, dtype=np.float32),
        bo=np.ascontiguousarray(L0_attn_bo, dtype=np.float32),
        relk=np.ascontiguousarray(L0_attn_rel_k, dtype=np.float32),
        relv=np.ascontiguousarray(L0_attn_rel_v, dtype=np.float32),
        wg=np.ascontiguousarray(L0_ffn_w_gate, dtype=np.float32),
        wu=np.ascontiguousarray(L0_ffn_w_up, dtype=np.float32),
        wd=np.ascontiguousarray(L0_ffn_w_down, dtype=np.float32),
    )
    shared.update({k: v for k, v in CONSTS.items()})
    xb = np.ascontiguousarray(np.asarray(L0_attn_x, dtype=np.float32))
    in_maps = []
    for c in range(8):
        m = dict(shared)
        m["x_in"] = xb[c]
        in_maps.append(m)
    res = run_bass_kernel_spmd(nc, in_maps, core_ids=list(range(8)))
    return np.stack([r["y_out"] for r in res.results], axis=0)
